# version 30
# baseline (speedup 1.0000x reference)
import numpy as np
from contextlib import ExitStack

import concourse.bass as bass
import concourse.mybir as mybir
from concourse.bass_utils import run_bass_kernel_spmd

F32 = mybir.dt.float32
BF16 = mybir.dt.bfloat16
AF = mybir.ActivationFunctionType
ALU = mybir.AluOpType

D = 1024
T = 2048
CH = D // 128
NH = 16
DH = 64
RQ = 48
TQ = 512
NTQ = T // TQ
TH = 1024
EPS = 1e-5
INVD = 1.0 / D
VAW = NH * (DH + 1)

CONST_VALS = [0.0, 1.0, -1.0, 0.5, -0.5, 0.125, INVD, EPS]

WTENSORS = ["wrT", "wkT", "wvT", "Upm", "V2",
            "outU", "outV", "fc1U", "fc1V", "fc2U", "fc2V",
            "smalls", "rows", "bconsts", "ident"]


def _f32(a):
    a = np.asarray(a)
    assert a.dtype == np.float32, a.dtype
    return a

def feat_chunks(A):
    A = np.ascontiguousarray(_f32(A))
    C, X = A.shape
    return np.ascontiguousarray(A.reshape(C // 128, 128, X).transpose(1, 0, 2).reshape(128, -1))

def vec_cols(v):
    v = _f32(v).reshape(-1)
    return np.ascontiguousarray(v.reshape(-1, 128).T)

def pack_U(U):
    U = _f32(U)
    Up = np.zeros((D, 1024), np.float32)
    for h in range(NH):
        c0 = 128 * (h // 2) + 64 * (h % 2)
        Up[:, c0:c0 + RQ] = U[:, h, :]
    return feat_chunks(Up)

def pack_V2(Vw):
    Vw = _f32(Vw)
    P = np.zeros((128, 1024), np.float32)
    for h in range(NH):
        P[64 * (h % 2):64 * (h % 2) + RQ, DH * h:DH * h + DH] = Vw[h]
    return P

def build_masks():
    q = np.arange(TQ)[None, :]
    k = np.arange(128)[:, None]
    return np.concatenate([(q >= 128 * d + k).astype(np.float32) for d in range(4)], axis=1)


def prep_host(inputs):
    g = {}
    g["wrT"] = feat_chunks(np.ascontiguousarray(_f32(inputs["L0_ffn_W_r"]).T))
    g["wkT"] = feat_chunks(np.ascontiguousarray(_f32(inputs["L0_ffn_W_k"]).T))
    g["wvT"] = feat_chunks(np.ascontiguousarray(_f32(inputs["L0_ffn_W_v"]).T))
    qU = pack_U(inputs["L0_attn_q_U"])
    kU = pack_U(inputs["L0_attn_k_U"])
    vU = pack_U(inputs["L0_attn_v_U"])
    Upm = np.zeros((128, 8 * 3072), np.float32)
    for m in range(8):
        for pi, Uf in enumerate((qU, kU, vU)):
            for c in range(8):
                Upm[:, 3072 * m + 1024 * pi + 128 * c:3072 * m + 1024 * pi + 128 * (c + 1)] = \
                    Uf[:, 1024 * c + 128 * m:1024 * c + 128 * (m + 1)]
    g["Upm"] = np.ascontiguousarray(Upm)
    g["V2"] = np.ascontiguousarray(np.concatenate(
        [pack_V2(inputs["L0_attn_q_V"]), pack_V2(inputs["L0_attn_k_V"]),
         pack_V2(inputs["L0_attn_v_V"])], axis=1))
    g["outU"] = feat_chunks(inputs["L0_attn_out_U"])
    g["outV"] = feat_chunks(inputs["L0_attn_out_V"])
    g["fc1U"] = feat_chunks(inputs["L0_attn_fc1_U"])
    g["fc1V"] = feat_chunks(inputs["L0_attn_fc1_V"])
    g["fc2U"] = feat_chunks(inputs["L0_attn_fc2_U"])
    g["fc2V"] = feat_chunks(inputs["L0_attn_fc2_V"])

    cols, reg = [], {}
    def add(name, arr):
        arr = _f32(arr)
        if arr.ndim == 1:
            arr = arr[:, None]
        reg[name] = (sum(c.shape[1] for c in cols), arr.shape[1])
        cols.append(arr)
    add("consts", np.broadcast_to(np.array(CONST_VALS, np.float32),
                                  (128, len(CONST_VALS))).copy())
    add("qb", vec_cols(_f32(inputs["L0_attn_q_b"]).reshape(-1)))
    add("kb", vec_cols(_f32(inputs["L0_attn_k_b"]).reshape(-1)))
    add("outb", vec_cols(inputs["L0_attn_out_b"]))
    add("fc1b", vec_cols(inputs["L0_attn_fc1_b"]))
    add("fc2b", vec_cols(inputs["L0_attn_fc2_b"]))
    add("mur", vec_cols(inputs["L0_ffn_mu_r"]))
    add("muk", vec_cols(inputs["L0_ffn_mu_k"]))
    add("g1", vec_cols(inputs["L0_attn_ln1_g"]))
    add("b1", vec_cols(inputs["L0_attn_ln1_b"]))
    add("g2", vec_cols(inputs["L0_attn_ln2_g"]))
    add("b2", vec_cols(inputs["L0_attn_ln2_b"]))
    g["smalls"] = np.ascontiguousarray(np.concatenate(cols, axis=1))
    g["smalls_reg"] = reg

    g["rows"] = np.ones((1, 128), np.float32)

    bcols, breg = [], {}
    def addb(name, arr):
        arr = _f32(arr)
        breg[name] = (sum(c.shape[1] for c in bcols), arr.shape[1])
        bcols.append(arr)
    addb("masks", build_masks())
    addb("ones", np.ones((128, 64), np.float32))
    addb("vb_bf", vec_cols(_f32(inputs["L0_attn_v_b"]).reshape(-1)))
    addb("b2_bf", vec_cols(inputs["L0_attn_ln2_b"]))
    g["bconsts"] = np.ascontiguousarray(np.concatenate(bcols, axis=1))
    g["bconsts_reg"] = breg

    g["ident"] = np.ascontiguousarray(np.eye(128, dtype=np.float32))
    return g



DMA_SEMS = ["DC", "DBC", "DI0", "DI1", "DSS", "DS2", "DSH", "DCT0", "DCT1",
            "DO0", "DO1", "DD", "DGD", "DWK", "DWV", "DWR", "DV2", "DWU",
            "DWU2", "DOU", "DOV", "DF1U", "DF1V", "DF2U", "DF2V", "DACC0", "DACC1"]


class Prog:
    SEM_OF = {"t": "T", "v": "V", "s": "S"}

    def __init__(self, nc):
        self.nc = nc
        self.q = {k: [] for k in "tvsyg"}
        self.c = {sn: 0 for sn in ["T", "V", "S"] + DMA_SEMS}
        self.sems = None

    def _mk(self, e, fn, wait, incs):
        wait = tuple(w for w in wait if w is not None and w[1] > 0)
        def run(eng, fn=fn, wait=wait, incs=tuple(incs)):
            for sn, v in wait:
                eng.wait_ge(self.sems[sn], v)
            inst = fn(eng)
            for sn, amt in incs:
                inst = inst.then_inc(self.sems[sn], amt)
        self.q[e].append(run)

    def op(self, e, fn, wait=(), inc=False):
        sn = self.SEM_OF[e]
        if e in "vs":
            wait = list(wait) + [(sn, self.c[sn])]
            inc = True
        elif e == "t" and inc:
            wait = list(wait) + [(sn, self.c[sn])]
        incs = [(sn, 1)] if inc else []
        self._mk(e, fn, wait, incs)
        if inc:
            self.c[sn] += 1
            return self.c[sn]
        return None

    def t(self, fn, wait=(), inc=False): return self.op("t", fn, wait, inc)
    def v(self, fn, wait=(), inc=False): return self.op("v", fn, wait, inc)
    def s(self, fn, wait=(), inc=False): return self.op("s", fn, wait, inc)

    def dma(self, e, sem, fn, wait=()):
        self._mk(e, fn, wait, [(sem, 16)])
        self.c[sem] += 16
        return self.c[sem]

    def y(self, fn, wait=(), sem="DD"):  return self.dma("y", sem, fn, wait)
    def g(self, fn, wait=(), sem="DGD"): return self.dma("g", sem, fn, wait)

    def barrier(self, engines="tvs"):
        engines = [e for e in engines if e in "tvs"]
        marks = {}
        for e in engines:
            marks[e] = (self.SEM_OF[e], self.op(e, lambda eng: eng.nop(), inc=True))
        for e in engines:
            waits = [marks[o] for o in engines if o != e]
            self._mk(e, lambda eng: eng.nop(), waits, [])

    def flush(self):
        with self.nc.Block() as block:
            for e, sect in (("t", block.tensor), ("v", block.vector),
                            ("s", block.scalar), ("y", block.sync),
                            ("g", block.gpsimd)):
                ops = self.q[e]
                if ops:
                    sect(lambda eng, ops=tuple(ops): [f(eng) for f in ops])
                self.q[e] = []


class Pool:
    def __init__(self, p, tiles):
        self.p = p
        self.tiles = tiles
        self.st = [None] * len(tiles)
        self.rr = 0

    def acq(self):
        n = len(self.tiles)
        for k in range(n):
            i = (self.rr + k) % n
            if self.st[i] != "held":
                break
        else:
            raise RuntimeError("psum pool exhausted")
        self.rr = i + 1
        w = self.st[i]
        self.st[i] = "held"
        return self.tiles[i], i, w

    def rel(self, i, wait_tuple):
        assert self.st[i] == "held"
        self.st[i] = wait_tuple



def build_program(host, dumps=(), stop_after=None):
    nc = bass.Bass("TRN2", target_bir_lowering=False)
    p = Prog(nc)

    x_in = nc.dram_tensor("x_in", [128, CH * T], F32, kind="ExternalInput").ap()
    x_out = nc.dram_tensor("x_out", [128, CH * T], F32, kind="ExternalOutput").ap()
    dram = {}
    for name in WTENSORS:
        arr = host[name]
        dram[name] = nc.dram_tensor(name, list(arr.shape), F32, kind="ExternalInput").ap()
    spill_s = nc.dram_tensor("spill_s", [128, CH * T], F32, kind="Internal").ap()
    spill_h = nc.dram_tensor("spill_h", [128, CH * T], F32, kind="Internal").ap()
    dump_aps = {}
    for dname, ncols in dumps:
        dump_aps[dname] = nc.dram_tensor("dump_" + dname, [128, ncols], F32,
                                         kind="ExternalOutput").ap()
    dump_names = set(dump_aps)

    sreg = host["smalls_reg"]
    breg = host["bconsts_reg"]

    with ExitStack() as es:
        p.sems = {sn: es.enter_context(nc.semaphore(sn))
                  for sn in ["T", "V", "S"] + DMA_SEMS}

        csts = es.enter_context(nc.sbuf_tensor("csts", [128, host["smalls"].shape[1]], F32, side="left"))
        rows = es.enter_context(nc.sbuf_tensor("rows_sb", [1, 128], F32, side="left"))
        bcst = es.enter_context(nc.sbuf_tensor("bcst", [128, host["bconsts"].shape[1]], BF16, side="left"))
        ident = es.enter_context(nc.sbuf_tensor("ident_sb", [128, 128], F32, side="left"))
        statA = es.enter_context(nc.sbuf_tensor("statA", [1, T], F32, side="left"))
        bA = es.enter_context(nc.sbuf_tensor("bA", [128, T], F32, side="left"))
        bB = es.enter_context(nc.sbuf_tensor("bB", [128, T], F32, side="left"))
        bnd = es.enter_context(nc.sbuf_tensor("bnd", [128, CH], F32, side="left"))

        def scol(name, j=0):
            off, n = sreg[name]
            assert j < n, (name, j, n)
            return csts[:, off + j:off + j + 1]
        CONST_COL = {v: i for i, v in enumerate(CONST_VALS)}
        coff = sreg["consts"][0]
        def ccol(v):
            return csts[:, coff + CONST_COL[v]:coff + CONST_COL[v] + 1]
        for v in CONST_VALS:
            nc.const_aps.register(F32, float(v), ccol(v))
        def bsl(name, j0, n, p0=0, pn=128):
            off, _ = breg[name]
            return bcst[p0:p0 + pn, off + j0:off + j0 + n]

        p.y(lambda e: e.dma_start(csts[:, :], dram["smalls"]), sem="DC")
        p.y(lambda e: e.dma_start(rows[:, :], dram["rows"]), sem="DC")
        d_id = p.y(lambda e: e.dma_start(ident[:, :], dram["ident"]), sem="DC")
        d_bc = p.g(lambda e: e.dma_start(bcst[:, :], dram["bconsts"]), sem="DBC")
        W_INIT = [("DC", d_id), ("DBC", d_bc)]

        def finalize_early(buf):
            dst = x_out
            p.g(lambda e, buf=buf, dst=dst: e.dma_start(dst, buf[:, :]),
                wait=[("V", p.c["V"]), ("S", p.c["S"]), ("T", p.c["T"])], sem="DGD")
            p._mk("y", lambda eng: eng.nop(),
                  [(sn, p.c[sn]) for sn in DMA_SEMS if p.c[sn] > 0], [])
            p.flush()

        def emit_dump(name, buf, wait, bf16=False):
            if name not in dump_names:
                return
            if bf16:
                dm = p.g(lambda e, name=name, buf=buf: e.dma_start(dump_aps[name], buf[:, :]),
                         wait=list(wait), sem="DGD")
                gate = ("DGD", dm)
            else:
                dm = p.y(lambda e, name=name, buf=buf: e.dma_start(dump_aps[name], buf[:, :]),
                         wait=list(wait), sem="DD")
                gate = ("DD", dm)
            p.v(lambda e: e.nop(), wait=[gate])
            p.s(lambda e: e.nop(), wait=[gate])


        def emit_ln(pool, X, scratch, pre_w=(), pre_w_chunk=None, scratch2=None):
            inv_tile = [None] * NTQ
            neg_tile = [None] * NTQ
            half_T_end = [0, 0]
            scr_of = [scratch, scratch2 if scratch2 is not None else scratch]
            for half in range(2):
                scratch = scr_of[half]
                cast_marks = []
                for c in range(CH):
                    w = list(pre_w)
                    if pre_w_chunk is not None:
                        w.append(pre_w_chunk[c])
                    if half > 0 and scratch2 is None:
                        w.append(("T", half_T_end[0]))
                    cm = p.s(lambda e, c=c, half=half, X=X, scratch=scratch: e.activation(
                        scratch[:, TH * c:TH * (c + 1)],
                        X[:, T * c + TH * half: T * c + TH * half + TH],
                        AF.Identity), wait=w, inc=True)
                    cast_marks.append(cm)
                t1a, i1a, pw1a = pool.acq()
                t1b, i1b, pw1b = pool.acq()
                s1tiles = [t1a, t1b]
                c_done = []
                for c in range(CH):
                    for tt in range(2):
                        p.t(lambda e, tile=s1tiles[tt], c=c, tt=tt, scratch=scratch: e.matmul(
                            tile[0:1, :], bsl("ones", 0, 1),
                            scratch[:, TH * c + TQ * tt: TH * c + TQ * tt + TQ],
                            start=(c == 0), stop=(c == CH - 1)),
                            wait=[("S", cast_marks[c])] +
                                 ([pw1a] if (pw1a and c == 0 and tt == 0) else []) +
                                 ([pw1b] if (pw1b and c == 0 and tt == 1) else []),
                            inc=(tt == 1))
                    c_done.append(p.c["T"])
                sq_marks = []
                for c in range(CH):
                    vm = p.v(lambda e, c=c, scratch=scratch: e.tensor_tensor(
                        scratch[:, TH * c:TH * (c + 1)], scratch[:, TH * c:TH * (c + 1)],
                        scratch[:, TH * c:TH * (c + 1)], ALU.mult),
                        wait=[("T", c_done[c])], inc=True)
                    sq_marks.append(vm)
                t2a, i2a, pw2a = pool.acq()
                t2b, i2b, pw2b = pool.acq()
                s2tiles = [t2a, t2b]
                for c in range(CH):
                    for tt in range(2):
                        p.t(lambda e, tile=s2tiles[tt], c=c, tt=tt, scratch=scratch: e.matmul(
                            tile[0:1, :], bsl("ones", 0, 1),
                            scratch[:, TH * c + TQ * tt: TH * c + TQ * tt + TQ],
                            start=(c == 0), stop=(c == CH - 1)),
                            wait=[("V", sq_marks[c])] +
                                 ([pw2a] if (pw2a and c == 0 and tt == 0) else []) +
                                 ([pw2b] if (pw2b and c == 0 and tt == 1) else []),
                            inc=(tt == 1))
                half_T_end[half] = p.c["T"]
                mm_done = ("T", p.c["T"])
                for tt in range(2):
                    j = 2 * half + tt
                    sl = slice(TQ * j, TQ * (j + 1))
                    p.v(lambda e, tile=s1tiles[tt], sl=sl: e.tensor_scalar(
                        statA[0:1, sl], tile[0:1, :], float(INVD), None, ALU.mult),
                        wait=[mm_done], inc=True)
                    p.v(lambda e, tile=s2tiles[tt], sl=sl: e.tensor_scalar(
                        bB[0:1, sl], tile[0:1, :], float(INVD), None, ALU.mult), inc=True)
                    vrel = p.c["V"]
                    p.v(lambda e, sl=sl: e.tensor_tensor(
                        bA[0:1, sl], statA[0:1, sl], statA[0:1, sl], ALU.mult),
                        wait=[("V", p.c["V"])], inc=True)
                    vv = p.v(lambda e, sl=sl: e.tensor_tensor(
                        bB[0:1, sl], bB[0:1, sl], bA[0:1, sl], ALU.subtract),
                        wait=[("V", p.c["V"])], inc=True)
                    p.s(lambda e, sl=sl: e.activation(bB[0:1, sl], bB[0:1, sl], AF.Ln,
                                                      bias=ccol(EPS)[0:1, :], scale=1.0),
                        wait=[("V", vv)], inc=True)
                    sv = p.s(lambda e, sl=sl: e.activation(bB[0:1, sl], bB[0:1, sl],
                                                           AF.Exp, scale=-0.5), inc=True)
                    inv_tile[j] = ("S", sv)
                    nm = p.v(lambda e, sl=sl: e.scalar_tensor_tensor(
                        statA[0:1, sl], statA[0:1, sl], -1.0, bB[0:1, sl],
                        ALU.mult, ALU.mult), wait=[inv_tile[j]], inc=True)
                    neg_tile[j] = ("V", nm)
                    pool.rel(i1a if tt == 0 else i1b, ("V", vrel))
                    pool.rel(i2a if tt == 0 else i2b, ("V", vrel))
            return {"inv": inv_tile[-1][1], "neg": neg_tile[-1][1],
                    "inv_tile": inv_tile, "neg_tile": neg_tile}

        def emit_outer_pair(pool, rowA, rowB, wA, wB, wA_tile=None, wB_tile=None):
            for j in range(NTQ):
                tile, ti, pw = pool.acq()
                tm = p.t(lambda e, tile=tile, j=j, rowA=rowA: e.matmul(
                    tile[:, :], rows[0:1, :], rowA[0:1, TQ * j:TQ * (j + 1)],
                    start=True, stop=True),
                    wait=[wA_tile[j] if wA_tile else wA] + ([pw] if pw else []), inc=True)
                vmA = p.v(lambda e, tile=tile, j=j: e.tensor_copy(
                    bA[:, TQ * j:TQ * (j + 1)], tile[:, :]), wait=[("T", tm)], inc=True)
                pool.rel(ti, ("V", vmA))
                tile2, ti2, pw2 = pool.acq()
                tm2 = p.t(lambda e, tile2=tile2, j=j, rowB=rowB: e.matmul(
                    tile2[:, :], rows[0:1, :], rowB[0:1, TQ * j:TQ * (j + 1)],
                    start=True, stop=True),
                    wait=[wB_tile[j] if wB_tile else wB, ("V", vmA)] +
                         ([pw2] if pw2 else []), inc=True)
                vmB = p.v(lambda e, tile2=tile2, j=j: e.tensor_copy(
                    bB[:, TQ * j:TQ * (j + 1)], tile2[:, :]), wait=[("T", tm2)], inc=True)
                pool.rel(ti2, ("V", vmB))
            return ("V", p.c["V"])

        def emit_ln_tiles(pool, X, scratch, pre_w=(), pre_w_chunk=None, scratch2=None):
            r = emit_ln(pool, X, scratch, pre_w, pre_w_chunk, scratch2)
            return emit_outer_pair(pool, bB, statA, None, None,
                                   wA_tile=r["inv_tile"], wB_tile=r["neg_tile"])

        def emit_apply(X, OUT, mark_b, affine=None, extra_w=()):
            marks = []
            for c in range(CH):
                p.v(lambda e, c=c, X=X, OUT=OUT: e.tensor_tensor(
                    OUT[:, T * c:T * (c + 1)], X[:, T * c:T * (c + 1)], bA[:, :], ALU.mult),
                    wait=[mark_b] + list(extra_w), inc=True)
                vm = p.v(lambda e, c=c, OUT=OUT: e.tensor_tensor(
                    OUT[:, T * c:T * (c + 1)], OUT[:, T * c:T * (c + 1)], bB[:, :], ALU.add),
                    wait=[("V", p.c["V"])], inc=True)
                if affine is not None:
                    gname, bname = affine
                    sm = p.s(lambda e, c=c, gname=gname, bname=bname, OUT=OUT: e.activation(
                        OUT[:, T * c:T * (c + 1)], OUT[:, T * c:T * (c + 1)], AF.Identity,
                        bias=scol(bname, c), scale=scol(gname, c)),
                        wait=[("V", vm)], inc=True)
                    marks.append(("S", sm))
                else:
                    marks.append(("V", vm))
            return marks

        esS = ExitStack()
        sbS = esS.enter_context(nc.sbuf_tensor("S", [128, CH * T], F32, side="right"))

        ps8 = ExitStack()
        pool_tiles = [ps8.enter_context(nc.psum_tensor("pp%d" % i, [128, TQ], F32))
                      for i in range(8)]
        pool = Pool(p, pool_tiles)

        sv_in = sbS[:, :].rearrange("p (c t) -> p c t", t=T)
        xv_in = x_in.rearrange("p (c t) -> p c t", t=T)
        d_in = []
        for hh in range(2):
            d_in.append(p.y(lambda e, hh=hh: e.dma_start(
                sv_in[:, 4 * hh:4 * (hh + 1), :], xv_in[:, 4 * hh:4 * (hh + 1), :]),
                sem=["DI0", "DI1"][hh]))
        in_chunk_marks = [(["DI0", "DI1"][c // 4], d_in[c // 4]) for c in range(CH)]
        emit_dump("p0", sbS, [in_chunk_marks[-1]] + W_INIT)
        p.flush()

        def emit_ffn(X, XR, XK, KA, wr, wk, wv, rt, t1f, krl, wmarks, pre_w,
                     pre_w_chunk=None):
            mark_b = emit_ln_tiles(pool, X, XR, pre_w, pre_w_chunk, scratch2=XK)
            QTR = CH * TQ
            slot_free = [("T", 0), ("T", 0)]
            add_chunk_marks = {}

            def emit_mix(q):
                s0 = QTR * (q % 2)
                t0 = TQ * q
                marks = []
                for c in range(CH):
                    w = [mark_b, slot_free[q % 2]] + list(pre_w)
                    if q > 0:
                        p.v(lambda e, c=c, t0=t0: e.tensor_tensor(
                            bnd[:, c:c + 1], X[:, T * c + t0 - 1:T * c + t0],
                            bA[:, t0 - 1:t0], ALU.mult), wait=w, inc=True)
                        p.v(lambda e, c=c, t0=t0: e.tensor_tensor(
                            bnd[:, c:c + 1], bnd[:, c:c + 1], bB[:, t0 - 1:t0],
                            ALU.add), inc=True)
                        w = []
                    p.v(lambda e, c=c, t0=t0, s0=s0: e.tensor_tensor(
                        KA[:, s0 + TQ * c:s0 + TQ * (c + 1)],
                        X[:, T * c + t0:T * c + t0 + TQ],
                        bA[:, t0:t0 + TQ], ALU.mult), wait=w, inc=True)
                    p.v(lambda e, c=c, s0=s0: e.tensor_tensor(
                        KA[:, s0 + TQ * c:s0 + TQ * (c + 1)],
                        KA[:, s0 + TQ * c:s0 + TQ * (c + 1)],
                        bB[:, TQ * q:TQ * q + TQ], ALU.add), inc=True)
                    p.v(lambda e, c=c, s0=s0: e.tensor_tensor(
                        XK[:, s0 + TQ * c + 1:s0 + TQ * c + TQ],
                        KA[:, s0 + TQ * c + 1:s0 + TQ * c + TQ],
                        KA[:, s0 + TQ * c:s0 + TQ * c + TQ - 1], ALU.subtract), inc=True)
                    if q == 0:
                        p.v(lambda e, c=c, s0=s0: e.tensor_scalar(
                            XR[:, s0 + TQ * c:s0 + TQ * c + 1],
                            KA[:, s0 + TQ * c:s0 + TQ * c + 1],
                            scol("mur", c), None, ALU.mult), inc=True)
                        p.v(lambda e, c=c, s0=s0: e.tensor_scalar(
                            XK[:, s0 + TQ * c:s0 + TQ * c + 1],
                            KA[:, s0 + TQ * c:s0 + TQ * c + 1],
                            scol("muk", c), None, ALU.mult), inc=True)
                    else:
                        p.v(lambda e, c=c, s0=s0: e.tensor_tensor(
                            XR[:, s0 + TQ * c:s0 + TQ * c + 1],
                            KA[:, s0 + TQ * c:s0 + TQ * c + 1],
                            bnd[:, c:c + 1], ALU.subtract), inc=True)
                        p.v(lambda e, c=c, s0=s0: e.scalar_tensor_tensor(
                            XK[:, s0 + TQ * c:s0 + TQ * c + 1],
                            XR[:, s0 + TQ * c:s0 + TQ * c + 1],
                            scol("muk", c), bnd[:, c:c + 1], ALU.mult, ALU.add), inc=True)
                        p.v(lambda e, c=c, s0=s0: e.scalar_tensor_tensor(
                            XR[:, s0 + TQ * c:s0 + TQ * c + 1],
                            XR[:, s0 + TQ * c:s0 + TQ * c + 1],
                            scol("mur", c), bnd[:, c:c + 1], ALU.mult, ALU.add), inc=True)
                    p.v(lambda e, c=c, s0=s0: e.scalar_tensor_tensor(
                        XR[:, s0 + TQ * c + 1:s0 + TQ * c + TQ],
                        XK[:, s0 + TQ * c + 1:s0 + TQ * c + TQ],
                        scol("mur", c), KA[:, s0 + TQ * c:s0 + TQ * c + TQ - 1],
                        ALU.mult, ALU.add), inc=True)
                    vm = p.v(lambda e, c=c, s0=s0: e.scalar_tensor_tensor(
                        XK[:, s0 + TQ * c + 1:s0 + TQ * c + TQ],
                        XK[:, s0 + TQ * c + 1:s0 + TQ * c + TQ],
                        scol("muk", c), KA[:, s0 + TQ * c:s0 + TQ * c + TQ - 1],
                        ALU.mult, ALU.add), inc=True)
                    marks.append(vm)
                return ("V", marks[-1])

            mix_marks = {0: emit_mix(0)}
            for q in range(4):
                s0 = QTR * (q % 2)
                t0 = TQ * q
                mix_done = mix_marks[q]
                relu_last = 0
                for dk in range(CH):
                    tile, ti, pw = pool.acq()
                    for c in range(CH):
                        p.t(lambda e, tile=tile, dk=dk, c=c, s0=s0: e.matmul(
                            tile[:, :], wk[:, D * c + 128 * dk:D * c + 128 * (dk + 1)],
                            XK[:, s0 + TQ * c:s0 + TQ * (c + 1)],
                            start=(c == 0), stop=(c == CH - 1)),
                            wait=([mix_done, wmarks["wk"]] if c == 0 else []) +
                                 ([pw] if (pw and c == 0) else []),
                            inc=(c == CH - 1))
                    tmm = p.c["T"]
                    p.s(lambda e, tile=tile, krl=krl: e.activation(
                        krl[:, :], tile[:, :], AF.Relu), wait=[("T", tmm)], inc=True)
                    relu_last = p.s(lambda e, dk=dk, s0=s0, krl=krl: e.activation(
                        KA[:, s0 + TQ * dk:s0 + TQ * (dk + 1)], krl[:, :],
                        AF.Square), inc=True)
                    pool.rel(ti, ("S", relu_last))
                ka_done = ("S", relu_last)
                if q + 1 < 4:
                    mix_marks[q + 1] = emit_mix(q + 1)
                for dv in range(CH):
                    tileKV, tiKV, pwKV = pool.acq()
                    for c in range(CH):
                        p.t(lambda e, tile=tileKV, dv=dv, c=c, s0=s0: e.matmul(
                            tile[:, :], wv[:, D * c + 128 * dv:D * c + 128 * (dv + 1)],
                            KA[:, s0 + TQ * c:s0 + TQ * (c + 1)],
                            start=(c == 0), stop=(c == CH - 1)),
                            wait=([ka_done, wmarks["wv"]] if c == 0 else []) +
                                 ([pwKV] if (pwKV and c == 0) else []),
                            inc=(c == CH - 1))
                    tileR, tiR, pwR = pool.acq()
                    for c in range(CH):
                        p.t(lambda e, tile=tileR, dv=dv, c=c, s0=s0: e.matmul(
                            tile[:, :], wr[:, D * c + 128 * dv:D * c + 128 * (dv + 1)],
                            XR[:, s0 + TQ * c:s0 + TQ * (c + 1)],
                            start=(c == 0), stop=(c == CH - 1)),
                            wait=([wmarks["wr"]] if c == 0 else []) +
                                 ([pwR] if (pwR and c == 0) else []),
                            inc=(c == CH - 1))
                    tmm = p.c["T"]
                    sm = p.s(lambda e, tile=tileR, rt=rt: e.activation(
                        rt[:, :], tile[:, :], AF.Sigmoid),
                        wait=[("T", tmm), ("V", p.c["V"])], inc=True)
                    pool.rel(tiR, ("S", sm))
                    p.v(lambda e, tile=tileKV, rt=rt, t1f=t1f: e.scalar_tensor_tensor(
                        t1f[:, :], tile[:, :], 0.5, rt[:, :], ALU.mult, ALU.mult),
                        wait=[("S", sm)], inc=True)
                    vm = p.v(lambda e, dv=dv, t0=t0, X=X, t1f=t1f: e.tensor_tensor(
                        X[:, T * dv + t0:T * dv + t0 + TQ],
                        X[:, T * dv + t0:T * dv + t0 + TQ],
                        t1f[:, :], ALU.add), inc=True)
                    pool.rel(tiKV, ("V", vm))
                    add_chunk_marks[dv] = ("V", vm)
                slot_free[q % 2] = ("T", p.c["T"])
            last = add_chunk_marks[CH - 1]
            return last, [add_chunk_marks[c] for c in range(CH)]

        ffn1 = ExitStack()
        XRb = ffn1.enter_context(nc.sbuf_tensor("XR1", [128, CH * TH], BF16, side="right"))
        XKb = ffn1.enter_context(nc.sbuf_tensor("XK1", [128, CH * TH], BF16, side="right"))
        KAb = ffn1.enter_context(nc.sbuf_tensor("KA1", [128, CH * TH], BF16, side="right"))
        wrS = ffn1.enter_context(nc.sbuf_tensor("wrS1", [128, CH * D], BF16, side="right"))
        wkS = ffn1.enter_context(nc.sbuf_tensor("wkS1", [128, CH * D], BF16, side="right"))
        wvS = ffn1.enter_context(nc.sbuf_tensor("wvS1", [128, CH * D], BF16, side="right"))
        rt1 = ffn1.enter_context(nc.sbuf_tensor("rt1", [128, TQ], BF16, side="right"))
        krl1 = ffn1.enter_context(nc.sbuf_tensor("krl1", [128, TQ], BF16, side="right"))
        t1f1 = ffn1.enter_context(nc.sbuf_tensor("t1f1", [128, TQ], F32, side="right"))
        dwk = p.g(lambda e: e.dma_start(wkS[:, :], dram["wkT"]), sem="DWK")
        dwv = p.g(lambda e: e.dma_start(wvS[:, :], dram["wvT"]), sem="DWV")
        dwr = p.g(lambda e: e.dma_start(wrS[:, :], dram["wrT"]), sem="DWR")
        wm1 = {"wk": ("DWK", dwk), "wv": ("DWV", dwv), "wr": ("DWR", dwr)}
        mark_ffn1, ffn1_chunk_marks = emit_ffn(sbS, XRb, XKb, KAb, wrS, wkS, wvS,
                                               rt1, t1f1, krl1, wm1, W_INIT,
                                               in_chunk_marks)
        ffn1_T_end = ("T", p.c["T"])
        d_spill_s = p.y(lambda e: e.dma_start(spill_s, sbS[:, :]), wait=[mark_ffn1],
                        sem="DSS")
        emit_dump("ffn1", sbS, [mark_ffn1])
        p.barrier("tvs")
        p.flush()
        ffn1.close()
        if stop_after == "ffn1":
            finalize_early(sbS)
            return nc

        esH = ExitStack()
        sbH = esH.enter_context(nc.sbuf_tensor("H", [128, CH * T], F32, side="left"))
        statB1 = esH.enter_context(nc.sbuf_tensor("statB1", [1, T], F32, side="left"))
        statB2 = esH.enter_context(nc.sbuf_tensor("statB2", [1, T], F32, side="left"))
        p2a = ExitStack()
        scr = p2a.enter_context(nc.sbuf_tensor("scr", [128, 2 * CH * TH], BF16, side="right"))

        scrA = scr[:, 0:CH * TH]
        scrB = scr[:, CH * TH:2 * CH * TH]
        mark_b0 = emit_ln_tiles(pool, sbS, scrA, [], ffn1_chunk_marks, scratch2=scrB)
        hs_marks = emit_apply(sbS, sbH, mark_b0)
        mark_hs = hs_marks[-1]
        emit_dump("hs", sbH, [mark_hs])
        sv1 = p.v(lambda e: e.tensor_copy(statB1[0:1, :], bA[0:1, :]), wait=[mark_b0], inc=True)
        sv2 = p.v(lambda e: e.tensor_copy(statB2[0:1, :], bB[0:1, :]), inc=True)
        d_spill_h = p.y(lambda e: e.dma_start(spill_h, sbH[:, :]), wait=[mark_hs],
                        sem="DSH")
        r1 = emit_ln(pool, sbH, scrA, [("V", sv2)], hs_marks, scratch2=scrB)
        p.v(lambda e: e.tensor_tensor(statB1[0:1, :], statB1[0:1, :], bB[0:1, :], ALU.mult),
            wait=[("V", r1["neg"])], inc=True)
        p.v(lambda e: e.tensor_tensor(statB2[0:1, :], statB2[0:1, :], bB[0:1, :], ALU.mult),
            wait=[("V", p.c["V"])], inc=True)
        cm2 = p.v(lambda e: e.tensor_tensor(statB2[0:1, :], statB2[0:1, :], statA[0:1, :],
                                            ALU.add), wait=[("V", p.c["V"])], inc=True)
        mark_bx = emit_outer_pair(pool, statB1, statB2, ("V", cm2), ("V", cm2))
        p.flush()
        p2a.close()
        esH.close()
        if stop_after == "p2a":
            finalize_early(sbS)
            return nc

        esx1 = ExitStack()
        x1b = esx1.enter_context(nc.sbuf_tensor("x1", [128, CH * T], BF16, side="left"))
        x1_marks = emit_apply(sbS, x1b, mark_bx, affine=("g1", "b1"),
                              extra_w=[("DSH", d_spill_h)])
        mark_x1 = x1_marks[-1]
        x1_chunk_marks = list(x1_marks)
        emit_dump("x1", x1b, [mark_x1], bf16=True)
        p.barrier("tvs")
        p.flush()
        esS.close()

        esQB = ExitStack()
        QB = esQB.enter_context(nc.sbuf_tensor("QB", [128, CH * T], BF16, side="right"))
        esKV = ExitStack()
        KB = esKV.enter_context(nc.sbuf_tensor("KB", [128, CH * T], BF16, side="right"))
        VA = esKV.enter_context(nc.sbuf_tensor("VA", [128, 16 * VAW], BF16, side="right"))
        pb = ExitStack()
        ATb = pb.enter_context(nc.sbuf_tensor("AT", [128, 8 * TQ], BF16, side="right"))
        CTb = pb.enter_context(nc.sbuf_tensor("CT", [64, 2 * TQ], BF16, side="right"))
        rlb = pb.enter_context(nc.sbuf_tensor("rl", [128, 2 * TQ], BF16, side="right"))
        brl = pb.enter_context(nc.sbuf_tensor("brl", [64, 2 * TQ], BF16, side="right"))
        lnt = pb.enter_context(nc.sbuf_tensor("lnt", [128, TQ], F32, side="right"))
        pa = ExitStack()
        Pq = pa.enter_context(nc.sbuf_tensor("Pq", [128, 3 * T], BF16, side="right"))
        Wu = pa.enter_context(nc.sbuf_tensor("Wu", [128, 2 * 3072], BF16, side="right"))
        V2s = pa.enter_context(nc.sbuf_tensor("V2s", [128, 3 * 1024], BF16, side="right"))

        ps8.close()
        psB = ExitStack()
        SD = [psB.enter_context(nc.psum_tensor("sd%d" % i, [128, 2 * TQ], F32)) for i in range(2)]
        pool4 = Pool(p, [psB.enter_context(nc.psum_tensor("pb%d" % i, [128, TQ], F32))
                         for i in range(4)])

        spill_gate = [("DSS", d_spill_s), ("DSH", d_spill_h)]
        d_v2 = p.g(lambda e: e.dma_start(V2s[:, :], dram["V2"]), wait=[ffn1_T_end], sem="DV2")
        VA_view = VA[:, :].rearrange("p (tc h e) -> p tc h e", h=NH, e=DH + 1)
        ones_set = p.v(lambda e: e.memset(VA_view[:, :, :, DH:DH + 1], 1.0),
                       wait=spill_gate, inc=True)

        pq_free = [("T", 0), ("T", 0), ("T", 0)]
        wu_free = [("T", 0), ("T", 0)]
        qk_ready = {}
        va_ready = {}
        qb_of = {0: "qb", 1: "kb"}

        qkv_queue = []

        def drain_qkv(n=1):
            k = 0
            while qkv_queue and k < n:
                qkv_queue.pop(0)[1]()
                k += 1
            return k

        def drain_tag(tag):
            while qkv_queue and qkv_queue[0][0] <= tag:
                qkv_queue.pop(0)[1]()

        def emit_qkv_pair(m):
            wslot = m % 2
            woff = 3072 * wslot
            wsem = "DWU" if wslot == 0 else "DWU2"
            dwu = p.g(lambda e, m=m, woff=woff: e.dma_start(
                Wu[:, woff:woff + 3072], dram["Upm"][:, 3072 * m:3072 * (m + 1)]),
                wait=[wu_free[wslot]] + ([ffn1_T_end] if m < 2 else []), sem=wsem)
            wu_mark = (wsem, dwu)
            state = {"s1_done": {}}

            def mk_s1(tt, pi):
                def run(tt=tt, pi=pi, m=m, woff=woff, wu_mark=wu_mark):
                    tile, ti, pw = pool4.acq()
                    for c in range(CH):
                        p.t(lambda e, tile=tile, m=m, pi=pi, tt=tt, c=c, woff=woff: e.matmul(
                            tile[:, :],
                            Wu[:, woff + 1024 * pi + 128 * c:woff + 1024 * pi + 128 * (c + 1)],
                            x1b[:, T * c + TQ * tt:T * c + TQ * tt + TQ],
                            start=(c == 0), stop=(c == CH - 1)),
                            wait=([wu_mark, pq_free[pi]] if c == 0 else []) +
                                 [x1_chunk_marks[c]] +
                                 ([pw] if (pw and c == 0) else []),
                            inc=(c == CH - 1))
                    tmm = p.c["T"]
                    vm = p.v(lambda e, tile=tile, pi=pi, tt=tt: e.tensor_copy(
                        Pq[:, T * pi + TQ * tt:T * pi + TQ * tt + TQ], tile[:, :]),
                        wait=[("T", tmm)], inc=True)
                    pool4.rel(ti, ("V", vm))
                    if tt == NTQ - 1:
                        state["s1_done"][pi] = ("V", vm)
                return run

            def mk_s2qk(tt, pi):
                def run(tt=tt, pi=pi, m=m):
                    dstb = QB if pi == 0 else KB
                    tile, ti, pw = pool4.acq()
                    p.t(lambda e, tile=tile, pi=pi, m=m, tt=tt: e.matmul(
                        tile[:, :], V2s[:, 1024 * pi + 128 * m:1024 * pi + 128 * (m + 1)],
                        Pq[:, T * pi + TQ * tt:T * pi + TQ * tt + TQ],
                        start=True, stop=True),
                        wait=[state["s1_done"][pi], ("DV2", d_v2)] + ([pw] if pw else []),
                        inc=True)
                    tmm = p.c["T"]
                    sm = p.v(lambda e, tile=tile, dstb=dstb, m=m, tt=tt, pi=pi: e.tensor_scalar(
                        dstb[:, T * m + TQ * tt:T * m + TQ * tt + TQ], tile[:, :],
                        scol(qb_of[pi], m), None, ALU.add),
                        wait=[("T", tmm)] +
                             (spill_gate if (m == 0 and pi == 0 and tt == 0) else []),
                        inc=True)
                    pool4.rel(ti, ("V", sm))
                    if tt == NTQ - 1:
                        pq_free[pi] = ("T", p.c["T"])
                        if pi == 1:
                            qk_ready[m] = ("V", p.c["V"])
                return run

            def mk_s2v(tg):
                def run(tg=tg, m=m, wslot=wslot):
                    tile, ti, pw = pool4.acq()
                    for k in range(4):
                        tc = tg * 4 + k
                        p.t(lambda e, tile=tile, m=m, tc=tc, k=k: e.matmul(
                            tile[:, 128 * k:128 * (k + 1)],
                            Pq[:, 2 * T + 128 * tc:2 * T + 128 * (tc + 1)],
                            V2s[:, 2048 + 128 * m:2048 + 128 * (m + 1)],
                            start=True, stop=True),
                            wait=([state["s1_done"][2], ("V", ones_set), ("DV2", d_v2)] if k == 0 else []) +
                                 ([pw] if (pw and k == 0) else []),
                            inc=(k == 3))
                    tmm = p.c["T"]
                    dst = VA_view[:, tg * 4:tg * 4 + 4, 2 * m:2 * m + 2, 0:DH]
                    src_ = tile[:, :].rearrange("p (tc rest) -> p tc rest", rest=128) \
                                     .rearrange("p tc (he e) -> p tc he e", e=DH)
                    vm = p.v(lambda e, dst=dst, src_=src_: e.tensor_copy(dst, src_),
                             wait=[("T", tmm)], inc=True)
                    pool4.rel(ti, ("V", vm))
                    if tg == 3:
                        pq_free[2] = ("T", p.c["T"])
                        va_ready[m] = ("V", p.c["V"])
                        wu_free[wslot] = ("T", p.c["T"])
                return run

            for pi in range(2):
                for tt in range(NTQ):
                    qkv_queue.append((m, mk_s1(tt, pi)))
                for tt in range(NTQ):
                    qkv_queue.append((m, mk_s2qk(tt, pi)))
            for tt in range(NTQ):
                qkv_queue.append((m, mk_s1(tt, 2)))
            for tg in range(4):
                qkv_queue.append((m, mk_s2v(tg)))

        sd_free = [("S", 0), ("S", 0)]
        at_free = {(s, pp): ("T", 0) for s in range(2) for pp in range(2)}
        ct_free = [("DCT0", 0), ("DCT1", 0)]
        rl_free = [("V", 0), ("V", 0)]
        brl_free = [("V", 0), ("V", 0)]
        tail_idx = [0]
        pending_tails = [[]]
        hooks = {}

        def make_tail(hc, hp, j, tC, iC, t_av):
            par = tail_idx[0] % 2
            tail_idx[0] += 1

            def tailA():
                p.s(lambda e: e.activation(
                    lnt[64:65, 0:TQ], tC[DH:DH + 1, :], AF.Ln),
                    wait=[("T", t_av), rl_free[par]], inc=True)
                vr_ = p.s(lambda e: e.activation(
                    rlb[64:65, TQ * par:TQ * par + TQ], lnt[64:65, 0:TQ],
                    AF.Exp, scale=-1.0), inc=True)
                return vr_

            def tailB(vr):
                tileB, tiB, pwB = pool4.acq()
                tb = p.t(lambda e, tileB=tileB: e.matmul(
                    tileB[0:64, :], bsl("ones", 0, 64, p0=64, pn=1),
                    rlb[64:65, TQ * par:TQ * par + TQ], start=True, stop=True),
                    wait=[("S", vr)] + ([pwB] if pwB else []), inc=True)
                rl_free[par] = ("T", tb)
                sb_ = p.v(lambda e, tileB=tileB: e.tensor_copy(
                    brl[0:64, TQ * par:TQ * par + TQ], tileB[0:64, :]),
                    wait=[("T", tb), brl_free[par]], inc=True)
                pool4.rel(tiB, ("V", sb_))
                if hp == 0:
                    vm = p.v(lambda e: e.tensor_tensor(
                        QB[0:64, T * hc + TQ * j:T * hc + TQ * j + TQ], tC[0:64, :],
                        brl[0:64, TQ * par:TQ * par + TQ], ALU.mult),
                        wait=[("V", sb_)], inc=True)
                    brl_free[par] = ("V", vm)
                    pool4.rel(iC, ("V", vm))
                else:
                    cpar = par
                    vm = p.v(lambda e: e.tensor_tensor(
                        CTb[0:64, TQ * cpar:TQ * cpar + TQ], tC[0:64, :],
                        brl[0:64, TQ * par:TQ * par + TQ], ALU.mult),
                        wait=[("V", sb_), ct_free[cpar]], inc=True)
                    brl_free[par] = ("V", vm)
                    pool4.rel(iC, ("V", vm))
                    dm = p.y(lambda e: e.dma_start(
                        QB[64:128, T * hc + TQ * j:T * hc + TQ * j + TQ],
                        CTb[0:64, TQ * cpar:TQ * cpar + TQ]), wait=[("V", vm)],
                        sem="DCT%d" % cpar)
                    ct_free[cpar] = ("DCT%d" % cpar, dm)

            def tail():
                vr = tailA()
                return lambda vr=vr: tailB(vr)
            return tail

        def emit_attention_hc(hc):
            drain_tag(hc)
            for j in range(NTQ):
                kcm = 4 * (j + 1)
                npair = kcm // 2
                exp_marks = [{}, {}]
                msk_marks = [{}, {}]
                ctx_hold = [{}, {}]

                def emit_qk_quad(ip, hc=hc, j=j, exp_marks=exp_marks, msk_marks=msk_marks):
                    i0 = 2 * ip
                    if not drain_qkv(2 if j <= 1 else 1):
                        for s in range(2):
                            p.t(lambda e, s=s: e.matmul(
                                SD[s][:, 0:TQ], bsl("masks", 0, 128), bsl("masks", 0, TQ),
                                start=True, stop=True), wait=[sd_free[s]])
                    for u in range(2):
                        i = i0 + u
                        for s in range(2):
                            r0 = 64 * s
                            p.t(lambda e, s=s, u=u, i=i, r0=r0, hc=hc, j=j: e.matmul(
                                SD[s][:, TQ * u:TQ * (u + 1)],
                                KB[r0:r0 + 64, T * hc + 128 * i:T * hc + 128 * (i + 1)],
                                QB[r0:r0 + 64, T * hc + TQ * j:T * hc + TQ * j + TQ],
                                start=True, stop=True),
                                wait=([sd_free[s]] if u == 0 else []) +
                                     ([qk_ready[hc]] if (u == 0 and s == 0 and ip == 0 and j == 0) else []),
                                inc=(u == 1 and s == 1))
                    tmm = p.c["T"]
                    pp = ip % 2
                    for s in range(2):
                        sm = p.s(lambda e, s=s, i0=i0, pp=pp: e.activation(
                            ATb[:, TQ * (4 * s + 2 * pp):TQ * (4 * s + 2 * pp) + 2 * TQ],
                            SD[s][:, :], AF.Exp, scale=0.125),
                            wait=[("T", tmm), at_free[(s, pp)]], inc=True)
                        sd_free[s] = ("S", sm)
                        for u in range(2):
                            i = i0 + u
                            exp_marks[s][i] = ("S", sm)
                            if i >= 4 * j:
                                d = i - 4 * j
                                slot = 4 * s + 2 * pp + u
                                mm_ = p.v(lambda e, slot=slot, d=d: e.tensor_tensor(
                                    ATb[:, TQ * slot:TQ * slot + TQ],
                                    ATb[:, TQ * slot:TQ * slot + TQ],
                                    bsl("masks", TQ * d, TQ), ALU.mult),
                                    wait=[("S", sm)], inc=True)
                                msk_marks[s][i] = ("V", mm_)

                def emit_av_quad(ip, last, hc=hc, exp_marks=exp_marks, msk_marks=msk_marks,
                                 ctx_hold=ctx_hold):
                    i0 = 2 * ip
                    for s in range(2):
                        h = 2 * hc + s
                        for u in range(2):
                            i = i0 + u
                            w = [msk_marks[s].get(i, exp_marks[s][i])]
                            if i == 0:
                                tC, iC, pwC = pool4.acq()
                                ctx_hold[s]["t"] = tC
                                ctx_hold[s]["i"] = iC
                                if pwC:
                                    w.append(pwC)
                                w.append(va_ready[hc])
                            tC = ctx_hold[s]["t"]
                            pp = ip % 2
                            slot = 4 * s + 2 * pp + u
                            is_last = last and u == 1
                            p.t(lambda e, i=i, h=h, tC=tC, slot=slot, is_last=is_last: e.matmul(
                                tC[0:DH + 1, :], VA[:, VAW * i + 65 * h:VAW * i + 65 * h + 65],
                                ATb[:, TQ * slot:TQ * slot + TQ],
                                start=(i == 0), stop=is_last),
                                wait=w, inc=is_last)
                            if u == 1:
                                at_free[(s, pp)] = ("T", p.c["T"])

                emit_qk_quad(0)
                if npair == 1:
                    for t_ in pending_tails[0]:
                        t_()
                    pending_tails[0] = []
                for ip in range(1, npair):
                    emit_qk_quad(ip)
                    if ip == 1:
                        for t_ in pending_tails[0]:
                            t_()
                        pending_tails[0] = []
                    emit_av_quad(ip - 1, False)
                emit_av_quad(npair - 1, True)
                t_av = p.c["T"]
                pending_tails[0] = [t_() for t_ in [
                    make_tail(hc, 0, j, ctx_hold[0]["t"], ctx_hold[0]["i"], t_av),
                    make_tail(hc, 1, j, ctx_hold[1]["t"], ctx_hold[1]["i"], t_av)]]
                if hc == 4 and j == 0:
                    hooks["hc4"] = ("T", p.c["T"])

        for m in range(CH):
            emit_qkv_pair(m)
            if m >= 1:
                emit_attention_hc(m - 1)
                if m - 1 == 3:
                    hooks["hc3_done"] = ("T", p.c["T"])
                if m - 1 == 6:
                    hooks["hc6_done"] = ("T", p.c["T"])
        drain_tag(CH)
        hooks["qkv_done"] = ("T", p.c["T"])
        emit_dump("k", KB, [("S", p.c["S"])], bf16=True)
        emit_dump("va", VA, [("V", p.c["V"])], bf16=True)
        p.flush()
        pa.close()
        esx1.close()

        esH2 = ExitStack()
        sbH2 = esH2.enter_context(nc.sbuf_tensor("H2", [128, CH * T], F32, side="left"))
        h2v = sbH2[:, :].rearrange("p (c t) -> p c t", t=T)
        spv = spill_h.rearrange("p (c t) -> p c t", t=T)
        gate_h2 = [("T", p.c["T"]), ("S", p.c["S"])]
        d_rel_h = []
        sems_rh = ["DSH", "DS2"]
        for hh in range(2):
            d_rel_h.append(p.y(lambda e, hh=hh: e.dma_start(
                h2v[:, 4 * hh:4 * (hh + 1), :], spv[:, 4 * hh:4 * (hh + 1), :]),
                wait=gate_h2, sem=sems_rh[hh]))
        emit_attention_hc(CH - 1)
        for t_ in pending_tails[0]:
            t_()
        pending_tails[0] = []
        ctx_dma_done = [("DCT0", p.c["DCT0"]), ("DCT1", p.c["DCT1"])]
        emit_dump("ctx", QB, [("V", p.c["V"])] + ctx_dma_done, bf16=True)
        p.barrier("tvs")
        p.t(lambda e: e.nop(), wait=ctx_dma_done)
        p.flush()
        psB.close()
        pb.close()
        esKV.close()

        ps8b = ExitStack()
        pool_tiles2 = [ps8b.enter_context(nc.psum_tensor("pq%d" % i, [128, TQ], F32))
                       for i in range(8)]
        pool = Pool(p, pool_tiles2)

        po = ExitStack()
        oU = po.enter_context(nc.sbuf_tensor("oU", [128, CH * 768], BF16, side="right"))
        oV = po.enter_context(nc.sbuf_tensor("oV", [128, 6 * D], BF16, side="right"))
        O1 = po.enter_context(nc.sbuf_tensor("O1", [128, 6 * T], BF16, side="right"))
        b1sb = po.enter_context(nc.sbuf_tensor("b1sb", [128, 6], F32, side="right"))
        d_oU = p.g(lambda e: e.dma_start(oU[:, :], dram["outU"]),
                   wait=[hooks["hc3_done"]], sem="DOU")
        d_oV = p.g(lambda e: e.dma_start(oV[:, :], dram["outV"]),
                   wait=[hooks["hc6_done"]], sem="DOV")
        tile, ti, pw = pool.acq()
        for m in range(6):
            for c in range(CH):
                p.t(lambda e, tile=tile, m=m, c=c: e.matmul(
                    tile[:, m:m + 1], oU[:, 768 * c + 128 * m:768 * c + 128 * (m + 1)],
                    bsl("vb_bf", c, 1), start=(c == 0), stop=(c == CH - 1)),
                    wait=([("DOU", d_oU)] if (m == 0 and c == 0) else []) +
                         ([pw] if (pw and m == 0 and c == 0) else []),
                    inc=(m == 5 and c == CH - 1))
        tb1 = p.c["T"]
        vb1 = p.v(lambda e, tile=tile: e.tensor_copy(b1sb[:, :], tile[:, 0:6]),
                  wait=[("T", tb1)], inc=True)
        pool.rel(ti, ("V", vb1))
        for m in range(6):
            for tt in range(NTQ):
                tile, ti, pw = pool.acq()
                for c in range(CH):
                    p.t(lambda e, tile=tile, m=m, tt=tt, c=c: e.matmul(
                        tile[:, :], oU[:, 768 * c + 128 * m:768 * c + 128 * (m + 1)],
                        QB[:, T * c + TQ * tt:T * c + TQ * tt + TQ],
                        start=(c == 0), stop=(c == CH - 1)),
                        wait=([pw] if (pw and c == 0) else []), inc=(c == CH - 1))
                tmm = p.c["T"]
                sm = p.s(lambda e, tile=tile, m=m, tt=tt: e.activation(
                    O1[:, T * m + TQ * tt:T * m + TQ * tt + TQ], tile[:, :], AF.Identity,
                    bias=b1sb[:, m:m + 1], scale=1.0),
                    wait=[("T", tmm), ("V", vb1)], inc=True)
                pool.rel(ti, ("S", sm))
        s1_done = ("S", p.c["S"])
        h2_chunk_marks = {}
        for do in range(CH):
            for tt in range(NTQ):
                tile, ti, pw = pool.acq()
                for k6 in range(6):
                    p.t(lambda e, tile=tile, do=do, tt=tt, k6=k6: e.matmul(
                        tile[:, :], oV[:, D * k6 + 128 * do:D * k6 + 128 * (do + 1)],
                        O1[:, T * k6 + TQ * tt:T * k6 + TQ * tt + TQ],
                        start=(k6 == 0), stop=(k6 == 5)),
                        wait=([s1_done, ("DOV", d_oV)] if k6 == 0 else []) +
                             ([pw] if (pw and k6 == 0) else []),
                        inc=(k6 == 5))
                tmm = p.c["T"]
                vm = p.v(lambda e, tile=tile, do=do, tt=tt: e.scalar_tensor_tensor(
                    sbH2[:, T * do + TQ * tt:T * do + TQ * tt + TQ], tile[:, :],
                    scol("outb", do), sbH2[:, T * do + TQ * tt:T * do + TQ * tt + TQ],
                    ALU.add, ALU.add),
                    wait=[("T", tmm), (sems_rh[do // 4], d_rel_h[do // 4])], inc=True)
                pool.rel(ti, ("V", vm))
                h2_chunk_marks[do] = ("V", vm)
        mark_h = ("V", p.c["V"])
        emit_dump("h", sbH2, [mark_h])
        p.barrier("tvs")
        p.flush()
        po.close()
        esQB.close()
        if stop_after == "outproj":
            finalize_early(sbH2)
            return nc

        pm0 = ExitStack()
        scr2 = pm0.enter_context(nc.sbuf_tensor("scr2", [128, 2 * CH * TH], BF16, side="right"))
        mark_b2 = emit_ln_tiles(pool, sbH2, scr2[:, 0:CH * TH], [],
                                [h2_chunk_marks[c] for c in range(CH)],
                                scratch2=scr2[:, CH * TH:2 * CH * TH])
        p.flush()
        pm0.close()

        pm = ExitStack()
        f1U = pm.enter_context(nc.sbuf_tensor("f1U", [128, CH * 512], BF16, side="right"))
        f1V = pm.enter_context(nc.sbuf_tensor("f1V", [128, 4 * 4096], BF16, side="right"))
        f2U = pm.enter_context(nc.sbuf_tensor("f2U", [128, 32 * 512], BF16, side="right"))
        f2V = pm.enter_context(nc.sbuf_tensor("f2V", [128, 4 * D], BF16, side="right"))
        x2t = pm.enter_context(nc.sbuf_tensor("x2t", [128, 2 * CH * TQ], BF16, side="right"))
        P1t = pm.enter_context(nc.sbuf_tensor("P1t", [128, 4 * TQ], BF16, side="right"))
        P2t = pm.enter_context(nc.sbuf_tensor("P2t", [128, 4 * TQ], BF16, side="right"))
        HMt = pm.enter_context(nc.sbuf_tensor("HMt", [128, 2 * TQ], BF16, side="right"))
        mlp_gate = ("T", p.c["T"])
        b2m = pm.enter_context(nc.sbuf_tensor("b2m", [128, 4], F32, side="right"))
        d_f1U = p.g(lambda e: e.dma_start(f1U[:, :], dram["fc1U"]), wait=[mlp_gate], sem="DF1U")
        d_f1V = p.g(lambda e: e.dma_start(f1V[:, :], dram["fc1V"]), wait=[mlp_gate], sem="DF1V")
        d_f2U = p.g(lambda e: e.dma_start(f2U[:, :], dram["fc2U"]), wait=[mlp_gate], sem="DF2U")
        d_f2V = p.g(lambda e: e.dma_start(f2V[:, :], dram["fc2V"]), wait=[mlp_gate], sem="DF2V")

        tileb, tib, pwb = pool.acq()
        for pc in range(4):
            for c in range(CH):
                p.t(lambda e, tileb=tileb, pc=pc, c=c: e.matmul(
                    tileb[:, pc:pc + 1], f1U[:, 512 * c + 128 * pc:512 * c + 128 * (pc + 1)],
                    bsl("b2_bf", c, 1), start=(c == 0), stop=(c == CH - 1)),
                    wait=([("DF1U", d_f1U)] if (pc == 0 and c == 0) else []) +
                         ([pwb] if (pwb and pc == 0 and c == 0) else []),
                    inc=(pc == 3 and c == CH - 1))
        t_b2m = p.c["T"]
        v_b2m = p.v(lambda e, tileb=tileb: e.tensor_copy(b2m[:, :], tileb[:, 0:4]),
                    wait=[("T", t_b2m)], inc=True)
        pool.rel(tib, ("V", v_b2m))
        g2_marks = []
        for c in range(CH):
            smsc = p.s(lambda e, c=c: e.activation(
                f1U[:, 512 * c:512 * (c + 1)], f1U[:, 512 * c:512 * (c + 1)],
                AF.Identity, bias=0.0, scale=scol("g2", c)),
                wait=[("T", t_b2m)], inc=True)
            g2_marks.append(smsc)
        f1u_ready = ("S", g2_marks[-1])

        x2_free = [("T", 0), ("T", 0)]
        p1_free = ("T", 0)
        p2_free = ("V", 0)

        def emit_x2(jt):
            slot = jt % 2
            x0 = CH * TQ * slot
            marks = []
            for c in range(CH):
                p.v(lambda e, c=c, jt=jt, x0=x0: e.tensor_tensor(
                    x2t[:, x0 + TQ * c:x0 + TQ * (c + 1)],
                    sbH2[:, T * c + TQ * jt:T * c + TQ * jt + TQ],
                    bA[:, TQ * jt:TQ * jt + TQ], ALU.mult),
                    wait=[mark_b2, x2_free[slot], mark_h], inc=True)
                vmx = p.v(lambda e, c=c, jt=jt, x0=x0: e.tensor_tensor(
                    x2t[:, x0 + TQ * c:x0 + TQ * (c + 1)], x2t[:, x0 + TQ * c:x0 + TQ * (c + 1)],
                    bB[:, TQ * jt:TQ * jt + TQ], ALU.add), inc=True)
                marks.append(("V", vmx))
            return marks

        mlp_chunk_marks = {}
        x2_marks_by_tile = {0: emit_x2(0)}
        for jt in range(NTQ):
            slot = jt % 2
            x0 = CH * TQ * slot
            x2_marks = x2_marks_by_tile[jt]
            p1_marks = []
            for pc in range(4):
                tile, ti, pw = pool.acq()
                for c in range(CH):
                    p.t(lambda e, tile=tile, pc=pc, c=c, x0=x0: e.matmul(
                        tile[:, :], f1U[:, 512 * c + 128 * pc:512 * c + 128 * (pc + 1)],
                        x2t[:, x0 + TQ * c:x0 + TQ * (c + 1)],
                        start=(c == 0), stop=(c == CH - 1)),
                        wait=([x2_marks[-1], f1u_ready, p1_free] if c == 0 else []) +
                             ([pw] if (pw and c == 0) else []),
                        inc=(c == CH - 1))
                tmm = p.c["T"]
                vm = p.s(lambda e, tile=tile, pc=pc: e.activation(
                    P1t[:, TQ * pc:TQ * (pc + 1)], tile[:, :], AF.Identity,
                    bias=b2m[:, pc:pc + 1], scale=1.0),
                    wait=[("T", tmm), ("V", v_b2m)], inc=True)
                pool.rel(ti, ("S", vm))
                p1_marks.append(vm)
            p1_done = ("S", p1_marks[-1])
            s1_T = p.c["T"]
            if jt + 1 < NTQ:
                x2_marks_by_tile[jt + 1] = emit_x2(jt + 1)
            acc = []
            for p2 in range(4):
                tileA, tiA, pwA = pool.acq()
                acc.append((tileA, tiA, pwA))
            hm_free = [("T", 0), ("T", 0)]
            for ic in range(32):
                tile, ti, pw = pool.acq()
                for pc in range(4):
                    p.t(lambda e, tile=tile, ic=ic, pc=pc: e.matmul(
                        tile[:, :], f1V[:, 4096 * pc + 128 * ic:4096 * pc + 128 * (ic + 1)],
                        P1t[:, TQ * pc:TQ * (pc + 1)],
                        start=(pc == 0), stop=(pc == 3)),
                        wait=([p1_done, ("DF1V", d_f1V)] if pc == 0 else []) +
                             ([pw] if (pw and pc == 0) else []),
                        inc=(pc == 3))
                tmm = p.c["T"]
                hslot = ic % 2
                smg = p.s(lambda e, tile=tile, ic=ic, hslot=hslot: e.activation(
                    HMt[:, TQ * hslot:TQ * (hslot + 1)], tile[:, :], AF.Gelu_apprx_tanh,
                    bias=scol("fc1b", ic), scale=1.0),
                    wait=[("T", tmm), hm_free[hslot]], inc=True)
                pool.rel(ti, ("S", smg))
                for p2 in range(4):
                    tileA, tiA, pwA = acc[p2]
                    p.t(lambda e, tileA=tileA, ic=ic, p2=p2, hslot=hslot: e.matmul(
                        tileA[:, :], f2U[:, 512 * ic + 128 * p2:512 * ic + 128 * (p2 + 1)],
                        HMt[:, TQ * hslot:TQ * (hslot + 1)],
                        start=(ic == 0), stop=(ic == 31)),
                        wait=[("S", smg), ("DF2U", d_f2U)] +
                             ([pwA] if (pwA and ic == 0) else []) +
                             ([p2_free] if ic == 0 else []),
                        inc=(p2 == 3))
                hm_free[hslot] = ("T", p.c["T"])
            t_acc = p.c["T"]
            for p2 in range(4):
                tileA, tiA, _ = acc[p2]
                vm = p.s(lambda e, tileA=tileA, p2=p2: e.activation(
                    P2t[:, TQ * p2:TQ * (p2 + 1)], tileA[:, :], AF.Identity,
                    bias=0.0, scale=1.0), wait=[("T", t_acc)], inc=True)
                pool.rel(tiA, ("S", vm))
            p2_done = ("S", p.c["S"])
            for do in range(CH):
                tile, ti, pw = pool.acq()
                for p2 in range(4):
                    p.t(lambda e, tile=tile, do=do, p2=p2: e.matmul(
                        tile[:, :], f2V[:, D * p2 + 128 * do:D * p2 + 128 * (do + 1)],
                        P2t[:, TQ * p2:TQ * (p2 + 1)],
                        start=(p2 == 0), stop=(p2 == 3)),
                        wait=([p2_done, ("DF2V", d_f2V)] if p2 == 0 else []) +
                             ([pw] if (pw and p2 == 0) else []),
                        inc=(p2 == 3))
                tmm = p.c["T"]
                vm = p.v(lambda e, tile=tile, do=do, jt=jt: e.scalar_tensor_tensor(
                    sbH2[:, T * do + TQ * jt:T * do + TQ * jt + TQ], tile[:, :],
                    scol("fc2b", do), sbH2[:, T * do + TQ * jt:T * do + TQ * jt + TQ],
                    ALU.add, ALU.add), wait=[("T", tmm)], inc=True)
                pool.rel(ti, ("V", vm))
                mlp_chunk_marks[do] = ("V", vm)
            x2_free[slot] = ("T", t_acc)
            p1_free = ("T", t_acc)
            p2_free = ("V", p.c["V"])
        mark_mlp = ("V", p.c["V"])
        emit_dump("mlp", sbH2, [mark_mlp])
        p.barrier("tvs")
        p.flush()
        pm.close()
        if stop_after == "mlp":
            finalize_early(sbH2)
            return nc

        esS2 = ExitStack()
        sbS2 = esS2.enter_context(nc.sbuf_tensor("S2", [128, CH * T], F32, side="right"))
        s2v = sbS2[:, :].rearrange("p (c t) -> p c t", t=T)
        sp2v = spill_s.rearrange("p (c t) -> p c t", t=T)
        gate_s2 = [("T", p.c["T"]), ("V", p.c["V"]), ("S", p.c["S"])]
        d_rel = []
        sems_rel = ["DSS", "DS2"]
        for hh in range(2):
            d_rel.append(p.y(lambda e, hh=hh: e.dma_start(
                s2v[:, 4 * hh:4 * (hh + 1), :], sp2v[:, 4 * hh:4 * (hh + 1), :]),
                wait=gate_s2, sem=sems_rel[hh]))
        add_marks_ch = []
        for c in range(CH):
            vm = p.v(lambda e, c=c: e.tensor_tensor(
                sbS2[:, T * c:T * (c + 1)], sbS2[:, T * c:T * (c + 1)],
                sbH2[:, T * c:T * (c + 1)], ALU.add),
                wait=[(sems_rel[c // 4], d_rel[c // 4]), mlp_chunk_marks[c]], inc=True)
            add_marks_ch.append(("V", vm))
        p.flush()
        esH2.close()

        ffn2 = ExitStack()
        XRb2 = ffn2.enter_context(nc.sbuf_tensor("XR2", [128, CH * TH], BF16, side="right"))
        XKb2 = ffn2.enter_context(nc.sbuf_tensor("XK2", [128, CH * TH], BF16, side="right"))
        KAb2 = ffn2.enter_context(nc.sbuf_tensor("KA2", [128, CH * TH], BF16, side="right"))
        wrS2 = ffn2.enter_context(nc.sbuf_tensor("wrS2", [128, CH * D], BF16, side="right"))
        wkS2 = ffn2.enter_context(nc.sbuf_tensor("wkS2", [128, CH * D], BF16, side="right"))
        wvS2 = ffn2.enter_context(nc.sbuf_tensor("wvS2", [128, CH * D], BF16, side="right"))
        rt2 = ffn2.enter_context(nc.sbuf_tensor("rt2", [128, TQ], BF16, side="right"))
        krl2 = ffn2.enter_context(nc.sbuf_tensor("krl2", [128, TQ], BF16, side="right"))
        t1f2 = ffn2.enter_context(nc.sbuf_tensor("t1f2", [128, TQ], F32, side="right"))
        dwk2 = p.g(lambda e: e.dma_start(wkS2[:, :], dram["wkT"]), sem="DWK")
        dwv2 = p.g(lambda e: e.dma_start(wvS2[:, :], dram["wvT"]), sem="DWV")
        dwr2 = p.g(lambda e: e.dma_start(wrS2[:, :], dram["wrT"]), sem="DWR")
        wm2 = {"wk": ("DWK", dwk2), "wv": ("DWV", dwv2), "wr": ("DWR", dwr2)}
        mark_ffn2, ffn2_chunk_marks = emit_ffn(sbS2, XRb2, XKb2, KAb2, wrS2, wkS2,
                                               wvS2, rt2, t1f2, krl2, wm2, [],
                                               add_marks_ch)
        emit_dump("out_fm", sbS2, [mark_ffn2])
        p.barrier("tvs")
        p.flush()
        ffn2.close()
        if stop_after == "ffn2":
            finalize_early(sbS2)
            return nc

        xv_out = x_out.rearrange("p (c t) -> p c t", t=T)
        sv_out = sbS2[:, :].rearrange("p (c t) -> p c t", t=T)
        for hh in range(2):
            w = [ffn2_chunk_marks[c] for c in range(4 * hh, 4 * hh + 4)]
            p.y(lambda e, hh=hh: e.dma_start(
                xv_out[:, 4 * hh:4 * (hh + 1), :], sv_out[:, 4 * hh:4 * (hh + 1), :]),
                wait=w, sem=["DO0", "DO1"][hh])
        p._mk("y", lambda eng: eng.nop(),
              [(sn, p.c[sn]) for sn in DMA_SEMS if p.c[sn] > 0], [])
        p.flush()
        ps8b.close()
        esS2.close()

    return nc



def kernel(**inputs):
    host = prep_host(inputs)
    nc = build_program(host)
    x = _f32(inputs["L0_attn_hidden_states"])
    in_maps = []
    for c in range(8):
        m = {k: host[k] for k in WTENSORS}
        m["x_in"] = feat_chunks(np.ascontiguousarray(x[c].T))
        in_maps.append(m)
    res = run_bass_kernel_spmd(nc, in_maps, core_ids=list(range(8)))
    outs = []
    for c in range(8):
        fm = np.asarray(res.results[c]["x_out"])
        td = fm.reshape(128, CH, T).transpose(1, 0, 2).reshape(D, T).T
        outs.append(np.ascontiguousarray(td))
    return np.stack(outs, axis=0)


# version 31
# speedup vs baseline: 1.0070x; 1.0070x over previous
import numpy as np
from contextlib import ExitStack

import concourse.bass as bass
import concourse.mybir as mybir
from concourse.bass_utils import run_bass_kernel_spmd

F32 = mybir.dt.float32
BF16 = mybir.dt.bfloat16
AF = mybir.ActivationFunctionType
ALU = mybir.AluOpType

D = 1024
T = 2048
CH = D // 128
NH = 16
DH = 64
RQ = 48
TQ = 512
NTQ = T // TQ
TH = 1024
EPS = 1e-5
INVD = 1.0 / D
VAW = NH * (DH + 1)

CONST_VALS = [0.0, 1.0, -1.0, 0.5, -0.5, 0.125, INVD, EPS]

WTENSORS = ["wrT", "wkT", "wvT", "Upm", "V2",
            "outU", "outV", "fc1U", "fc1V", "fc2U", "fc2V",
            "smalls", "rows", "bconsts", "ident"]


def _f32(a):
    a = np.asarray(a)
    assert a.dtype == np.float32, a.dtype
    return a

def feat_chunks(A):
    A = np.ascontiguousarray(_f32(A))
    C, X = A.shape
    return np.ascontiguousarray(A.reshape(C // 128, 128, X).transpose(1, 0, 2).reshape(128, -1))

def vec_cols(v):
    v = _f32(v).reshape(-1)
    return np.ascontiguousarray(v.reshape(-1, 128).T)

def pack_U(U):
    U = _f32(U)
    Up = np.zeros((D, 1024), np.float32)
    for h in range(NH):
        c0 = 128 * (h // 2) + 64 * (h % 2)
        Up[:, c0:c0 + RQ] = U[:, h, :]
    return feat_chunks(Up)

def pack_V2(Vw):
    Vw = _f32(Vw)
    P = np.zeros((128, 1024), np.float32)
    for h in range(NH):
        P[64 * (h % 2):64 * (h % 2) + RQ, DH * h:DH * h + DH] = Vw[h]
    return P

def build_masks():
    q = np.arange(TQ)[None, :]
    k = np.arange(128)[:, None]
    return np.concatenate([(q >= 128 * d + k).astype(np.float32) for d in range(4)], axis=1)


def prep_host(inputs):
    g = {}
    g["wrT"] = feat_chunks(np.ascontiguousarray(_f32(inputs["L0_ffn_W_r"]).T))
    g["wkT"] = feat_chunks(np.ascontiguousarray(_f32(inputs["L0_ffn_W_k"]).T))
    g["wvT"] = feat_chunks(np.ascontiguousarray(_f32(inputs["L0_ffn_W_v"]).T))
    qU = pack_U(inputs["L0_attn_q_U"])
    kU = pack_U(inputs["L0_attn_k_U"])
    vU = pack_U(inputs["L0_attn_v_U"])
    Upm = np.zeros((128, 8 * 3072), np.float32)
    for m in range(8):
        for pi, Uf in enumerate((qU, kU, vU)):
            for c in range(8):
                Upm[:, 3072 * m + 1024 * pi + 128 * c:3072 * m + 1024 * pi + 128 * (c + 1)] = \
                    Uf[:, 1024 * c + 128 * m:1024 * c + 128 * (m + 1)]
    g["Upm"] = np.ascontiguousarray(Upm)
    g["V2"] = np.ascontiguousarray(np.concatenate(
        [pack_V2(inputs["L0_attn_q_V"]), pack_V2(inputs["L0_attn_k_V"]),
         pack_V2(inputs["L0_attn_v_V"])], axis=1))
    g["outU"] = feat_chunks(inputs["L0_attn_out_U"])
    g["outV"] = feat_chunks(inputs["L0_attn_out_V"])
    g["fc1U"] = feat_chunks(inputs["L0_attn_fc1_U"])
    g["fc1V"] = feat_chunks(inputs["L0_attn_fc1_V"])
    g["fc2U"] = feat_chunks(inputs["L0_attn_fc2_U"])
    g["fc2V"] = feat_chunks(inputs["L0_attn_fc2_V"])

    cols, reg = [], {}
    def add(name, arr):
        arr = _f32(arr)
        if arr.ndim == 1:
            arr = arr[:, None]
        reg[name] = (sum(c.shape[1] for c in cols), arr.shape[1])
        cols.append(arr)
    add("consts", np.broadcast_to(np.array(CONST_VALS, np.float32),
                                  (128, len(CONST_VALS))).copy())
    add("qb", vec_cols(_f32(inputs["L0_attn_q_b"]).reshape(-1)))
    add("kb", vec_cols(_f32(inputs["L0_attn_k_b"]).reshape(-1)))
    add("outb", vec_cols(inputs["L0_attn_out_b"]))
    add("fc1b", vec_cols(inputs["L0_attn_fc1_b"]))
    add("fc2b", vec_cols(inputs["L0_attn_fc2_b"]))
    add("mur", vec_cols(inputs["L0_ffn_mu_r"]))
    add("muk", vec_cols(inputs["L0_ffn_mu_k"]))
    add("g1", vec_cols(inputs["L0_attn_ln1_g"]))
    add("b1", vec_cols(inputs["L0_attn_ln1_b"]))
    add("g2", vec_cols(inputs["L0_attn_ln2_g"]))
    add("b2", vec_cols(inputs["L0_attn_ln2_b"]))
    g["smalls"] = np.ascontiguousarray(np.concatenate(cols, axis=1))
    g["smalls_reg"] = reg

    g["rows"] = np.ones((1, 128), np.float32)

    bcols, breg = [], {}
    def addb(name, arr):
        arr = _f32(arr)
        breg[name] = (sum(c.shape[1] for c in bcols), arr.shape[1])
        bcols.append(arr)
    addb("masks", build_masks())
    addb("ones", np.ones((128, 64), np.float32))
    addb("vb_bf", vec_cols(_f32(inputs["L0_attn_v_b"]).reshape(-1)))
    addb("b2_bf", vec_cols(inputs["L0_attn_ln2_b"]))
    g["bconsts"] = np.ascontiguousarray(np.concatenate(bcols, axis=1))
    g["bconsts_reg"] = breg

    g["ident"] = np.ascontiguousarray(np.eye(128, dtype=np.float32))
    return g



DMA_SEMS = ["DC", "DBC", "DI0", "DI1", "DSS", "DS2", "DSH", "DCT0", "DCT1",
            "DO0", "DO1", "DD", "DGD", "DWK", "DWV", "DWR", "DV2", "DWU",
            "DWU2", "DOU", "DOV", "DF1U", "DF1V", "DF2U", "DF2V", "DACC0", "DACC1"]


class Prog:
    SEM_OF = {"t": "T", "v": "V", "s": "S"}

    def __init__(self, nc):
        self.nc = nc
        self.q = {k: [] for k in "tvsyg"}
        self.c = {sn: 0 for sn in ["T", "V", "S"] + DMA_SEMS}
        self.sems = None

    def _mk(self, e, fn, wait, incs):
        wait = tuple(w for w in wait if w is not None and w[1] > 0)
        def run(eng, fn=fn, wait=wait, incs=tuple(incs)):
            for sn, v in wait:
                eng.wait_ge(self.sems[sn], v)
            inst = fn(eng)
            for sn, amt in incs:
                inst = inst.then_inc(self.sems[sn], amt)
        self.q[e].append(run)

    def op(self, e, fn, wait=(), inc=False):
        sn = self.SEM_OF[e]
        if e in "vs":
            wait = list(wait) + [(sn, self.c[sn])]
            inc = True
        elif e == "t" and inc:
            wait = list(wait) + [(sn, self.c[sn])]
        incs = [(sn, 1)] if inc else []
        self._mk(e, fn, wait, incs)
        if inc:
            self.c[sn] += 1
            return self.c[sn]
        return None

    def t(self, fn, wait=(), inc=False): return self.op("t", fn, wait, inc)
    def v(self, fn, wait=(), inc=False): return self.op("v", fn, wait, inc)
    def s(self, fn, wait=(), inc=False): return self.op("s", fn, wait, inc)

    def dma(self, e, sem, fn, wait=()):
        self._mk(e, fn, wait, [(sem, 16)])
        self.c[sem] += 16
        return self.c[sem]

    def y(self, fn, wait=(), sem="DD"):  return self.dma("y", sem, fn, wait)
    def g(self, fn, wait=(), sem="DGD"): return self.dma("g", sem, fn, wait)

    def barrier(self, engines="tvs"):
        engines = [e for e in engines if e in "tvs"]
        marks = {}
        for e in engines:
            marks[e] = (self.SEM_OF[e], self.op(e, lambda eng: eng.nop(), inc=True))
        for e in engines:
            waits = [marks[o] for o in engines if o != e]
            self._mk(e, lambda eng: eng.nop(), waits, [])

    def flush(self):
        with self.nc.Block() as block:
            for e, sect in (("t", block.tensor), ("v", block.vector),
                            ("s", block.scalar), ("y", block.sync),
                            ("g", block.gpsimd)):
                ops = self.q[e]
                if ops:
                    sect(lambda eng, ops=tuple(ops): [f(eng) for f in ops])
                self.q[e] = []


class Pool:
    def __init__(self, p, tiles):
        self.p = p
        self.tiles = tiles
        self.st = [None] * len(tiles)
        self.rr = 0

    def acq(self):
        n = len(self.tiles)
        for k in range(n):
            i = (self.rr + k) % n
            if self.st[i] != "held":
                break
        else:
            raise RuntimeError("psum pool exhausted")
        self.rr = i + 1
        w = self.st[i]
        self.st[i] = "held"
        return self.tiles[i], i, w

    def rel(self, i, wait_tuple):
        assert self.st[i] == "held"
        self.st[i] = wait_tuple



def build_program(host, dumps=(), stop_after=None):
    nc = bass.Bass("TRN2", target_bir_lowering=False)
    p = Prog(nc)

    x_in = nc.dram_tensor("x_in", [128, CH * T], F32, kind="ExternalInput").ap()
    x_out = nc.dram_tensor("x_out", [128, CH * T], F32, kind="ExternalOutput").ap()
    dram = {}
    for name in WTENSORS:
        arr = host[name]
        dram[name] = nc.dram_tensor(name, list(arr.shape), F32, kind="ExternalInput").ap()
    spill_s = nc.dram_tensor("spill_s", [128, CH * T], F32, kind="Internal").ap()
    spill_h = nc.dram_tensor("spill_h", [128, CH * T], F32, kind="Internal").ap()
    dump_aps = {}
    for dname, ncols in dumps:
        dump_aps[dname] = nc.dram_tensor("dump_" + dname, [128, ncols], F32,
                                         kind="ExternalOutput").ap()
    dump_names = set(dump_aps)

    sreg = host["smalls_reg"]
    breg = host["bconsts_reg"]

    with ExitStack() as es:
        p.sems = {sn: es.enter_context(nc.semaphore(sn))
                  for sn in ["T", "V", "S"] + DMA_SEMS}

        csts = es.enter_context(nc.sbuf_tensor("csts", [128, host["smalls"].shape[1]], F32, side="left"))
        rows = es.enter_context(nc.sbuf_tensor("rows_sb", [1, 128], F32, side="left"))
        bcst = es.enter_context(nc.sbuf_tensor("bcst", [128, host["bconsts"].shape[1]], BF16, side="left"))
        ident = es.enter_context(nc.sbuf_tensor("ident_sb", [128, 128], F32, side="left"))
        statA = es.enter_context(nc.sbuf_tensor("statA", [1, T], F32, side="left"))
        bA = es.enter_context(nc.sbuf_tensor("bA", [128, T], F32, side="left"))
        bB = es.enter_context(nc.sbuf_tensor("bB", [128, T], F32, side="left"))
        bnd = es.enter_context(nc.sbuf_tensor("bnd", [128, CH], F32, side="left"))

        def scol(name, j=0):
            off, n = sreg[name]
            assert j < n, (name, j, n)
            return csts[:, off + j:off + j + 1]
        CONST_COL = {v: i for i, v in enumerate(CONST_VALS)}
        coff = sreg["consts"][0]
        def ccol(v):
            return csts[:, coff + CONST_COL[v]:coff + CONST_COL[v] + 1]
        for v in CONST_VALS:
            nc.const_aps.register(F32, float(v), ccol(v))
        def bsl(name, j0, n, p0=0, pn=128):
            off, _ = breg[name]
            return bcst[p0:p0 + pn, off + j0:off + j0 + n]

        p.y(lambda e: e.dma_start(csts[:, :], dram["smalls"]), sem="DC")
        p.y(lambda e: e.dma_start(rows[:, :], dram["rows"]), sem="DC")
        d_id = p.y(lambda e: e.dma_start(ident[:, :], dram["ident"]), sem="DC")
        d_bc = p.g(lambda e: e.dma_start(bcst[:, :], dram["bconsts"]), sem="DBC")
        W_INIT = [("DC", d_id), ("DBC", d_bc)]

        def finalize_early(buf):
            dst = x_out
            p.g(lambda e, buf=buf, dst=dst: e.dma_start(dst, buf[:, :]),
                wait=[("V", p.c["V"]), ("S", p.c["S"]), ("T", p.c["T"])], sem="DGD")
            p._mk("y", lambda eng: eng.nop(),
                  [(sn, p.c[sn]) for sn in DMA_SEMS if p.c[sn] > 0], [])
            p.flush()

        def emit_dump(name, buf, wait, bf16=False):
            if name not in dump_names:
                return
            if bf16:
                dm = p.g(lambda e, name=name, buf=buf: e.dma_start(dump_aps[name], buf[:, :]),
                         wait=list(wait), sem="DGD")
                gate = ("DGD", dm)
            else:
                dm = p.y(lambda e, name=name, buf=buf: e.dma_start(dump_aps[name], buf[:, :]),
                         wait=list(wait), sem="DD")
                gate = ("DD", dm)
            p.v(lambda e: e.nop(), wait=[gate])
            p.s(lambda e: e.nop(), wait=[gate])


        def emit_ln(pool, X, scratch, pre_w=(), pre_w_chunk=None, scratch2=None):
            inv_tile = [None] * NTQ
            neg_tile = [None] * NTQ
            half_T_end = [0, 0]
            scr_of = [scratch, scratch2 if scratch2 is not None else scratch]
            for half in range(2):
                scratch = scr_of[half]
                cast_marks = []
                for c in range(CH):
                    w = list(pre_w)
                    if pre_w_chunk is not None:
                        w.append(pre_w_chunk[c])
                    if half > 0 and scratch2 is None:
                        w.append(("T", half_T_end[0]))
                    cm = p.s(lambda e, c=c, half=half, X=X, scratch=scratch: e.activation(
                        scratch[:, TH * c:TH * (c + 1)],
                        X[:, T * c + TH * half: T * c + TH * half + TH],
                        AF.Identity), wait=w, inc=True)
                    cast_marks.append(cm)
                t1a, i1a, pw1a = pool.acq()
                t1b, i1b, pw1b = pool.acq()
                s1tiles = [t1a, t1b]
                c_done = []
                for c in range(CH):
                    for tt in range(2):
                        p.t(lambda e, tile=s1tiles[tt], c=c, tt=tt, scratch=scratch: e.matmul(
                            tile[0:1, :], bsl("ones", 0, 1),
                            scratch[:, TH * c + TQ * tt: TH * c + TQ * tt + TQ],
                            start=(c == 0), stop=(c == CH - 1)),
                            wait=[("S", cast_marks[c])] +
                                 ([pw1a] if (pw1a and c == 0 and tt == 0) else []) +
                                 ([pw1b] if (pw1b and c == 0 and tt == 1) else []),
                            inc=(tt == 1))
                    c_done.append(p.c["T"])
                sq_marks = []
                for c in range(CH):
                    vm = p.v(lambda e, c=c, scratch=scratch: e.tensor_tensor(
                        scratch[:, TH * c:TH * (c + 1)], scratch[:, TH * c:TH * (c + 1)],
                        scratch[:, TH * c:TH * (c + 1)], ALU.mult),
                        wait=[("T", c_done[c])], inc=True)
                    sq_marks.append(vm)
                t2a, i2a, pw2a = pool.acq()
                t2b, i2b, pw2b = pool.acq()
                s2tiles = [t2a, t2b]
                for c in range(CH):
                    for tt in range(2):
                        p.t(lambda e, tile=s2tiles[tt], c=c, tt=tt, scratch=scratch: e.matmul(
                            tile[0:1, :], bsl("ones", 0, 1),
                            scratch[:, TH * c + TQ * tt: TH * c + TQ * tt + TQ],
                            start=(c == 0), stop=(c == CH - 1)),
                            wait=[("V", sq_marks[c])] +
                                 ([pw2a] if (pw2a and c == 0 and tt == 0) else []) +
                                 ([pw2b] if (pw2b and c == 0 and tt == 1) else []),
                            inc=(tt == 1))
                half_T_end[half] = p.c["T"]
                mm_done = ("T", p.c["T"])
                for tt in range(2):
                    j = 2 * half + tt
                    sl = slice(TQ * j, TQ * (j + 1))
                    p.v(lambda e, tile=s1tiles[tt], sl=sl: e.tensor_scalar(
                        statA[0:1, sl], tile[0:1, :], float(INVD), None, ALU.mult),
                        wait=[mm_done], inc=True)
                    p.v(lambda e, tile=s2tiles[tt], sl=sl: e.tensor_scalar(
                        bB[0:1, sl], tile[0:1, :], float(INVD), None, ALU.mult), inc=True)
                    vrel = p.c["V"]
                    p.v(lambda e, sl=sl: e.tensor_tensor(
                        bA[0:1, sl], statA[0:1, sl], statA[0:1, sl], ALU.mult),
                        wait=[("V", p.c["V"])], inc=True)
                    vv = p.v(lambda e, sl=sl: e.tensor_tensor(
                        bB[0:1, sl], bB[0:1, sl], bA[0:1, sl], ALU.subtract),
                        wait=[("V", p.c["V"])], inc=True)
                    p.s(lambda e, sl=sl: e.activation(bB[0:1, sl], bB[0:1, sl], AF.Ln,
                                                      bias=ccol(EPS)[0:1, :], scale=1.0),
                        wait=[("V", vv)], inc=True)
                    sv = p.s(lambda e, sl=sl: e.activation(bB[0:1, sl], bB[0:1, sl],
                                                           AF.Exp, scale=-0.5), inc=True)
                    inv_tile[j] = ("S", sv)
                    nm = p.v(lambda e, sl=sl: e.scalar_tensor_tensor(
                        statA[0:1, sl], statA[0:1, sl], -1.0, bB[0:1, sl],
                        ALU.mult, ALU.mult), wait=[inv_tile[j]], inc=True)
                    neg_tile[j] = ("V", nm)
                    pool.rel(i1a if tt == 0 else i1b, ("V", vrel))
                    pool.rel(i2a if tt == 0 else i2b, ("V", vrel))
            return {"inv": inv_tile[-1][1], "neg": neg_tile[-1][1],
                    "inv_tile": inv_tile, "neg_tile": neg_tile}

        def emit_outer_pair(pool, rowA, rowB, wA, wB, wA_tile=None, wB_tile=None):
            tile_marks = []
            for j in range(NTQ):
                tile, ti, pw = pool.acq()
                tm = p.t(lambda e, tile=tile, j=j, rowA=rowA: e.matmul(
                    tile[:, :], rows[0:1, :], rowA[0:1, TQ * j:TQ * (j + 1)],
                    start=True, stop=True),
                    wait=[wA_tile[j] if wA_tile else wA] + ([pw] if pw else []), inc=True)
                vmA = p.v(lambda e, tile=tile, j=j: e.tensor_copy(
                    bA[:, TQ * j:TQ * (j + 1)], tile[:, :]), wait=[("T", tm)], inc=True)
                pool.rel(ti, ("V", vmA))
                tile2, ti2, pw2 = pool.acq()
                tm2 = p.t(lambda e, tile2=tile2, j=j, rowB=rowB: e.matmul(
                    tile2[:, :], rows[0:1, :], rowB[0:1, TQ * j:TQ * (j + 1)],
                    start=True, stop=True),
                    wait=[wB_tile[j] if wB_tile else wB, ("V", vmA)] +
                         ([pw2] if pw2 else []), inc=True)
                vmB = p.v(lambda e, tile2=tile2, j=j: e.tensor_copy(
                    bB[:, TQ * j:TQ * (j + 1)], tile2[:, :]), wait=[("T", tm2)], inc=True)
                pool.rel(ti2, ("V", vmB))
                tile_marks.append(("V", vmB))
            return ("V", p.c["V"]), tile_marks

        def emit_ln_tiles(pool, X, scratch, pre_w=(), pre_w_chunk=None, scratch2=None):
            r = emit_ln(pool, X, scratch, pre_w, pre_w_chunk, scratch2)
            return emit_outer_pair(pool, bB, statA, None, None,
                                   wA_tile=r["inv_tile"], wB_tile=r["neg_tile"])[1]

        def emit_apply(X, OUT, mark_b, affine=None, extra_w=()):
            marks = []
            for c in range(CH):
                p.v(lambda e, c=c, X=X, OUT=OUT: e.tensor_tensor(
                    OUT[:, T * c:T * (c + 1)], X[:, T * c:T * (c + 1)], bA[:, :], ALU.mult),
                    wait=[mark_b] + list(extra_w), inc=True)
                vm = p.v(lambda e, c=c, OUT=OUT: e.tensor_tensor(
                    OUT[:, T * c:T * (c + 1)], OUT[:, T * c:T * (c + 1)], bB[:, :], ALU.add),
                    wait=[("V", p.c["V"])], inc=True)
                if affine is not None:
                    gname, bname = affine
                    sm = p.s(lambda e, c=c, gname=gname, bname=bname, OUT=OUT: e.activation(
                        OUT[:, T * c:T * (c + 1)], OUT[:, T * c:T * (c + 1)], AF.Identity,
                        bias=scol(bname, c), scale=scol(gname, c)),
                        wait=[("V", vm)], inc=True)
                    marks.append(("S", sm))
                else:
                    marks.append(("V", vm))
            return marks

        esS = ExitStack()
        sbS = esS.enter_context(nc.sbuf_tensor("S", [128, CH * T], F32, side="right"))

        ps8 = ExitStack()
        pool_tiles = [ps8.enter_context(nc.psum_tensor("pp%d" % i, [128, TQ], F32))
                      for i in range(8)]
        pool = Pool(p, pool_tiles)

        sv_in = sbS[:, :].rearrange("p (c t) -> p c t", t=T)
        xv_in = x_in.rearrange("p (c t) -> p c t", t=T)
        d_in = []
        for hh in range(2):
            d_in.append(p.y(lambda e, hh=hh: e.dma_start(
                sv_in[:, 4 * hh:4 * (hh + 1), :], xv_in[:, 4 * hh:4 * (hh + 1), :]),
                sem=["DI0", "DI1"][hh]))
        in_chunk_marks = [(["DI0", "DI1"][c // 4], d_in[c // 4]) for c in range(CH)]
        emit_dump("p0", sbS, [in_chunk_marks[-1]] + W_INIT)
        p.flush()

        def emit_ffn(X, XR, XK, KA, wr, wk, wv, rt, t1f, krl, wmarks, pre_w,
                     pre_w_chunk=None):
            btile = emit_ln_tiles(pool, X, XR, pre_w, pre_w_chunk, scratch2=XK)
            QTR = CH * TQ
            slot_free = [("T", 0), ("T", 0)]
            add_chunk_marks = {}

            def emit_mix(q):
                s0 = QTR * (q % 2)
                t0 = TQ * q
                marks = []
                for c in range(CH):
                    w = [btile[q], slot_free[q % 2]] + list(pre_w)
                    if q > 0:
                        w.append(btile[q - 1])
                    if q > 0:
                        p.v(lambda e, c=c, t0=t0: e.tensor_tensor(
                            bnd[:, c:c + 1], X[:, T * c + t0 - 1:T * c + t0],
                            bA[:, t0 - 1:t0], ALU.mult), wait=w, inc=True)
                        p.v(lambda e, c=c, t0=t0: e.tensor_tensor(
                            bnd[:, c:c + 1], bnd[:, c:c + 1], bB[:, t0 - 1:t0],
                            ALU.add), inc=True)
                        w = []
                    p.v(lambda e, c=c, t0=t0, s0=s0: e.tensor_tensor(
                        KA[:, s0 + TQ * c:s0 + TQ * (c + 1)],
                        X[:, T * c + t0:T * c + t0 + TQ],
                        bA[:, t0:t0 + TQ], ALU.mult), wait=w, inc=True)
                    p.v(lambda e, c=c, s0=s0: e.tensor_tensor(
                        KA[:, s0 + TQ * c:s0 + TQ * (c + 1)],
                        KA[:, s0 + TQ * c:s0 + TQ * (c + 1)],
                        bB[:, TQ * q:TQ * q + TQ], ALU.add), inc=True)
                    p.v(lambda e, c=c, s0=s0: e.tensor_tensor(
                        XK[:, s0 + TQ * c + 1:s0 + TQ * c + TQ],
                        KA[:, s0 + TQ * c + 1:s0 + TQ * c + TQ],
                        KA[:, s0 + TQ * c:s0 + TQ * c + TQ - 1], ALU.subtract), inc=True)
                    if q == 0:
                        p.v(lambda e, c=c, s0=s0: e.tensor_scalar(
                            XR[:, s0 + TQ * c:s0 + TQ * c + 1],
                            KA[:, s0 + TQ * c:s0 + TQ * c + 1],
                            scol("mur", c), None, ALU.mult), inc=True)
                        p.v(lambda e, c=c, s0=s0: e.tensor_scalar(
                            XK[:, s0 + TQ * c:s0 + TQ * c + 1],
                            KA[:, s0 + TQ * c:s0 + TQ * c + 1],
                            scol("muk", c), None, ALU.mult), inc=True)
                    else:
                        p.v(lambda e, c=c, s0=s0: e.tensor_tensor(
                            XR[:, s0 + TQ * c:s0 + TQ * c + 1],
                            KA[:, s0 + TQ * c:s0 + TQ * c + 1],
                            bnd[:, c:c + 1], ALU.subtract), inc=True)
                        p.v(lambda e, c=c, s0=s0: e.scalar_tensor_tensor(
                            XK[:, s0 + TQ * c:s0 + TQ * c + 1],
                            XR[:, s0 + TQ * c:s0 + TQ * c + 1],
                            scol("muk", c), bnd[:, c:c + 1], ALU.mult, ALU.add), inc=True)
                        p.v(lambda e, c=c, s0=s0: e.scalar_tensor_tensor(
                            XR[:, s0 + TQ * c:s0 + TQ * c + 1],
                            XR[:, s0 + TQ * c:s0 + TQ * c + 1],
                            scol("mur", c), bnd[:, c:c + 1], ALU.mult, ALU.add), inc=True)
                    p.v(lambda e, c=c, s0=s0: e.scalar_tensor_tensor(
                        XR[:, s0 + TQ * c + 1:s0 + TQ * c + TQ],
                        XK[:, s0 + TQ * c + 1:s0 + TQ * c + TQ],
                        scol("mur", c), KA[:, s0 + TQ * c:s0 + TQ * c + TQ - 1],
                        ALU.mult, ALU.add), inc=True)
                    vm = p.v(lambda e, c=c, s0=s0: e.scalar_tensor_tensor(
                        XK[:, s0 + TQ * c + 1:s0 + TQ * c + TQ],
                        XK[:, s0 + TQ * c + 1:s0 + TQ * c + TQ],
                        scol("muk", c), KA[:, s0 + TQ * c:s0 + TQ * c + TQ - 1],
                        ALU.mult, ALU.add), inc=True)
                    marks.append(vm)
                return ("V", marks[-1])

            mix_marks = {0: emit_mix(0)}
            for q in range(4):
                s0 = QTR * (q % 2)
                t0 = TQ * q
                mix_done = mix_marks[q]
                relu_last = 0
                for dk in range(CH):
                    tile, ti, pw = pool.acq()
                    for c in range(CH):
                        p.t(lambda e, tile=tile, dk=dk, c=c, s0=s0: e.matmul(
                            tile[:, :], wk[:, D * c + 128 * dk:D * c + 128 * (dk + 1)],
                            XK[:, s0 + TQ * c:s0 + TQ * (c + 1)],
                            start=(c == 0), stop=(c == CH - 1)),
                            wait=([mix_done, wmarks["wk"]] if c == 0 else []) +
                                 ([pw] if (pw and c == 0) else []),
                            inc=(c == CH - 1))
                    tmm = p.c["T"]
                    p.s(lambda e, tile=tile, krl=krl: e.activation(
                        krl[:, :], tile[:, :], AF.Relu), wait=[("T", tmm)], inc=True)
                    relu_last = p.s(lambda e, dk=dk, s0=s0, krl=krl: e.activation(
                        KA[:, s0 + TQ * dk:s0 + TQ * (dk + 1)], krl[:, :],
                        AF.Square), inc=True)
                    pool.rel(ti, ("S", relu_last))
                ka_done = ("S", relu_last)
                if q + 1 < 4:
                    mix_marks[q + 1] = emit_mix(q + 1)
                for dv in range(CH):
                    tileKV, tiKV, pwKV = pool.acq()
                    for c in range(CH):
                        p.t(lambda e, tile=tileKV, dv=dv, c=c, s0=s0: e.matmul(
                            tile[:, :], wv[:, D * c + 128 * dv:D * c + 128 * (dv + 1)],
                            KA[:, s0 + TQ * c:s0 + TQ * (c + 1)],
                            start=(c == 0), stop=(c == CH - 1)),
                            wait=([ka_done, wmarks["wv"]] if c == 0 else []) +
                                 ([pwKV] if (pwKV and c == 0) else []),
                            inc=(c == CH - 1))
                    tileR, tiR, pwR = pool.acq()
                    for c in range(CH):
                        p.t(lambda e, tile=tileR, dv=dv, c=c, s0=s0: e.matmul(
                            tile[:, :], wr[:, D * c + 128 * dv:D * c + 128 * (dv + 1)],
                            XR[:, s0 + TQ * c:s0 + TQ * (c + 1)],
                            start=(c == 0), stop=(c == CH - 1)),
                            wait=([wmarks["wr"]] if c == 0 else []) +
                                 ([pwR] if (pwR and c == 0) else []),
                            inc=(c == CH - 1))
                    tmm = p.c["T"]
                    sm = p.s(lambda e, tile=tileR, rt=rt: e.activation(
                        rt[:, :], tile[:, :], AF.Sigmoid),
                        wait=[("T", tmm), ("V", p.c["V"])], inc=True)
                    pool.rel(tiR, ("S", sm))
                    p.v(lambda e, tile=tileKV, rt=rt, t1f=t1f: e.scalar_tensor_tensor(
                        t1f[:, :], tile[:, :], 0.5, rt[:, :], ALU.mult, ALU.mult),
                        wait=[("S", sm)], inc=True)
                    vm = p.v(lambda e, dv=dv, t0=t0, X=X, t1f=t1f: e.tensor_tensor(
                        X[:, T * dv + t0:T * dv + t0 + TQ],
                        X[:, T * dv + t0:T * dv + t0 + TQ],
                        t1f[:, :], ALU.add), inc=True)
                    pool.rel(tiKV, ("V", vm))
                    add_chunk_marks[dv] = ("V", vm)
                slot_free[q % 2] = ("T", p.c["T"])
            last = add_chunk_marks[CH - 1]
            return last, [add_chunk_marks[c] for c in range(CH)]

        ffn1 = ExitStack()
        XRb = ffn1.enter_context(nc.sbuf_tensor("XR1", [128, CH * TH], BF16, side="right"))
        XKb = ffn1.enter_context(nc.sbuf_tensor("XK1", [128, CH * TH], BF16, side="right"))
        KAb = ffn1.enter_context(nc.sbuf_tensor("KA1", [128, CH * TH], BF16, side="right"))
        wrS = ffn1.enter_context(nc.sbuf_tensor("wrS1", [128, CH * D], BF16, side="right"))
        wkS = ffn1.enter_context(nc.sbuf_tensor("wkS1", [128, CH * D], BF16, side="right"))
        wvS = ffn1.enter_context(nc.sbuf_tensor("wvS1", [128, CH * D], BF16, side="right"))
        rt1 = ffn1.enter_context(nc.sbuf_tensor("rt1", [128, TQ], BF16, side="right"))
        krl1 = ffn1.enter_context(nc.sbuf_tensor("krl1", [128, TQ], BF16, side="right"))
        t1f1 = ffn1.enter_context(nc.sbuf_tensor("t1f1", [128, TQ], F32, side="right"))
        dwk = p.g(lambda e: e.dma_start(wkS[:, :], dram["wkT"]), sem="DWK")
        dwv = p.g(lambda e: e.dma_start(wvS[:, :], dram["wvT"]), sem="DWV")
        dwr = p.g(lambda e: e.dma_start(wrS[:, :], dram["wrT"]), sem="DWR")
        wm1 = {"wk": ("DWK", dwk), "wv": ("DWV", dwv), "wr": ("DWR", dwr)}
        mark_ffn1, ffn1_chunk_marks = emit_ffn(sbS, XRb, XKb, KAb, wrS, wkS, wvS,
                                               rt1, t1f1, krl1, wm1, W_INIT,
                                               in_chunk_marks)
        ffn1_T_end = ("T", p.c["T"])
        d_spill_s = p.y(lambda e: e.dma_start(spill_s, sbS[:, :]), wait=[mark_ffn1],
                        sem="DSS")
        emit_dump("ffn1", sbS, [mark_ffn1])
        p.barrier("tvs")
        p.flush()
        ffn1.close()
        if stop_after == "ffn1":
            finalize_early(sbS)
            return nc

        esH = ExitStack()
        sbH = esH.enter_context(nc.sbuf_tensor("H", [128, CH * T], F32, side="left"))
        statB1 = esH.enter_context(nc.sbuf_tensor("statB1", [1, T], F32, side="left"))
        statB2 = esH.enter_context(nc.sbuf_tensor("statB2", [1, T], F32, side="left"))
        p2a = ExitStack()
        scr = p2a.enter_context(nc.sbuf_tensor("scr", [128, 2 * CH * TH], BF16, side="right"))

        scrA = scr[:, 0:CH * TH]
        scrB = scr[:, CH * TH:2 * CH * TH]
        mark_b0 = emit_ln_tiles(pool, sbS, scrA, [], ffn1_chunk_marks, scratch2=scrB)[-1]
        hs_marks = emit_apply(sbS, sbH, mark_b0)
        mark_hs = hs_marks[-1]
        emit_dump("hs", sbH, [mark_hs])
        sv1 = p.v(lambda e: e.tensor_copy(statB1[0:1, :], bA[0:1, :]), wait=[mark_b0], inc=True)
        sv2 = p.v(lambda e: e.tensor_copy(statB2[0:1, :], bB[0:1, :]), inc=True)
        d_spill_h = p.y(lambda e: e.dma_start(spill_h, sbH[:, :]), wait=[mark_hs],
                        sem="DSH")
        r1 = emit_ln(pool, sbH, scrA, [("V", sv2)], hs_marks, scratch2=scrB)
        p.v(lambda e: e.tensor_tensor(statB1[0:1, :], statB1[0:1, :], bB[0:1, :], ALU.mult),
            wait=[("V", r1["neg"])], inc=True)
        p.v(lambda e: e.tensor_tensor(statB2[0:1, :], statB2[0:1, :], bB[0:1, :], ALU.mult),
            wait=[("V", p.c["V"])], inc=True)
        cm2 = p.v(lambda e: e.tensor_tensor(statB2[0:1, :], statB2[0:1, :], statA[0:1, :],
                                            ALU.add), wait=[("V", p.c["V"])], inc=True)
        mark_bx = emit_outer_pair(pool, statB1, statB2, ("V", cm2), ("V", cm2))[0]
        p.flush()
        p2a.close()
        esH.close()
        if stop_after == "p2a":
            finalize_early(sbS)
            return nc

        esx1 = ExitStack()
        x1b = esx1.enter_context(nc.sbuf_tensor("x1", [128, CH * T], BF16, side="left"))
        x1_marks = emit_apply(sbS, x1b, mark_bx, affine=("g1", "b1"),
                              extra_w=[("DSH", d_spill_h)])
        mark_x1 = x1_marks[-1]
        x1_chunk_marks = list(x1_marks)
        emit_dump("x1", x1b, [mark_x1], bf16=True)
        p.barrier("tvs")
        p.flush()
        esS.close()

        esQB = ExitStack()
        QB = esQB.enter_context(nc.sbuf_tensor("QB", [128, CH * T], BF16, side="right"))
        esKV = ExitStack()
        KB = esKV.enter_context(nc.sbuf_tensor("KB", [128, CH * T], BF16, side="right"))
        VA = esKV.enter_context(nc.sbuf_tensor("VA", [128, 16 * VAW], BF16, side="right"))
        pb = ExitStack()
        ATb = pb.enter_context(nc.sbuf_tensor("AT", [128, 8 * TQ], BF16, side="right"))
        CTb = pb.enter_context(nc.sbuf_tensor("CT", [64, 2 * TQ], BF16, side="right"))
        rlb = pb.enter_context(nc.sbuf_tensor("rl", [128, 2 * TQ], BF16, side="right"))
        brl = pb.enter_context(nc.sbuf_tensor("brl", [64, 2 * TQ], BF16, side="right"))
        lnt = pb.enter_context(nc.sbuf_tensor("lnt", [128, TQ], F32, side="right"))
        pa = ExitStack()
        Pq = pa.enter_context(nc.sbuf_tensor("Pq", [128, 3 * T], BF16, side="right"))
        Wu = pa.enter_context(nc.sbuf_tensor("Wu", [128, 2 * 3072], BF16, side="right"))
        V2s = pa.enter_context(nc.sbuf_tensor("V2s", [128, 3 * 1024], BF16, side="right"))

        ps8.close()
        psB = ExitStack()
        SD = [psB.enter_context(nc.psum_tensor("sd%d" % i, [128, 2 * TQ], F32)) for i in range(2)]
        pool4 = Pool(p, [psB.enter_context(nc.psum_tensor("pb%d" % i, [128, TQ], F32))
                         for i in range(4)])

        spill_gate = [("DSS", d_spill_s), ("DSH", d_spill_h)]
        d_v2 = p.g(lambda e: e.dma_start(V2s[:, :], dram["V2"]), wait=[ffn1_T_end], sem="DV2")
        VA_view = VA[:, :].rearrange("p (tc h e) -> p tc h e", h=NH, e=DH + 1)
        ones_set = p.v(lambda e: e.memset(VA_view[:, :, :, DH:DH + 1], 1.0),
                       wait=spill_gate, inc=True)

        pq_free = [("T", 0), ("T", 0), ("T", 0)]
        wu_free = [("T", 0), ("T", 0)]
        qk_ready = {}
        va_ready = {}
        qb_of = {0: "qb", 1: "kb"}

        qkv_queue = []

        def drain_qkv(n=1):
            k = 0
            while qkv_queue and k < n:
                qkv_queue.pop(0)[1]()
                k += 1
            return k

        def drain_tag(tag):
            while qkv_queue and qkv_queue[0][0] <= tag:
                qkv_queue.pop(0)[1]()

        def emit_qkv_pair(m):
            wslot = m % 2
            woff = 3072 * wslot
            wsem = "DWU" if wslot == 0 else "DWU2"
            dwu = p.g(lambda e, m=m, woff=woff: e.dma_start(
                Wu[:, woff:woff + 3072], dram["Upm"][:, 3072 * m:3072 * (m + 1)]),
                wait=[wu_free[wslot]] + ([ffn1_T_end] if m < 2 else []), sem=wsem)
            wu_mark = (wsem, dwu)
            state = {"s1_done": {}}

            def mk_s1(tt, pi):
                def run(tt=tt, pi=pi, m=m, woff=woff, wu_mark=wu_mark):
                    tile, ti, pw = pool4.acq()
                    for c in range(CH):
                        p.t(lambda e, tile=tile, m=m, pi=pi, tt=tt, c=c, woff=woff: e.matmul(
                            tile[:, :],
                            Wu[:, woff + 1024 * pi + 128 * c:woff + 1024 * pi + 128 * (c + 1)],
                            x1b[:, T * c + TQ * tt:T * c + TQ * tt + TQ],
                            start=(c == 0), stop=(c == CH - 1)),
                            wait=([wu_mark, pq_free[pi]] if c == 0 else []) +
                                 [x1_chunk_marks[c]] +
                                 ([pw] if (pw and c == 0) else []),
                            inc=(c == CH - 1))
                    tmm = p.c["T"]
                    vm = p.v(lambda e, tile=tile, pi=pi, tt=tt: e.tensor_copy(
                        Pq[:, T * pi + TQ * tt:T * pi + TQ * tt + TQ], tile[:, :]),
                        wait=[("T", tmm)], inc=True)
                    pool4.rel(ti, ("V", vm))
                    if tt == NTQ - 1:
                        state["s1_done"][pi] = ("V", vm)
                return run

            def mk_s2qk(tt, pi):
                def run(tt=tt, pi=pi, m=m):
                    dstb = QB if pi == 0 else KB
                    tile, ti, pw = pool4.acq()
                    p.t(lambda e, tile=tile, pi=pi, m=m, tt=tt: e.matmul(
                        tile[:, :], V2s[:, 1024 * pi + 128 * m:1024 * pi + 128 * (m + 1)],
                        Pq[:, T * pi + TQ * tt:T * pi + TQ * tt + TQ],
                        start=True, stop=True),
                        wait=[state["s1_done"][pi], ("DV2", d_v2)] + ([pw] if pw else []),
                        inc=True)
                    tmm = p.c["T"]
                    sm = p.v(lambda e, tile=tile, dstb=dstb, m=m, tt=tt, pi=pi: e.tensor_scalar(
                        dstb[:, T * m + TQ * tt:T * m + TQ * tt + TQ], tile[:, :],
                        scol(qb_of[pi], m), None, ALU.add),
                        wait=[("T", tmm)] +
                             (spill_gate if (m == 0 and pi == 0 and tt == 0) else []),
                        inc=True)
                    pool4.rel(ti, ("V", sm))
                    if tt == NTQ - 1:
                        pq_free[pi] = ("T", p.c["T"])
                        if pi == 1:
                            qk_ready[m] = ("V", p.c["V"])
                return run

            def mk_s2v(tg):
                def run(tg=tg, m=m, wslot=wslot):
                    tile, ti, pw = pool4.acq()
                    for k in range(4):
                        tc = tg * 4 + k
                        p.t(lambda e, tile=tile, m=m, tc=tc, k=k: e.matmul(
                            tile[:, 128 * k:128 * (k + 1)],
                            Pq[:, 2 * T + 128 * tc:2 * T + 128 * (tc + 1)],
                            V2s[:, 2048 + 128 * m:2048 + 128 * (m + 1)],
                            start=True, stop=True),
                            wait=([state["s1_done"][2], ("V", ones_set), ("DV2", d_v2)] if k == 0 else []) +
                                 ([pw] if (pw and k == 0) else []),
                            inc=(k == 3))
                    tmm = p.c["T"]
                    dst = VA_view[:, tg * 4:tg * 4 + 4, 2 * m:2 * m + 2, 0:DH]
                    src_ = tile[:, :].rearrange("p (tc rest) -> p tc rest", rest=128) \
                                     .rearrange("p tc (he e) -> p tc he e", e=DH)
                    vm = p.v(lambda e, dst=dst, src_=src_: e.tensor_copy(dst, src_),
                             wait=[("T", tmm)], inc=True)
                    pool4.rel(ti, ("V", vm))
                    if tg == 3:
                        pq_free[2] = ("T", p.c["T"])
                        va_ready[m] = ("V", p.c["V"])
                        wu_free[wslot] = ("T", p.c["T"])
                return run

            for pi in range(2):
                for tt in range(NTQ):
                    qkv_queue.append((m, mk_s1(tt, pi)))
                for tt in range(NTQ):
                    qkv_queue.append((m, mk_s2qk(tt, pi)))
            for tt in range(NTQ):
                qkv_queue.append((m, mk_s1(tt, 2)))
            for tg in range(4):
                qkv_queue.append((m, mk_s2v(tg)))

        sd_free = [("S", 0), ("S", 0)]
        at_free = {(s, pp): ("T", 0) for s in range(2) for pp in range(2)}
        ct_free = [("DCT0", 0), ("DCT1", 0)]
        rl_free = [("V", 0), ("V", 0)]
        brl_free = [("V", 0), ("V", 0)]
        tail_idx = [0]
        pending_tails = [[]]
        hooks = {}

        def make_tail(hc, hp, j, tC, iC, t_av):
            par = tail_idx[0] % 2
            tail_idx[0] += 1

            def tailA():
                p.s(lambda e: e.activation(
                    lnt[64:65, 0:TQ], tC[DH:DH + 1, :], AF.Ln),
                    wait=[("T", t_av), rl_free[par]], inc=True)
                vr_ = p.s(lambda e: e.activation(
                    rlb[64:65, TQ * par:TQ * par + TQ], lnt[64:65, 0:TQ],
                    AF.Exp, scale=-1.0), inc=True)
                return vr_

            def tailB(vr):
                tileB, tiB, pwB = pool4.acq()
                tb = p.t(lambda e, tileB=tileB: e.matmul(
                    tileB[0:64, :], bsl("ones", 0, 64, p0=64, pn=1),
                    rlb[64:65, TQ * par:TQ * par + TQ], start=True, stop=True),
                    wait=[("S", vr)] + ([pwB] if pwB else []), inc=True)
                rl_free[par] = ("T", tb)
                sb_ = p.v(lambda e, tileB=tileB: e.tensor_copy(
                    brl[0:64, TQ * par:TQ * par + TQ], tileB[0:64, :]),
                    wait=[("T", tb), brl_free[par]], inc=True)
                pool4.rel(tiB, ("V", sb_))
                if hp == 0:
                    vm = p.v(lambda e: e.tensor_tensor(
                        QB[0:64, T * hc + TQ * j:T * hc + TQ * j + TQ], tC[0:64, :],
                        brl[0:64, TQ * par:TQ * par + TQ], ALU.mult),
                        wait=[("V", sb_)], inc=True)
                    brl_free[par] = ("V", vm)
                    pool4.rel(iC, ("V", vm))
                else:
                    cpar = par
                    vm = p.v(lambda e: e.tensor_tensor(
                        CTb[0:64, TQ * cpar:TQ * cpar + TQ], tC[0:64, :],
                        brl[0:64, TQ * par:TQ * par + TQ], ALU.mult),
                        wait=[("V", sb_), ct_free[cpar]], inc=True)
                    brl_free[par] = ("V", vm)
                    pool4.rel(iC, ("V", vm))
                    dm = p.y(lambda e: e.dma_start(
                        QB[64:128, T * hc + TQ * j:T * hc + TQ * j + TQ],
                        CTb[0:64, TQ * cpar:TQ * cpar + TQ]), wait=[("V", vm)],
                        sem="DCT%d" % cpar)
                    ct_free[cpar] = ("DCT%d" % cpar, dm)

            def tail():
                vr = tailA()
                return lambda vr=vr: tailB(vr)
            return tail

        def emit_attention_hc(hc):
            drain_tag(hc)
            for j in range(NTQ):
                kcm = 4 * (j + 1)
                npair = kcm // 2
                exp_marks = [{}, {}]
                msk_marks = [{}, {}]
                ctx_hold = [{}, {}]

                def emit_qk_quad(ip, hc=hc, j=j, exp_marks=exp_marks, msk_marks=msk_marks):
                    i0 = 2 * ip
                    if not drain_qkv(1):
                        for s in range(2):
                            p.t(lambda e, s=s: e.matmul(
                                SD[s][:, 0:TQ], bsl("masks", 0, 128), bsl("masks", 0, TQ),
                                start=True, stop=True), wait=[sd_free[s]])
                    for u in range(2):
                        i = i0 + u
                        for s in range(2):
                            r0 = 64 * s
                            p.t(lambda e, s=s, u=u, i=i, r0=r0, hc=hc, j=j: e.matmul(
                                SD[s][:, TQ * u:TQ * (u + 1)],
                                KB[r0:r0 + 64, T * hc + 128 * i:T * hc + 128 * (i + 1)],
                                QB[r0:r0 + 64, T * hc + TQ * j:T * hc + TQ * j + TQ],
                                start=True, stop=True),
                                wait=([sd_free[s]] if u == 0 else []) +
                                     ([qk_ready[hc]] if (u == 0 and s == 0 and ip == 0 and j == 0) else []),
                                inc=(u == 1 and s == 1))
                    tmm = p.c["T"]
                    pp = ip % 2
                    for s in range(2):
                        sm = p.s(lambda e, s=s, i0=i0, pp=pp: e.activation(
                            ATb[:, TQ * (4 * s + 2 * pp):TQ * (4 * s + 2 * pp) + 2 * TQ],
                            SD[s][:, :], AF.Exp, scale=0.125),
                            wait=[("T", tmm), at_free[(s, pp)]], inc=True)
                        sd_free[s] = ("S", sm)
                        for u in range(2):
                            i = i0 + u
                            exp_marks[s][i] = ("S", sm)
                            if i >= 4 * j:
                                d = i - 4 * j
                                slot = 4 * s + 2 * pp + u
                                mm_ = p.v(lambda e, slot=slot, d=d: e.tensor_tensor(
                                    ATb[:, TQ * slot:TQ * slot + TQ],
                                    ATb[:, TQ * slot:TQ * slot + TQ],
                                    bsl("masks", TQ * d, TQ), ALU.mult),
                                    wait=[("S", sm)], inc=True)
                                msk_marks[s][i] = ("V", mm_)

                def emit_av_quad(ip, last, hc=hc, exp_marks=exp_marks, msk_marks=msk_marks,
                                 ctx_hold=ctx_hold):
                    i0 = 2 * ip
                    for s in range(2):
                        h = 2 * hc + s
                        for u in range(2):
                            i = i0 + u
                            w = [msk_marks[s].get(i, exp_marks[s][i])]
                            if i == 0:
                                tC, iC, pwC = pool4.acq()
                                ctx_hold[s]["t"] = tC
                                ctx_hold[s]["i"] = iC
                                if pwC:
                                    w.append(pwC)
                                w.append(va_ready[hc])
                            tC = ctx_hold[s]["t"]
                            pp = ip % 2
                            slot = 4 * s + 2 * pp + u
                            is_last = last and u == 1
                            p.t(lambda e, i=i, h=h, tC=tC, slot=slot, is_last=is_last: e.matmul(
                                tC[0:DH + 1, :], VA[:, VAW * i + 65 * h:VAW * i + 65 * h + 65],
                                ATb[:, TQ * slot:TQ * slot + TQ],
                                start=(i == 0), stop=is_last),
                                wait=w, inc=is_last)
                            if u == 1:
                                at_free[(s, pp)] = ("T", p.c["T"])

                emit_qk_quad(0)
                if npair == 1:
                    for t_ in pending_tails[0]:
                        t_()
                    pending_tails[0] = []
                for ip in range(1, npair):
                    emit_qk_quad(ip)
                    if ip == 1:
                        for t_ in pending_tails[0]:
                            t_()
                        pending_tails[0] = []
                    emit_av_quad(ip - 1, False)
                emit_av_quad(npair - 1, True)
                t_av = p.c["T"]
                pending_tails[0] = [t_() for t_ in [
                    make_tail(hc, 0, j, ctx_hold[0]["t"], ctx_hold[0]["i"], t_av),
                    make_tail(hc, 1, j, ctx_hold[1]["t"], ctx_hold[1]["i"], t_av)]]
                if hc == 4 and j == 0:
                    hooks["hc4"] = ("T", p.c["T"])

        for m in range(CH):
            emit_qkv_pair(m)
            if m >= 1:
                emit_attention_hc(m - 1)
                if m - 1 == 3:
                    hooks["hc3_done"] = ("T", p.c["T"])
                if m - 1 == 6:
                    hooks["hc6_done"] = ("T", p.c["T"])
        drain_tag(CH)
        hooks["qkv_done"] = ("T", p.c["T"])
        emit_dump("k", KB, [("S", p.c["S"])], bf16=True)
        emit_dump("va", VA, [("V", p.c["V"])], bf16=True)
        p.flush()
        pa.close()
        esx1.close()

        esH2 = ExitStack()
        sbH2 = esH2.enter_context(nc.sbuf_tensor("H2", [128, CH * T], F32, side="left"))
        h2v = sbH2[:, :].rearrange("p (c t) -> p c t", t=T)
        spv = spill_h.rearrange("p (c t) -> p c t", t=T)
        gate_h2 = [("T", p.c["T"]), ("S", p.c["S"])]
        d_rel_h = []
        sems_rh = ["DSH", "DS2"]
        for hh in range(2):
            d_rel_h.append(p.y(lambda e, hh=hh: e.dma_start(
                h2v[:, 4 * hh:4 * (hh + 1), :], spv[:, 4 * hh:4 * (hh + 1), :]),
                wait=gate_h2, sem=sems_rh[hh]))
        emit_attention_hc(CH - 1)
        for t_ in pending_tails[0]:
            t_()
        pending_tails[0] = []
        ctx_dma_done = [("DCT0", p.c["DCT0"]), ("DCT1", p.c["DCT1"])]
        emit_dump("ctx", QB, [("V", p.c["V"])] + ctx_dma_done, bf16=True)
        p.barrier("tvs")
        p.t(lambda e: e.nop(), wait=ctx_dma_done)
        p.flush()
        psB.close()
        pb.close()
        esKV.close()

        ps8b = ExitStack()
        pool_tiles2 = [ps8b.enter_context(nc.psum_tensor("pq%d" % i, [128, TQ], F32))
                       for i in range(8)]
        pool = Pool(p, pool_tiles2)

        po = ExitStack()
        oU = po.enter_context(nc.sbuf_tensor("oU", [128, CH * 768], BF16, side="right"))
        oV = po.enter_context(nc.sbuf_tensor("oV", [128, 6 * D], BF16, side="right"))
        O1 = po.enter_context(nc.sbuf_tensor("O1", [128, 6 * T], BF16, side="right"))
        b1sb = po.enter_context(nc.sbuf_tensor("b1sb", [128, 6], F32, side="right"))
        d_oU = p.g(lambda e: e.dma_start(oU[:, :], dram["outU"]),
                   wait=[hooks["hc3_done"]], sem="DOU")
        d_oV = p.g(lambda e: e.dma_start(oV[:, :], dram["outV"]),
                   wait=[hooks["hc6_done"]], sem="DOV")
        tile, ti, pw = pool.acq()
        for m in range(6):
            for c in range(CH):
                p.t(lambda e, tile=tile, m=m, c=c: e.matmul(
                    tile[:, m:m + 1], oU[:, 768 * c + 128 * m:768 * c + 128 * (m + 1)],
                    bsl("vb_bf", c, 1), start=(c == 0), stop=(c == CH - 1)),
                    wait=([("DOU", d_oU)] if (m == 0 and c == 0) else []) +
                         ([pw] if (pw and m == 0 and c == 0) else []),
                    inc=(m == 5 and c == CH - 1))
        tb1 = p.c["T"]
        vb1 = p.v(lambda e, tile=tile: e.tensor_copy(b1sb[:, :], tile[:, 0:6]),
                  wait=[("T", tb1)], inc=True)
        pool.rel(ti, ("V", vb1))
        for m in range(6):
            for tt in range(NTQ):
                tile, ti, pw = pool.acq()
                for c in range(CH):
                    p.t(lambda e, tile=tile, m=m, tt=tt, c=c: e.matmul(
                        tile[:, :], oU[:, 768 * c + 128 * m:768 * c + 128 * (m + 1)],
                        QB[:, T * c + TQ * tt:T * c + TQ * tt + TQ],
                        start=(c == 0), stop=(c == CH - 1)),
                        wait=([pw] if (pw and c == 0) else []), inc=(c == CH - 1))
                tmm = p.c["T"]
                sm = p.s(lambda e, tile=tile, m=m, tt=tt: e.activation(
                    O1[:, T * m + TQ * tt:T * m + TQ * tt + TQ], tile[:, :], AF.Identity,
                    bias=b1sb[:, m:m + 1], scale=1.0),
                    wait=[("T", tmm), ("V", vb1)], inc=True)
                pool.rel(ti, ("S", sm))
        s1_done = ("S", p.c["S"])
        h2_chunk_marks = {}
        for do in range(CH):
            for tt in range(NTQ):
                tile, ti, pw = pool.acq()
                for k6 in range(6):
                    p.t(lambda e, tile=tile, do=do, tt=tt, k6=k6: e.matmul(
                        tile[:, :], oV[:, D * k6 + 128 * do:D * k6 + 128 * (do + 1)],
                        O1[:, T * k6 + TQ * tt:T * k6 + TQ * tt + TQ],
                        start=(k6 == 0), stop=(k6 == 5)),
                        wait=([s1_done, ("DOV", d_oV)] if k6 == 0 else []) +
                             ([pw] if (pw and k6 == 0) else []),
                        inc=(k6 == 5))
                tmm = p.c["T"]
                vm = p.v(lambda e, tile=tile, do=do, tt=tt: e.scalar_tensor_tensor(
                    sbH2[:, T * do + TQ * tt:T * do + TQ * tt + TQ], tile[:, :],
                    scol("outb", do), sbH2[:, T * do + TQ * tt:T * do + TQ * tt + TQ],
                    ALU.add, ALU.add),
                    wait=[("T", tmm), (sems_rh[do // 4], d_rel_h[do // 4])], inc=True)
                pool.rel(ti, ("V", vm))
                h2_chunk_marks[do] = ("V", vm)
        mark_h = ("V", p.c["V"])
        emit_dump("h", sbH2, [mark_h])
        p.barrier("tvs")
        p.flush()
        po.close()
        esQB.close()
        if stop_after == "outproj":
            finalize_early(sbH2)
            return nc

        pm0 = ExitStack()
        scr2 = pm0.enter_context(nc.sbuf_tensor("scr2", [128, 2 * CH * TH], BF16, side="right"))
        b2tile = emit_ln_tiles(pool, sbH2, scr2[:, 0:CH * TH], [],
                               [h2_chunk_marks[c] for c in range(CH)],
                               scratch2=scr2[:, CH * TH:2 * CH * TH])
        mark_b2 = b2tile[-1]
        p.flush()
        pm0.close()

        pm = ExitStack()
        f1U = pm.enter_context(nc.sbuf_tensor("f1U", [128, CH * 512], BF16, side="right"))
        f1V = pm.enter_context(nc.sbuf_tensor("f1V", [128, 4 * 4096], BF16, side="right"))
        f2U = pm.enter_context(nc.sbuf_tensor("f2U", [128, 32 * 512], BF16, side="right"))
        f2V = pm.enter_context(nc.sbuf_tensor("f2V", [128, 4 * D], BF16, side="right"))
        x2t = pm.enter_context(nc.sbuf_tensor("x2t", [128, 2 * CH * TQ], BF16, side="right"))
        P1t = pm.enter_context(nc.sbuf_tensor("P1t", [128, 4 * TQ], BF16, side="right"))
        P2t = pm.enter_context(nc.sbuf_tensor("P2t", [128, 4 * TQ], BF16, side="right"))
        HMt = pm.enter_context(nc.sbuf_tensor("HMt", [128, 2 * TQ], BF16, side="right"))
        mlp_gate = ("T", p.c["T"])
        b2m = pm.enter_context(nc.sbuf_tensor("b2m", [128, 4], F32, side="right"))
        d_f1U = p.g(lambda e: e.dma_start(f1U[:, :], dram["fc1U"]), wait=[mlp_gate], sem="DF1U")
        d_f1V = p.g(lambda e: e.dma_start(f1V[:, :], dram["fc1V"]), wait=[mlp_gate], sem="DF1V")
        d_f2U = p.g(lambda e: e.dma_start(f2U[:, :], dram["fc2U"]), wait=[mlp_gate], sem="DF2U")
        d_f2V = p.g(lambda e: e.dma_start(f2V[:, :], dram["fc2V"]), wait=[mlp_gate], sem="DF2V")

        tileb, tib, pwb = pool.acq()
        for pc in range(4):
            for c in range(CH):
                p.t(lambda e, tileb=tileb, pc=pc, c=c: e.matmul(
                    tileb[:, pc:pc + 1], f1U[:, 512 * c + 128 * pc:512 * c + 128 * (pc + 1)],
                    bsl("b2_bf", c, 1), start=(c == 0), stop=(c == CH - 1)),
                    wait=([("DF1U", d_f1U)] if (pc == 0 and c == 0) else []) +
                         ([pwb] if (pwb and pc == 0 and c == 0) else []),
                    inc=(pc == 3 and c == CH - 1))
        t_b2m = p.c["T"]
        v_b2m = p.v(lambda e, tileb=tileb: e.tensor_copy(b2m[:, :], tileb[:, 0:4]),
                    wait=[("T", t_b2m)], inc=True)
        pool.rel(tib, ("V", v_b2m))
        g2_marks = []
        for c in range(CH):
            smsc = p.s(lambda e, c=c: e.activation(
                f1U[:, 512 * c:512 * (c + 1)], f1U[:, 512 * c:512 * (c + 1)],
                AF.Identity, bias=0.0, scale=scol("g2", c)),
                wait=[("T", t_b2m)], inc=True)
            g2_marks.append(smsc)
        f1u_ready = ("S", g2_marks[-1])

        x2_free = [("T", 0), ("T", 0)]
        p1_free = ("T", 0)
        p2_free = ("V", 0)

        def emit_x2(jt):
            slot = jt % 2
            x0 = CH * TQ * slot
            marks = []
            for c in range(CH):
                p.v(lambda e, c=c, jt=jt, x0=x0: e.tensor_tensor(
                    x2t[:, x0 + TQ * c:x0 + TQ * (c + 1)],
                    sbH2[:, T * c + TQ * jt:T * c + TQ * jt + TQ],
                    bA[:, TQ * jt:TQ * jt + TQ], ALU.mult),
                    wait=[b2tile[jt], x2_free[slot], mark_h], inc=True)
                vmx = p.v(lambda e, c=c, jt=jt, x0=x0: e.tensor_tensor(
                    x2t[:, x0 + TQ * c:x0 + TQ * (c + 1)], x2t[:, x0 + TQ * c:x0 + TQ * (c + 1)],
                    bB[:, TQ * jt:TQ * jt + TQ], ALU.add), inc=True)
                marks.append(("V", vmx))
            return marks

        mlp_chunk_marks = {}
        x2_marks_by_tile = {0: emit_x2(0)}
        for jt in range(NTQ):
            slot = jt % 2
            x0 = CH * TQ * slot
            x2_marks = x2_marks_by_tile[jt]
            p1_marks = []
            for pc in range(4):
                tile, ti, pw = pool.acq()
                for c in range(CH):
                    p.t(lambda e, tile=tile, pc=pc, c=c, x0=x0: e.matmul(
                        tile[:, :], f1U[:, 512 * c + 128 * pc:512 * c + 128 * (pc + 1)],
                        x2t[:, x0 + TQ * c:x0 + TQ * (c + 1)],
                        start=(c == 0), stop=(c == CH - 1)),
                        wait=([x2_marks[-1], f1u_ready, p1_free] if c == 0 else []) +
                             ([pw] if (pw and c == 0) else []),
                        inc=(c == CH - 1))
                tmm = p.c["T"]
                vm = p.s(lambda e, tile=tile, pc=pc: e.activation(
                    P1t[:, TQ * pc:TQ * (pc + 1)], tile[:, :], AF.Identity,
                    bias=b2m[:, pc:pc + 1], scale=1.0),
                    wait=[("T", tmm), ("V", v_b2m)], inc=True)
                pool.rel(ti, ("S", vm))
                p1_marks.append(vm)
            p1_done = ("S", p1_marks[-1])
            s1_T = p.c["T"]
            if jt + 1 < NTQ:
                x2_marks_by_tile[jt + 1] = emit_x2(jt + 1)
            acc = []
            for p2 in range(4):
                tileA, tiA, pwA = pool.acq()
                acc.append((tileA, tiA, pwA))
            hm_free = [("T", 0), ("T", 0)]
            for ic in range(32):
                tile, ti, pw = pool.acq()
                for pc in range(4):
                    p.t(lambda e, tile=tile, ic=ic, pc=pc: e.matmul(
                        tile[:, :], f1V[:, 4096 * pc + 128 * ic:4096 * pc + 128 * (ic + 1)],
                        P1t[:, TQ * pc:TQ * (pc + 1)],
                        start=(pc == 0), stop=(pc == 3)),
                        wait=([p1_done, ("DF1V", d_f1V)] if pc == 0 else []) +
                             ([pw] if (pw and pc == 0) else []),
                        inc=(pc == 3))
                tmm = p.c["T"]
                hslot = ic % 2
                smg = p.s(lambda e, tile=tile, ic=ic, hslot=hslot: e.activation(
                    HMt[:, TQ * hslot:TQ * (hslot + 1)], tile[:, :], AF.Gelu_apprx_tanh,
                    bias=scol("fc1b", ic), scale=1.0),
                    wait=[("T", tmm), hm_free[hslot]], inc=True)
                pool.rel(ti, ("S", smg))
                for p2 in range(4):
                    tileA, tiA, pwA = acc[p2]
                    p.t(lambda e, tileA=tileA, ic=ic, p2=p2, hslot=hslot: e.matmul(
                        tileA[:, :], f2U[:, 512 * ic + 128 * p2:512 * ic + 128 * (p2 + 1)],
                        HMt[:, TQ * hslot:TQ * (hslot + 1)],
                        start=(ic == 0), stop=(ic == 31)),
                        wait=[("S", smg), ("DF2U", d_f2U)] +
                             ([pwA] if (pwA and ic == 0) else []) +
                             ([p2_free] if ic == 0 else []),
                        inc=(p2 == 3))
                hm_free[hslot] = ("T", p.c["T"])
            t_acc = p.c["T"]
            for p2 in range(4):
                tileA, tiA, _ = acc[p2]
                vm = p.s(lambda e, tileA=tileA, p2=p2: e.activation(
                    P2t[:, TQ * p2:TQ * (p2 + 1)], tileA[:, :], AF.Identity,
                    bias=0.0, scale=1.0), wait=[("T", t_acc)], inc=True)
                pool.rel(tiA, ("S", vm))
            p2_done = ("S", p.c["S"])
            for do in range(CH):
                tile, ti, pw = pool.acq()
                for p2 in range(4):
                    p.t(lambda e, tile=tile, do=do, p2=p2: e.matmul(
                        tile[:, :], f2V[:, D * p2 + 128 * do:D * p2 + 128 * (do + 1)],
                        P2t[:, TQ * p2:TQ * (p2 + 1)],
                        start=(p2 == 0), stop=(p2 == 3)),
                        wait=([p2_done, ("DF2V", d_f2V)] if p2 == 0 else []) +
                             ([pw] if (pw and p2 == 0) else []),
                        inc=(p2 == 3))
                tmm = p.c["T"]
                vm = p.v(lambda e, tile=tile, do=do, jt=jt: e.scalar_tensor_tensor(
                    sbH2[:, T * do + TQ * jt:T * do + TQ * jt + TQ], tile[:, :],
                    scol("fc2b", do), sbH2[:, T * do + TQ * jt:T * do + TQ * jt + TQ],
                    ALU.add, ALU.add), wait=[("T", tmm)], inc=True)
                pool.rel(ti, ("V", vm))
                mlp_chunk_marks[do] = ("V", vm)
            x2_free[slot] = ("T", t_acc)
            p1_free = ("T", t_acc)
            p2_free = ("V", p.c["V"])
        mark_mlp = ("V", p.c["V"])
        emit_dump("mlp", sbH2, [mark_mlp])
        p.barrier("tvs")
        p.flush()
        pm.close()
        if stop_after == "mlp":
            finalize_early(sbH2)
            return nc

        esS2 = ExitStack()
        sbS2 = esS2.enter_context(nc.sbuf_tensor("S2", [128, CH * T], F32, side="right"))
        s2v = sbS2[:, :].rearrange("p (c t) -> p c t", t=T)
        sp2v = spill_s.rearrange("p (c t) -> p c t", t=T)
        gate_s2 = [("T", p.c["T"]), ("V", p.c["V"]), ("S", p.c["S"])]
        d_rel = []
        sems_rel = ["DSS", "DS2"]
        for hh in range(2):
            d_rel.append(p.y(lambda e, hh=hh: e.dma_start(
                s2v[:, 4 * hh:4 * (hh + 1), :], sp2v[:, 4 * hh:4 * (hh + 1), :]),
                wait=gate_s2, sem=sems_rel[hh]))
        add_marks_ch = []
        for c in range(CH):
            vm = p.v(lambda e, c=c: e.tensor_tensor(
                sbS2[:, T * c:T * (c + 1)], sbS2[:, T * c:T * (c + 1)],
                sbH2[:, T * c:T * (c + 1)], ALU.add),
                wait=[(sems_rel[c // 4], d_rel[c // 4]), mlp_chunk_marks[c]], inc=True)
            add_marks_ch.append(("V", vm))
        p.flush()
        esH2.close()

        ffn2 = ExitStack()
        XRb2 = ffn2.enter_context(nc.sbuf_tensor("XR2", [128, CH * TH], BF16, side="right"))
        XKb2 = ffn2.enter_context(nc.sbuf_tensor("XK2", [128, CH * TH], BF16, side="right"))
        KAb2 = ffn2.enter_context(nc.sbuf_tensor("KA2", [128, CH * TH], BF16, side="right"))
        wrS2 = ffn2.enter_context(nc.sbuf_tensor("wrS2", [128, CH * D], BF16, side="right"))
        wkS2 = ffn2.enter_context(nc.sbuf_tensor("wkS2", [128, CH * D], BF16, side="right"))
        wvS2 = ffn2.enter_context(nc.sbuf_tensor("wvS2", [128, CH * D], BF16, side="right"))
        rt2 = ffn2.enter_context(nc.sbuf_tensor("rt2", [128, TQ], BF16, side="right"))
        krl2 = ffn2.enter_context(nc.sbuf_tensor("krl2", [128, TQ], BF16, side="right"))
        t1f2 = ffn2.enter_context(nc.sbuf_tensor("t1f2", [128, TQ], F32, side="right"))
        dwk2 = p.g(lambda e: e.dma_start(wkS2[:, :], dram["wkT"]), sem="DWK")
        dwv2 = p.g(lambda e: e.dma_start(wvS2[:, :], dram["wvT"]), sem="DWV")
        dwr2 = p.g(lambda e: e.dma_start(wrS2[:, :], dram["wrT"]), sem="DWR")
        wm2 = {"wk": ("DWK", dwk2), "wv": ("DWV", dwv2), "wr": ("DWR", dwr2)}
        mark_ffn2, ffn2_chunk_marks = emit_ffn(sbS2, XRb2, XKb2, KAb2, wrS2, wkS2,
                                               wvS2, rt2, t1f2, krl2, wm2, [],
                                               add_marks_ch)
        emit_dump("out_fm", sbS2, [mark_ffn2])
        p.barrier("tvs")
        p.flush()
        ffn2.close()
        if stop_after == "ffn2":
            finalize_early(sbS2)
            return nc

        xv_out = x_out.rearrange("p (c t) -> p c t", t=T)
        sv_out = sbS2[:, :].rearrange("p (c t) -> p c t", t=T)
        for hh in range(2):
            w = [ffn2_chunk_marks[c] for c in range(4 * hh, 4 * hh + 4)]
            p.y(lambda e, hh=hh: e.dma_start(
                xv_out[:, 4 * hh:4 * (hh + 1), :], sv_out[:, 4 * hh:4 * (hh + 1), :]),
                wait=w, sem=["DO0", "DO1"][hh])
        p._mk("y", lambda eng: eng.nop(),
              [(sn, p.c[sn]) for sn in DMA_SEMS if p.c[sn] > 0], [])
        p.flush()
        ps8b.close()
        esS2.close()

    return nc



def kernel(**inputs):
    host = prep_host(inputs)
    nc = build_program(host)
    x = _f32(inputs["L0_attn_hidden_states"])
    in_maps = []
    for c in range(8):
        m = {k: host[k] for k in WTENSORS}
        m["x_in"] = feat_chunks(np.ascontiguousarray(x[c].T))
        in_maps.append(m)
    res = run_bass_kernel_spmd(nc, in_maps, core_ids=list(range(8)))
    outs = []
    for c in range(8):
        fm = np.asarray(res.results[c]["x_out"])
        td = fm.reshape(128, CH, T).transpose(1, 0, 2).reshape(D, T).T
        outs.append(np.ascontiguousarray(td))
    return np.stack(outs, axis=0)


# version 32 (final)
# speedup vs baseline: 1.0256x; 1.0185x over previous
import numpy as np
from contextlib import ExitStack

import concourse.bass as bass
import concourse.mybir as mybir
from concourse.bass_utils import run_bass_kernel_spmd

F32 = mybir.dt.float32
BF16 = mybir.dt.bfloat16
AF = mybir.ActivationFunctionType
ALU = mybir.AluOpType

D = 1024
T = 2048
CH = D // 128
NH = 16
DH = 64
RQ = 48
TQ = 512
NTQ = T // TQ
TH = 1024
EPS = 1e-5
INVD = 1.0 / D
VAW = NH * (DH + 1)

CONST_VALS = [0.0, 1.0, -1.0, 0.5, -0.5, 0.125, INVD, EPS]

WTENSORS = ["wrT", "wkT", "wvT", "Upm", "V2",
            "outU", "outV", "fc1U", "fc1V", "fc2U", "fc2V",
            "smalls", "rows", "bconsts", "ident"]


def _f32(a):
    a = np.asarray(a)
    assert a.dtype == np.float32, a.dtype
    return a

def feat_chunks(A):
    A = np.ascontiguousarray(_f32(A))
    C, X = A.shape
    return np.ascontiguousarray(A.reshape(C // 128, 128, X).transpose(1, 0, 2).reshape(128, -1))

def vec_cols(v):
    v = _f32(v).reshape(-1)
    return np.ascontiguousarray(v.reshape(-1, 128).T)

def pack_U(U):
    U = _f32(U)
    Up = np.zeros((D, 1024), np.float32)
    for h in range(NH):
        c0 = 128 * (h // 2) + 64 * (h % 2)
        Up[:, c0:c0 + RQ] = U[:, h, :]
    return feat_chunks(Up)

def pack_V2(Vw):
    Vw = _f32(Vw)
    P = np.zeros((128, 1024), np.float32)
    for h in range(NH):
        P[64 * (h % 2):64 * (h % 2) + RQ, DH * h:DH * h + DH] = Vw[h]
    return P

def build_masks():
    q = np.arange(TQ)[None, :]
    k = np.arange(128)[:, None]
    return np.concatenate([(q >= 128 * d + k).astype(np.float32) for d in range(4)], axis=1)


def prep_host(inputs):
    g = {}
    g["wrT"] = feat_chunks(np.ascontiguousarray(_f32(inputs["L0_ffn_W_r"]).T))
    g["wkT"] = feat_chunks(np.ascontiguousarray(_f32(inputs["L0_ffn_W_k"]).T))
    g["wvT"] = feat_chunks(np.ascontiguousarray(_f32(inputs["L0_ffn_W_v"]).T))
    qU = pack_U(inputs["L0_attn_q_U"])
    kU = pack_U(inputs["L0_attn_k_U"])
    vU = pack_U(inputs["L0_attn_v_U"])
    Upm = np.zeros((128, 8 * 3072), np.float32)
    for m in range(8):
        for pi, Uf in enumerate((qU, kU, vU)):
            for c in range(8):
                Upm[:, 3072 * m + 1024 * pi + 128 * c:3072 * m + 1024 * pi + 128 * (c + 1)] = \
                    Uf[:, 1024 * c + 128 * m:1024 * c + 128 * (m + 1)]
    g["Upm"] = np.ascontiguousarray(Upm)
    g["V2"] = np.ascontiguousarray(np.concatenate(
        [pack_V2(inputs["L0_attn_q_V"]), pack_V2(inputs["L0_attn_k_V"]),
         pack_V2(inputs["L0_attn_v_V"])], axis=1))
    g["outU"] = feat_chunks(inputs["L0_attn_out_U"])
    g["outV"] = feat_chunks(inputs["L0_attn_out_V"])
    g["fc1U"] = feat_chunks(inputs["L0_attn_fc1_U"])
    g["fc1V"] = feat_chunks(inputs["L0_attn_fc1_V"])
    g["fc2U"] = feat_chunks(inputs["L0_attn_fc2_U"])
    g["fc2V"] = feat_chunks(inputs["L0_attn_fc2_V"])

    cols, reg = [], {}
    def add(name, arr):
        arr = _f32(arr)
        if arr.ndim == 1:
            arr = arr[:, None]
        reg[name] = (sum(c.shape[1] for c in cols), arr.shape[1])
        cols.append(arr)
    add("consts", np.broadcast_to(np.array(CONST_VALS, np.float32),
                                  (128, len(CONST_VALS))).copy())
    add("qb", vec_cols(_f32(inputs["L0_attn_q_b"]).reshape(-1)))
    add("kb", vec_cols(_f32(inputs["L0_attn_k_b"]).reshape(-1)))
    add("outb", vec_cols(inputs["L0_attn_out_b"]))
    add("fc1b", vec_cols(inputs["L0_attn_fc1_b"]))
    add("fc2b", vec_cols(inputs["L0_attn_fc2_b"]))
    add("mur", vec_cols(inputs["L0_ffn_mu_r"]))
    add("muk", vec_cols(inputs["L0_ffn_mu_k"]))
    add("g1", vec_cols(inputs["L0_attn_ln1_g"]))
    add("b1", vec_cols(inputs["L0_attn_ln1_b"]))
    add("g2", vec_cols(inputs["L0_attn_ln2_g"]))
    add("b2", vec_cols(inputs["L0_attn_ln2_b"]))
    g["smalls"] = np.ascontiguousarray(np.concatenate(cols, axis=1))
    g["smalls_reg"] = reg

    g["rows"] = np.ones((1, 128), np.float32)

    bcols, breg = [], {}
    def addb(name, arr):
        arr = _f32(arr)
        breg[name] = (sum(c.shape[1] for c in bcols), arr.shape[1])
        bcols.append(arr)
    addb("masks", build_masks())
    addb("ones", np.ones((128, 64), np.float32))
    addb("vb_bf", vec_cols(_f32(inputs["L0_attn_v_b"]).reshape(-1)))
    addb("b2_bf", vec_cols(inputs["L0_attn_ln2_b"]))
    g["bconsts"] = np.ascontiguousarray(np.concatenate(bcols, axis=1))
    g["bconsts_reg"] = breg

    g["ident"] = np.ascontiguousarray(np.eye(128, dtype=np.float32))
    return g



DMA_SEMS = ["DC", "DBC", "DI0", "DI1", "DSS", "DS2", "DSH", "DCT0", "DCT1",
            "DO0", "DO1", "DD", "DGD", "DWK", "DWV", "DWR", "DV2", "DWU",
            "DWU2", "DOU", "DOV", "DF1U", "DF1V", "DF2U", "DF2V", "DACC0", "DACC1"]


class Prog:
    SEM_OF = {"t": "T", "v": "V", "s": "S"}

    def __init__(self, nc):
        self.nc = nc
        self.q = {k: [] for k in "tvsyg"}
        self.c = {sn: 0 for sn in ["T", "V", "S"] + DMA_SEMS}
        self.sems = None

    def _mk(self, e, fn, wait, incs):
        wait = tuple(w for w in wait if w is not None and w[1] > 0)
        def run(eng, fn=fn, wait=wait, incs=tuple(incs)):
            for sn, v in wait:
                eng.wait_ge(self.sems[sn], v)
            inst = fn(eng)
            for sn, amt in incs:
                inst = inst.then_inc(self.sems[sn], amt)
        self.q[e].append(run)

    def op(self, e, fn, wait=(), inc=False):
        sn = self.SEM_OF[e]
        if e in "vs":
            wait = list(wait) + [(sn, self.c[sn])]
            inc = True
        elif e == "t" and inc:
            wait = list(wait) + [(sn, self.c[sn])]
        incs = [(sn, 1)] if inc else []
        self._mk(e, fn, wait, incs)
        if inc:
            self.c[sn] += 1
            return self.c[sn]
        return None

    def t(self, fn, wait=(), inc=False): return self.op("t", fn, wait, inc)
    def v(self, fn, wait=(), inc=False): return self.op("v", fn, wait, inc)
    def s(self, fn, wait=(), inc=False): return self.op("s", fn, wait, inc)

    def dma(self, e, sem, fn, wait=()):
        self._mk(e, fn, wait, [(sem, 16)])
        self.c[sem] += 16
        return self.c[sem]

    def y(self, fn, wait=(), sem="DD"):  return self.dma("y", sem, fn, wait)
    def g(self, fn, wait=(), sem="DGD"): return self.dma("g", sem, fn, wait)

    def barrier(self, engines="tvs"):
        engines = [e for e in engines if e in "tvs"]
        marks = {}
        for e in engines:
            marks[e] = (self.SEM_OF[e], self.op(e, lambda eng: eng.nop(), inc=True))
        for e in engines:
            waits = [marks[o] for o in engines if o != e]
            self._mk(e, lambda eng: eng.nop(), waits, [])

    def flush(self):
        with self.nc.Block() as block:
            for e, sect in (("t", block.tensor), ("v", block.vector),
                            ("s", block.scalar), ("y", block.sync),
                            ("g", block.gpsimd)):
                ops = self.q[e]
                if ops:
                    sect(lambda eng, ops=tuple(ops): [f(eng) for f in ops])
                self.q[e] = []


class Pool:
    def __init__(self, p, tiles):
        self.p = p
        self.tiles = tiles
        self.st = [None] * len(tiles)
        self.rr = 0

    def acq(self):
        n = len(self.tiles)
        for k in range(n):
            i = (self.rr + k) % n
            if self.st[i] != "held":
                break
        else:
            raise RuntimeError("psum pool exhausted")
        self.rr = i + 1
        w = self.st[i]
        self.st[i] = "held"
        return self.tiles[i], i, w

    def rel(self, i, wait_tuple):
        assert self.st[i] == "held"
        self.st[i] = wait_tuple



def build_program(host, dumps=(), stop_after=None):
    nc = bass.Bass("TRN2", target_bir_lowering=False)
    p = Prog(nc)

    x_in = nc.dram_tensor("x_in", [128, CH * T], F32, kind="ExternalInput").ap()
    x_out = nc.dram_tensor("x_out", [128, CH * T], F32, kind="ExternalOutput").ap()
    dram = {}
    for name in WTENSORS:
        arr = host[name]
        dram[name] = nc.dram_tensor(name, list(arr.shape), F32, kind="ExternalInput").ap()
    spill_s = nc.dram_tensor("spill_s", [128, CH * T], F32, kind="Internal").ap()
    spill_h = nc.dram_tensor("spill_h", [128, CH * T], F32, kind="Internal").ap()
    dump_aps = {}
    for dname, ncols in dumps:
        dump_aps[dname] = nc.dram_tensor("dump_" + dname, [128, ncols], F32,
                                         kind="ExternalOutput").ap()
    dump_names = set(dump_aps)

    sreg = host["smalls_reg"]
    breg = host["bconsts_reg"]

    with ExitStack() as es:
        p.sems = {sn: es.enter_context(nc.semaphore(sn))
                  for sn in ["T", "V", "S"] + DMA_SEMS}

        csts = es.enter_context(nc.sbuf_tensor("csts", [128, host["smalls"].shape[1]], F32, side="left"))
        rows = es.enter_context(nc.sbuf_tensor("rows_sb", [1, 128], F32, side="left"))
        bcst = es.enter_context(nc.sbuf_tensor("bcst", [128, host["bconsts"].shape[1]], BF16, side="left"))
        ident = es.enter_context(nc.sbuf_tensor("ident_sb", [128, 128], F32, side="left"))
        statA = es.enter_context(nc.sbuf_tensor("statA", [1, T], F32, side="left"))
        bA = es.enter_context(nc.sbuf_tensor("bA", [128, T], F32, side="left"))
        bB = es.enter_context(nc.sbuf_tensor("bB", [128, T], F32, side="left"))
        bnd = es.enter_context(nc.sbuf_tensor("bnd", [128, CH], F32, side="left"))

        def scol(name, j=0):
            off, n = sreg[name]
            assert j < n, (name, j, n)
            return csts[:, off + j:off + j + 1]
        CONST_COL = {v: i for i, v in enumerate(CONST_VALS)}
        coff = sreg["consts"][0]
        def ccol(v):
            return csts[:, coff + CONST_COL[v]:coff + CONST_COL[v] + 1]
        for v in CONST_VALS:
            nc.const_aps.register(F32, float(v), ccol(v))
        def bsl(name, j0, n, p0=0, pn=128):
            off, _ = breg[name]
            return bcst[p0:p0 + pn, off + j0:off + j0 + n]

        p.y(lambda e: e.dma_start(csts[:, :], dram["smalls"]), sem="DC")
        p.y(lambda e: e.dma_start(rows[:, :], dram["rows"]), sem="DC")
        d_id = p.y(lambda e: e.dma_start(ident[:, :], dram["ident"]), sem="DC")
        d_bc = p.g(lambda e: e.dma_start(bcst[:, :], dram["bconsts"]), sem="DBC")
        W_INIT = [("DC", d_id), ("DBC", d_bc)]

        def finalize_early(buf):
            dst = x_out
            p.g(lambda e, buf=buf, dst=dst: e.dma_start(dst, buf[:, :]),
                wait=[("V", p.c["V"]), ("S", p.c["S"]), ("T", p.c["T"])], sem="DGD")
            p._mk("y", lambda eng: eng.nop(),
                  [(sn, p.c[sn]) for sn in DMA_SEMS if p.c[sn] > 0], [])
            p.flush()

        def emit_dump(name, buf, wait, bf16=False):
            if name not in dump_names:
                return
            if bf16:
                dm = p.g(lambda e, name=name, buf=buf: e.dma_start(dump_aps[name], buf[:, :]),
                         wait=list(wait), sem="DGD")
                gate = ("DGD", dm)
            else:
                dm = p.y(lambda e, name=name, buf=buf: e.dma_start(dump_aps[name], buf[:, :]),
                         wait=list(wait), sem="DD")
                gate = ("DD", dm)
            p.v(lambda e: e.nop(), wait=[gate])
            p.s(lambda e: e.nop(), wait=[gate])


        def emit_ln(pool, X, scratch, pre_w=(), pre_w_chunk=None, scratch2=None):
            inv_tile = [None] * NTQ
            neg_tile = [None] * NTQ
            half_T_end = [0, 0]
            scr_of = [scratch, scratch2 if scratch2 is not None else scratch]
            for half in range(2):
                scratch = scr_of[half]
                cast_marks = []
                for c in range(CH):
                    w = list(pre_w)
                    if pre_w_chunk is not None:
                        w.append(pre_w_chunk[c])
                    if half > 0 and scratch2 is None:
                        w.append(("T", half_T_end[0]))
                    cm = p.s(lambda e, c=c, half=half, X=X, scratch=scratch: e.activation(
                        scratch[:, TH * c:TH * (c + 1)],
                        X[:, T * c + TH * half: T * c + TH * half + TH],
                        AF.Identity), wait=w, inc=True)
                    cast_marks.append(cm)
                t1a, i1a, pw1a = pool.acq()
                t1b, i1b, pw1b = pool.acq()
                s1tiles = [t1a, t1b]
                c_done = []
                for c in range(CH):
                    for tt in range(2):
                        p.t(lambda e, tile=s1tiles[tt], c=c, tt=tt, scratch=scratch: e.matmul(
                            tile[0:1, :], bsl("ones", 0, 1),
                            scratch[:, TH * c + TQ * tt: TH * c + TQ * tt + TQ],
                            start=(c == 0), stop=(c == CH - 1)),
                            wait=[("S", cast_marks[c])] +
                                 ([pw1a] if (pw1a and c == 0 and tt == 0) else []) +
                                 ([pw1b] if (pw1b and c == 0 and tt == 1) else []),
                            inc=(tt == 1))
                    c_done.append(p.c["T"])
                sq_marks = []
                for c in range(CH):
                    vm = p.v(lambda e, c=c, scratch=scratch: e.tensor_tensor(
                        scratch[:, TH * c:TH * (c + 1)], scratch[:, TH * c:TH * (c + 1)],
                        scratch[:, TH * c:TH * (c + 1)], ALU.mult),
                        wait=[("T", c_done[c])], inc=True)
                    sq_marks.append(vm)
                t2a, i2a, pw2a = pool.acq()
                t2b, i2b, pw2b = pool.acq()
                s2tiles = [t2a, t2b]
                for c in range(CH):
                    for tt in range(2):
                        p.t(lambda e, tile=s2tiles[tt], c=c, tt=tt, scratch=scratch: e.matmul(
                            tile[0:1, :], bsl("ones", 0, 1),
                            scratch[:, TH * c + TQ * tt: TH * c + TQ * tt + TQ],
                            start=(c == 0), stop=(c == CH - 1)),
                            wait=[("V", sq_marks[c])] +
                                 ([pw2a] if (pw2a and c == 0 and tt == 0) else []) +
                                 ([pw2b] if (pw2b and c == 0 and tt == 1) else []),
                            inc=(tt == 1))
                half_T_end[half] = p.c["T"]
                mm_done = ("T", p.c["T"])
                for tt in range(2):
                    j = 2 * half + tt
                    sl = slice(TQ * j, TQ * (j + 1))
                    p.v(lambda e, tile=s1tiles[tt], sl=sl: e.tensor_scalar(
                        statA[0:1, sl], tile[0:1, :], float(INVD), None, ALU.mult),
                        wait=[mm_done], inc=True)
                    p.v(lambda e, tile=s2tiles[tt], sl=sl: e.tensor_scalar(
                        bB[0:1, sl], tile[0:1, :], float(INVD), None, ALU.mult), inc=True)
                    vrel = p.c["V"]
                    p.v(lambda e, sl=sl: e.tensor_tensor(
                        bA[0:1, sl], statA[0:1, sl], statA[0:1, sl], ALU.mult),
                        wait=[("V", p.c["V"])], inc=True)
                    vv = p.v(lambda e, sl=sl: e.tensor_tensor(
                        bB[0:1, sl], bB[0:1, sl], bA[0:1, sl], ALU.subtract),
                        wait=[("V", p.c["V"])], inc=True)
                    p.s(lambda e, sl=sl: e.activation(bB[0:1, sl], bB[0:1, sl], AF.Ln,
                                                      bias=ccol(EPS)[0:1, :], scale=1.0),
                        wait=[("V", vv)], inc=True)
                    sv = p.s(lambda e, sl=sl: e.activation(bB[0:1, sl], bB[0:1, sl],
                                                           AF.Exp, scale=-0.5), inc=True)
                    inv_tile[j] = ("S", sv)
                    nm = p.v(lambda e, sl=sl: e.scalar_tensor_tensor(
                        statA[0:1, sl], statA[0:1, sl], -1.0, bB[0:1, sl],
                        ALU.mult, ALU.mult), wait=[inv_tile[j]], inc=True)
                    neg_tile[j] = ("V", nm)
                    pool.rel(i1a if tt == 0 else i1b, ("V", vrel))
                    pool.rel(i2a if tt == 0 else i2b, ("V", vrel))
            return {"inv": inv_tile[-1][1], "neg": neg_tile[-1][1],
                    "inv_tile": inv_tile, "neg_tile": neg_tile}

        def emit_outer_pair(pool, rowA, rowB, wA, wB, wA_tile=None, wB_tile=None):
            for j in range(NTQ):
                tile, ti, pw = pool.acq()
                tm = p.t(lambda e, tile=tile, j=j, rowA=rowA: e.matmul(
                    tile[:, :], rows[0:1, :], rowA[0:1, TQ * j:TQ * (j + 1)],
                    start=True, stop=True),
                    wait=[wA_tile[j] if wA_tile else wA] + ([pw] if pw else []), inc=True)
                vmA = p.v(lambda e, tile=tile, j=j: e.tensor_copy(
                    bA[:, TQ * j:TQ * (j + 1)], tile[:, :]), wait=[("T", tm)], inc=True)
                pool.rel(ti, ("V", vmA))
                tile2, ti2, pw2 = pool.acq()
                tm2 = p.t(lambda e, tile2=tile2, j=j, rowB=rowB: e.matmul(
                    tile2[:, :], rows[0:1, :], rowB[0:1, TQ * j:TQ * (j + 1)],
                    start=True, stop=True),
                    wait=[wB_tile[j] if wB_tile else wB, ("V", vmA)] +
                         ([pw2] if pw2 else []), inc=True)
                vmB = p.v(lambda e, tile2=tile2, j=j: e.tensor_copy(
                    bB[:, TQ * j:TQ * (j + 1)], tile2[:, :]), wait=[("T", tm2)], inc=True)
                pool.rel(ti2, ("V", vmB))
            return ("V", p.c["V"])

        def emit_ln_tiles(pool, X, scratch, pre_w=(), pre_w_chunk=None, scratch2=None):
            r = emit_ln(pool, X, scratch, pre_w, pre_w_chunk, scratch2)
            return emit_outer_pair(pool, bB, statA, None, None,
                                   wA_tile=r["inv_tile"], wB_tile=r["neg_tile"])

        def emit_apply(X, OUT, mark_b, affine=None, extra_w=()):
            marks = []
            for c in range(CH):
                p.v(lambda e, c=c, X=X, OUT=OUT: e.tensor_tensor(
                    OUT[:, T * c:T * (c + 1)], X[:, T * c:T * (c + 1)], bA[:, :], ALU.mult),
                    wait=[mark_b] + list(extra_w), inc=True)
                vm = p.v(lambda e, c=c, OUT=OUT: e.tensor_tensor(
                    OUT[:, T * c:T * (c + 1)], OUT[:, T * c:T * (c + 1)], bB[:, :], ALU.add),
                    wait=[("V", p.c["V"])], inc=True)
                if affine is not None:
                    gname, bname = affine
                    sm = p.s(lambda e, c=c, gname=gname, bname=bname, OUT=OUT: e.activation(
                        OUT[:, T * c:T * (c + 1)], OUT[:, T * c:T * (c + 1)], AF.Identity,
                        bias=scol(bname, c), scale=scol(gname, c)),
                        wait=[("V", vm)], inc=True)
                    marks.append(("S", sm))
                else:
                    marks.append(("V", vm))
            return marks

        esS = ExitStack()
        sbS = esS.enter_context(nc.sbuf_tensor("S", [128, CH * T], F32, side="right"))

        ps8 = ExitStack()
        pool_tiles = [ps8.enter_context(nc.psum_tensor("pp%d" % i, [128, TQ], F32))
                      for i in range(8)]
        pool = Pool(p, pool_tiles)

        sv_in = sbS[:, :].rearrange("p (c t) -> p c t", t=T)
        xv_in = x_in.rearrange("p (c t) -> p c t", t=T)
        d_in = []
        for hh in range(2):
            d_in.append(p.y(lambda e, hh=hh: e.dma_start(
                sv_in[:, 4 * hh:4 * (hh + 1), :], xv_in[:, 4 * hh:4 * (hh + 1), :]),
                sem=["DI0", "DI1"][hh]))
        in_chunk_marks = [(["DI0", "DI1"][c // 4], d_in[c // 4]) for c in range(CH)]
        emit_dump("p0", sbS, [in_chunk_marks[-1]] + W_INIT)
        p.flush()

        def emit_ffn(X, XR, XK, KA, wr, wk, wv, rt, t1f, krl, wmarks, pre_w,
                     pre_w_chunk=None):
            mark_b = emit_ln_tiles(pool, X, XR, pre_w, pre_w_chunk, scratch2=XK)
            QTR = CH * TQ
            slot_free = [("T", 0), ("T", 0)]
            add_chunk_marks = {}

            def emit_mix(q):
                s0 = QTR * (q % 2)
                t0 = TQ * q
                marks = []
                for c in range(CH):
                    w = [mark_b, slot_free[q % 2]] + list(pre_w)
                    if q > 0:
                        p.v(lambda e, c=c, t0=t0: e.tensor_tensor(
                            bnd[:, c:c + 1], X[:, T * c + t0 - 1:T * c + t0],
                            bA[:, t0 - 1:t0], ALU.mult), wait=w, inc=True)
                        p.v(lambda e, c=c, t0=t0: e.tensor_tensor(
                            bnd[:, c:c + 1], bnd[:, c:c + 1], bB[:, t0 - 1:t0],
                            ALU.add), inc=True)
                        w = []
                    p.v(lambda e, c=c, t0=t0, s0=s0: e.tensor_tensor(
                        KA[:, s0 + TQ * c:s0 + TQ * (c + 1)],
                        X[:, T * c + t0:T * c + t0 + TQ],
                        bA[:, t0:t0 + TQ], ALU.mult), wait=w, inc=True)
                    p.v(lambda e, c=c, s0=s0: e.tensor_tensor(
                        KA[:, s0 + TQ * c:s0 + TQ * (c + 1)],
                        KA[:, s0 + TQ * c:s0 + TQ * (c + 1)],
                        bB[:, TQ * q:TQ * q + TQ], ALU.add), inc=True)
                    p.v(lambda e, c=c, s0=s0: e.tensor_tensor(
                        XK[:, s0 + TQ * c + 1:s0 + TQ * c + TQ],
                        KA[:, s0 + TQ * c + 1:s0 + TQ * c + TQ],
                        KA[:, s0 + TQ * c:s0 + TQ * c + TQ - 1], ALU.subtract), inc=True)
                    if q == 0:
                        p.v(lambda e, c=c, s0=s0: e.tensor_scalar(
                            XR[:, s0 + TQ * c:s0 + TQ * c + 1],
                            KA[:, s0 + TQ * c:s0 + TQ * c + 1],
                            scol("mur", c), None, ALU.mult), inc=True)
                        p.v(lambda e, c=c, s0=s0: e.tensor_scalar(
                            XK[:, s0 + TQ * c:s0 + TQ * c + 1],
                            KA[:, s0 + TQ * c:s0 + TQ * c + 1],
                            scol("muk", c), None, ALU.mult), inc=True)
                    else:
                        p.v(lambda e, c=c, s0=s0: e.tensor_tensor(
                            XR[:, s0 + TQ * c:s0 + TQ * c + 1],
                            KA[:, s0 + TQ * c:s0 + TQ * c + 1],
                            bnd[:, c:c + 1], ALU.subtract), inc=True)
                        p.v(lambda e, c=c, s0=s0: e.scalar_tensor_tensor(
                            XK[:, s0 + TQ * c:s0 + TQ * c + 1],
                            XR[:, s0 + TQ * c:s0 + TQ * c + 1],
                            scol("muk", c), bnd[:, c:c + 1], ALU.mult, ALU.add), inc=True)
                        p.v(lambda e, c=c, s0=s0: e.scalar_tensor_tensor(
                            XR[:, s0 + TQ * c:s0 + TQ * c + 1],
                            XR[:, s0 + TQ * c:s0 + TQ * c + 1],
                            scol("mur", c), bnd[:, c:c + 1], ALU.mult, ALU.add), inc=True)
                    p.v(lambda e, c=c, s0=s0: e.scalar_tensor_tensor(
                        XR[:, s0 + TQ * c + 1:s0 + TQ * c + TQ],
                        XK[:, s0 + TQ * c + 1:s0 + TQ * c + TQ],
                        scol("mur", c), KA[:, s0 + TQ * c:s0 + TQ * c + TQ - 1],
                        ALU.mult, ALU.add), inc=True)
                    vm = p.v(lambda e, c=c, s0=s0: e.scalar_tensor_tensor(
                        XK[:, s0 + TQ * c + 1:s0 + TQ * c + TQ],
                        XK[:, s0 + TQ * c + 1:s0 + TQ * c + TQ],
                        scol("muk", c), KA[:, s0 + TQ * c:s0 + TQ * c + TQ - 1],
                        ALU.mult, ALU.add), inc=True)
                    marks.append(vm)
                return ("V", marks[-1])

            mix_marks = {0: emit_mix(0)}
            for q in range(4):
                s0 = QTR * (q % 2)
                t0 = TQ * q
                mix_done = mix_marks[q]
                relu_last = 0
                for dk in range(CH):
                    tile, ti, pw = pool.acq()
                    for c in range(CH):
                        p.t(lambda e, tile=tile, dk=dk, c=c, s0=s0: e.matmul(
                            tile[:, :], wk[:, D * c + 128 * dk:D * c + 128 * (dk + 1)],
                            XK[:, s0 + TQ * c:s0 + TQ * (c + 1)],
                            start=(c == 0), stop=(c == CH - 1)),
                            wait=([mix_done, wmarks["wk"]] if c == 0 else []) +
                                 ([pw] if (pw and c == 0) else []),
                            inc=(c == CH - 1))
                    tmm = p.c["T"]
                    p.s(lambda e, tile=tile, krl=krl: e.activation(
                        krl[:, :], tile[:, :], AF.Relu), wait=[("T", tmm)], inc=True)
                    relu_last = p.s(lambda e, dk=dk, s0=s0, krl=krl: e.activation(
                        KA[:, s0 + TQ * dk:s0 + TQ * (dk + 1)], krl[:, :],
                        AF.Square), inc=True)
                    pool.rel(ti, ("S", relu_last))
                ka_done = ("S", relu_last)
                if q + 1 < 4:
                    mix_marks[q + 1] = emit_mix(q + 1)
                for dv in range(CH):
                    tileKV, tiKV, pwKV = pool.acq()
                    for c in range(CH):
                        p.t(lambda e, tile=tileKV, dv=dv, c=c, s0=s0: e.matmul(
                            tile[:, :], wv[:, D * c + 128 * dv:D * c + 128 * (dv + 1)],
                            KA[:, s0 + TQ * c:s0 + TQ * (c + 1)],
                            start=(c == 0), stop=(c == CH - 1)),
                            wait=([ka_done, wmarks["wv"]] if c == 0 else []) +
                                 ([pwKV] if (pwKV and c == 0) else []),
                            inc=(c == CH - 1))
                    tileR, tiR, pwR = pool.acq()
                    for c in range(CH):
                        p.t(lambda e, tile=tileR, dv=dv, c=c, s0=s0: e.matmul(
                            tile[:, :], wr[:, D * c + 128 * dv:D * c + 128 * (dv + 1)],
                            XR[:, s0 + TQ * c:s0 + TQ * (c + 1)],
                            start=(c == 0), stop=(c == CH - 1)),
                            wait=([wmarks["wr"]] if c == 0 else []) +
                                 ([pwR] if (pwR and c == 0) else []),
                            inc=(c == CH - 1))
                    tmm = p.c["T"]
                    sm = p.s(lambda e, tile=tileR, rt=rt: e.activation(
                        rt[:, :], tile[:, :], AF.Sigmoid),
                        wait=[("T", tmm), ("V", p.c["V"])], inc=True)
                    pool.rel(tiR, ("S", sm))
                    p.v(lambda e, tile=tileKV, rt=rt, t1f=t1f: e.scalar_tensor_tensor(
                        t1f[:, :], tile[:, :], 0.5, rt[:, :], ALU.mult, ALU.mult),
                        wait=[("S", sm)], inc=True)
                    vm = p.v(lambda e, dv=dv, t0=t0, X=X, t1f=t1f: e.tensor_tensor(
                        X[:, T * dv + t0:T * dv + t0 + TQ],
                        X[:, T * dv + t0:T * dv + t0 + TQ],
                        t1f[:, :], ALU.add), inc=True)
                    pool.rel(tiKV, ("V", vm))
                    add_chunk_marks[dv] = ("V", vm)
                slot_free[q % 2] = ("T", p.c["T"])
            last = add_chunk_marks[CH - 1]
            return last, [add_chunk_marks[c] for c in range(CH)]

        ffn1 = ExitStack()
        XRb = ffn1.enter_context(nc.sbuf_tensor("XR1", [128, CH * TH], BF16, side="right"))
        XKb = ffn1.enter_context(nc.sbuf_tensor("XK1", [128, CH * TH], BF16, side="right"))
        KAb = ffn1.enter_context(nc.sbuf_tensor("KA1", [128, CH * TH], BF16, side="right"))
        wrS = ffn1.enter_context(nc.sbuf_tensor("wrS1", [128, CH * D], BF16, side="right"))
        wkS = ffn1.enter_context(nc.sbuf_tensor("wkS1", [128, CH * D], BF16, side="right"))
        wvS = ffn1.enter_context(nc.sbuf_tensor("wvS1", [128, CH * D], BF16, side="right"))
        rt1 = ffn1.enter_context(nc.sbuf_tensor("rt1", [128, TQ], BF16, side="right"))
        krl1 = ffn1.enter_context(nc.sbuf_tensor("krl1", [128, TQ], BF16, side="right"))
        t1f1 = ffn1.enter_context(nc.sbuf_tensor("t1f1", [128, TQ], F32, side="right"))
        dwk = p.g(lambda e: e.dma_start(wkS[:, :], dram["wkT"]), sem="DWK")
        dwv = p.g(lambda e: e.dma_start(wvS[:, :], dram["wvT"]), sem="DWV")
        dwr = p.g(lambda e: e.dma_start(wrS[:, :], dram["wrT"]), sem="DWR")
        wm1 = {"wk": ("DWK", dwk), "wv": ("DWV", dwv), "wr": ("DWR", dwr)}
        mark_ffn1, ffn1_chunk_marks = emit_ffn(sbS, XRb, XKb, KAb, wrS, wkS, wvS,
                                               rt1, t1f1, krl1, wm1, W_INIT,
                                               in_chunk_marks)
        ffn1_T_end = ("T", p.c["T"])
        d_spill_s = p.y(lambda e: e.dma_start(spill_s, sbS[:, :]), wait=[mark_ffn1],
                        sem="DSS")
        emit_dump("ffn1", sbS, [mark_ffn1])
        p.barrier("tvs")
        p.flush()
        ffn1.close()
        if stop_after == "ffn1":
            finalize_early(sbS)
            return nc

        esH = ExitStack()
        sbH = esH.enter_context(nc.sbuf_tensor("H", [128, CH * T], F32, side="left"))
        statB1 = esH.enter_context(nc.sbuf_tensor("statB1", [1, T], F32, side="left"))
        statB2 = esH.enter_context(nc.sbuf_tensor("statB2", [1, T], F32, side="left"))
        p2a = ExitStack()
        scr = p2a.enter_context(nc.sbuf_tensor("scr", [128, 2 * CH * TH], BF16, side="right"))

        scrA = scr[:, 0:CH * TH]
        scrB = scr[:, CH * TH:2 * CH * TH]
        mark_b0 = emit_ln_tiles(pool, sbS, scrA, [], ffn1_chunk_marks, scratch2=scrB)
        hs_marks = emit_apply(sbS, sbH, mark_b0)
        mark_hs = hs_marks[-1]
        emit_dump("hs", sbH, [mark_hs])
        sv1 = p.v(lambda e: e.tensor_copy(statB1[0:1, :], bA[0:1, :]), wait=[mark_b0], inc=True)
        sv2 = p.v(lambda e: e.tensor_copy(statB2[0:1, :], bB[0:1, :]), inc=True)
        spvh = spill_h.rearrange("p (c t) -> p c t", t=T)
        sbhv = sbH[:, :].rearrange("p (c t) -> p c t", t=T)
        d_spill_h1 = p.y(lambda e: e.dma_start(spvh[:, 0:4, :], sbhv[:, 0:4, :]),
                         wait=[hs_marks[3]], sem="DSH")
        d_spill_h2 = p.y(lambda e: e.dma_start(spvh[:, 4:8, :], sbhv[:, 4:8, :]),
                         wait=[mark_hs], sem="DS2")
        uq = p.v(lambda e: e.tensor_tensor(statA[0:1, :], statB1[0:1, :], statB1[0:1, :],
                                           ALU.mult), wait=[("V", sv2)], inc=True)
        p.s(lambda e: e.activation(statA[0:1, :], statA[0:1, :], AF.Identity,
                                   bias=ccol(1.0)[0:1, :], scale=-1.0),
            wait=[("V", uq)], inc=True)
        p.s(lambda e: e.activation(statA[0:1, :], statA[0:1, :], AF.Ln,
                                   bias=ccol(1.0)[0:1, :], scale=float(EPS)), inc=True)
        si1 = p.s(lambda e: e.activation(statA[0:1, :], statA[0:1, :], AF.Exp,
                                         scale=-0.5), inc=True)
        p.v(lambda e: e.tensor_tensor(statB1[0:1, :], statB1[0:1, :], statA[0:1, :],
                                      ALU.mult), wait=[("S", si1)], inc=True)
        cm2 = p.v(lambda e: e.tensor_tensor(statB2[0:1, :], statB2[0:1, :], statA[0:1, :],
                                            ALU.mult), wait=[("V", p.c["V"])], inc=True)
        mark_bx = emit_outer_pair(pool, statB1, statB2, ("V", cm2), ("V", cm2))
        p.flush()
        p2a.close()
        esH.close()
        if stop_after == "p2a":
            finalize_early(sbS)
            return nc

        esx1 = ExitStack()
        x1b = esx1.enter_context(nc.sbuf_tensor("x1", [128, CH * T], BF16, side="left"))
        x1_marks = emit_apply(sbS, x1b, mark_bx, affine=("g1", "b1"),
                              extra_w=[("DSH", d_spill_h1)])
        mark_x1 = x1_marks[-1]
        x1_chunk_marks = list(x1_marks)
        emit_dump("x1", x1b, [mark_x1], bf16=True)
        p.barrier("tvs")
        p.flush()
        esS.close()

        esQB = ExitStack()
        QB = esQB.enter_context(nc.sbuf_tensor("QB", [128, CH * T], BF16, side="right"))
        esKV = ExitStack()
        KB = esKV.enter_context(nc.sbuf_tensor("KB", [128, CH * T], BF16, side="right"))
        VA = esKV.enter_context(nc.sbuf_tensor("VA", [128, 16 * VAW], BF16, side="right"))
        pb = ExitStack()
        ATb = pb.enter_context(nc.sbuf_tensor("AT", [128, 8 * TQ], BF16, side="right"))
        CTb = pb.enter_context(nc.sbuf_tensor("CT", [64, 2 * TQ], BF16, side="right"))
        rlb = pb.enter_context(nc.sbuf_tensor("rl", [128, 2 * TQ], BF16, side="right"))
        brl = pb.enter_context(nc.sbuf_tensor("brl", [64, 2 * TQ], BF16, side="right"))
        lnt = pb.enter_context(nc.sbuf_tensor("lnt", [128, TQ], F32, side="right"))
        pa = ExitStack()
        Pq = pa.enter_context(nc.sbuf_tensor("Pq", [128, 3 * T], BF16, side="right"))
        Wu = pa.enter_context(nc.sbuf_tensor("Wu", [128, 2 * 3072], BF16, side="right"))
        V2s = pa.enter_context(nc.sbuf_tensor("V2s", [128, 3 * 1024], BF16, side="right"))

        ps8.close()
        psB = ExitStack()
        SD = [psB.enter_context(nc.psum_tensor("sd%d" % i, [128, 2 * TQ], F32)) for i in range(2)]
        pool4 = Pool(p, [psB.enter_context(nc.psum_tensor("pb%d" % i, [128, TQ], F32))
                         for i in range(4)])

        spill_gate = [("DSS", d_spill_s), ("DSH", d_spill_h1), ("DS2", d_spill_h2)]
        d_v2 = p.g(lambda e: e.dma_start(V2s[:, :], dram["V2"]),
                   wait=[ffn1_T_end, ("DSH", d_spill_h1), ("DS2", d_spill_h2)], sem="DV2")
        VA_view = VA[:, :].rearrange("p (tc h e) -> p tc h e", h=NH, e=DH + 1)
        ones_set = p.v(lambda e: e.memset(VA_view[:, :, :, DH:DH + 1], 1.0),
                       wait=spill_gate, inc=True)

        pq_free = [("T", 0), ("T", 0), ("T", 0)]
        wu_free = [("T", 0), ("T", 0)]
        qk_ready = {}
        va_ready = {}
        qb_of = {0: "qb", 1: "kb"}

        qkv_queue = []

        def drain_qkv(n=1):
            k = 0
            while qkv_queue and k < n:
                qkv_queue.pop(0)[1]()
                k += 1
            return k

        def drain_tag(tag):
            while qkv_queue and qkv_queue[0][0] <= tag:
                qkv_queue.pop(0)[1]()

        def emit_qkv_pair(m):
            wslot = m % 2
            woff = 3072 * wslot
            wsem = "DWU" if wslot == 0 else "DWU2"
            dwu = p.g(lambda e, m=m, woff=woff: e.dma_start(
                Wu[:, woff:woff + 3072], dram["Upm"][:, 3072 * m:3072 * (m + 1)]),
                wait=[wu_free[wslot]] + ([ffn1_T_end] if m < 2 else []), sem=wsem)
            wu_mark = (wsem, dwu)
            state = {"s1_done": {}}

            def mk_s1(tt, pi):
                def run(tt=tt, pi=pi, m=m, woff=woff, wu_mark=wu_mark):
                    tile, ti, pw = pool4.acq()
                    for c in range(CH):
                        p.t(lambda e, tile=tile, m=m, pi=pi, tt=tt, c=c, woff=woff: e.matmul(
                            tile[:, :],
                            Wu[:, woff + 1024 * pi + 128 * c:woff + 1024 * pi + 128 * (c + 1)],
                            x1b[:, T * c + TQ * tt:T * c + TQ * tt + TQ],
                            start=(c == 0), stop=(c == CH - 1)),
                            wait=([wu_mark, pq_free[pi]] if c == 0 else []) +
                                 [x1_chunk_marks[c]] +
                                 ([pw] if (pw and c == 0) else []),
                            inc=(c == CH - 1))
                    tmm = p.c["T"]
                    vm = p.v(lambda e, tile=tile, pi=pi, tt=tt: e.tensor_copy(
                        Pq[:, T * pi + TQ * tt:T * pi + TQ * tt + TQ], tile[:, :]),
                        wait=[("T", tmm)], inc=True)
                    pool4.rel(ti, ("V", vm))
                    if tt == NTQ - 1:
                        state["s1_done"][pi] = ("V", vm)
                return run

            def mk_s2qk(tt, pi):
                def run(tt=tt, pi=pi, m=m):
                    dstb = QB if pi == 0 else KB
                    tile, ti, pw = pool4.acq()
                    p.t(lambda e, tile=tile, pi=pi, m=m, tt=tt: e.matmul(
                        tile[:, :], V2s[:, 1024 * pi + 128 * m:1024 * pi + 128 * (m + 1)],
                        Pq[:, T * pi + TQ * tt:T * pi + TQ * tt + TQ],
                        start=True, stop=True),
                        wait=[state["s1_done"][pi], ("DV2", d_v2)] + ([pw] if pw else []),
                        inc=True)
                    tmm = p.c["T"]
                    sm = p.v(lambda e, tile=tile, dstb=dstb, m=m, tt=tt, pi=pi: e.tensor_scalar(
                        dstb[:, T * m + TQ * tt:T * m + TQ * tt + TQ], tile[:, :],
                        scol(qb_of[pi], m), None, ALU.add),
                        wait=[("T", tmm)] +
                             (spill_gate if (m == 0 and pi == 0 and tt == 0) else []),
                        inc=True)
                    pool4.rel(ti, ("V", sm))
                    if tt == NTQ - 1:
                        pq_free[pi] = ("T", p.c["T"])
                        if pi == 1:
                            qk_ready[m] = ("V", p.c["V"])
                return run

            def mk_s2v(tg):
                def run(tg=tg, m=m, wslot=wslot):
                    tile, ti, pw = pool4.acq()
                    for k in range(4):
                        tc = tg * 4 + k
                        p.t(lambda e, tile=tile, m=m, tc=tc, k=k: e.matmul(
                            tile[:, 128 * k:128 * (k + 1)],
                            Pq[:, 2 * T + 128 * tc:2 * T + 128 * (tc + 1)],
                            V2s[:, 2048 + 128 * m:2048 + 128 * (m + 1)],
                            start=True, stop=True),
                            wait=([state["s1_done"][2], ("V", ones_set), ("DV2", d_v2)] if k == 0 else []) +
                                 ([pw] if (pw and k == 0) else []),
                            inc=(k == 3))
                    tmm = p.c["T"]
                    dst = VA_view[:, tg * 4:tg * 4 + 4, 2 * m:2 * m + 2, 0:DH]
                    src_ = tile[:, :].rearrange("p (tc rest) -> p tc rest", rest=128) \
                                     .rearrange("p tc (he e) -> p tc he e", e=DH)
                    vm = p.v(lambda e, dst=dst, src_=src_: e.tensor_copy(dst, src_),
                             wait=[("T", tmm)], inc=True)
                    pool4.rel(ti, ("V", vm))
                    if tg == 3:
                        pq_free[2] = ("T", p.c["T"])
                        va_ready[m] = ("V", p.c["V"])
                        wu_free[wslot] = ("T", p.c["T"])
                return run

            for pi in range(2):
                for tt in range(NTQ):
                    qkv_queue.append((m, mk_s1(tt, pi)))
                for tt in range(NTQ):
                    qkv_queue.append((m, mk_s2qk(tt, pi)))
            for tt in range(NTQ):
                qkv_queue.append((m, mk_s1(tt, 2)))
            for tg in range(4):
                qkv_queue.append((m, mk_s2v(tg)))

        sd_free = [("S", 0), ("S", 0)]
        at_free = {(s, pp): ("T", 0) for s in range(2) for pp in range(2)}
        ct_free = [("DCT0", 0), ("DCT1", 0)]
        rl_free = [("V", 0), ("V", 0)]
        brl_free = [("V", 0), ("V", 0)]
        tail_idx = [0]
        pending_tails = [[]]
        hooks = {}

        def make_tail(hc, hp, j, tC, iC, t_av):
            par = tail_idx[0] % 2
            tail_idx[0] += 1

            def tailA():
                p.s(lambda e: e.activation(
                    lnt[64:65, 0:TQ], tC[DH:DH + 1, :], AF.Ln),
                    wait=[("T", t_av), rl_free[par]], inc=True)
                vr_ = p.s(lambda e: e.activation(
                    rlb[64:65, TQ * par:TQ * par + TQ], lnt[64:65, 0:TQ],
                    AF.Exp, scale=-1.0), inc=True)
                return vr_

            def tailB(vr):
                tileB, tiB, pwB = pool4.acq()
                tb = p.t(lambda e, tileB=tileB: e.matmul(
                    tileB[0:64, :], bsl("ones", 0, 64, p0=64, pn=1),
                    rlb[64:65, TQ * par:TQ * par + TQ], start=True, stop=True),
                    wait=[("S", vr)] + ([pwB] if pwB else []), inc=True)
                rl_free[par] = ("T", tb)
                sb_ = p.v(lambda e, tileB=tileB: e.tensor_copy(
                    brl[0:64, TQ * par:TQ * par + TQ], tileB[0:64, :]),
                    wait=[("T", tb), brl_free[par]], inc=True)
                pool4.rel(tiB, ("V", sb_))
                if hp == 0:
                    vm = p.v(lambda e: e.tensor_tensor(
                        QB[0:64, T * hc + TQ * j:T * hc + TQ * j + TQ], tC[0:64, :],
                        brl[0:64, TQ * par:TQ * par + TQ], ALU.mult),
                        wait=[("V", sb_)], inc=True)
                    brl_free[par] = ("V", vm)
                    pool4.rel(iC, ("V", vm))
                else:
                    cpar = par
                    vm = p.v(lambda e: e.tensor_tensor(
                        CTb[0:64, TQ * cpar:TQ * cpar + TQ], tC[0:64, :],
                        brl[0:64, TQ * par:TQ * par + TQ], ALU.mult),
                        wait=[("V", sb_), ct_free[cpar]], inc=True)
                    brl_free[par] = ("V", vm)
                    pool4.rel(iC, ("V", vm))
                    dm = p.y(lambda e: e.dma_start(
                        QB[64:128, T * hc + TQ * j:T * hc + TQ * j + TQ],
                        CTb[0:64, TQ * cpar:TQ * cpar + TQ]), wait=[("V", vm)],
                        sem="DCT%d" % cpar)
                    ct_free[cpar] = ("DCT%d" % cpar, dm)

            def tail():
                vr = tailA()
                return lambda vr=vr: tailB(vr)
            return tail

        def emit_attention_hc(hc):
            drain_tag(hc)
            for j in range(NTQ):
                kcm = 4 * (j + 1)
                npair = kcm // 2
                exp_marks = [{}, {}]
                msk_marks = [{}, {}]
                ctx_hold = [{}, {}]

                def emit_qk_quad(ip, hc=hc, j=j, exp_marks=exp_marks, msk_marks=msk_marks):
                    i0 = 2 * ip
                    if not drain_qkv(1):
                        for s in range(2):
                            p.t(lambda e, s=s: e.matmul(
                                SD[s][:, 0:TQ], bsl("masks", 0, 128), bsl("masks", 0, TQ),
                                start=True, stop=True), wait=[sd_free[s]])
                    for u in range(2):
                        i = i0 + u
                        for s in range(2):
                            r0 = 64 * s
                            p.t(lambda e, s=s, u=u, i=i, r0=r0, hc=hc, j=j: e.matmul(
                                SD[s][:, TQ * u:TQ * (u + 1)],
                                KB[r0:r0 + 64, T * hc + 128 * i:T * hc + 128 * (i + 1)],
                                QB[r0:r0 + 64, T * hc + TQ * j:T * hc + TQ * j + TQ],
                                start=True, stop=True),
                                wait=([sd_free[s]] if u == 0 else []) +
                                     ([qk_ready[hc]] if (u == 0 and s == 0 and ip == 0 and j == 0) else []),
                                inc=(u == 1 and s == 1))
                    tmm = p.c["T"]
                    pp = ip % 2
                    for s in range(2):
                        sm = p.s(lambda e, s=s, i0=i0, pp=pp: e.activation(
                            ATb[:, TQ * (4 * s + 2 * pp):TQ * (4 * s + 2 * pp) + 2 * TQ],
                            SD[s][:, :], AF.Exp, scale=0.125),
                            wait=[("T", tmm), at_free[(s, pp)]], inc=True)
                        sd_free[s] = ("S", sm)
                        for u in range(2):
                            i = i0 + u
                            exp_marks[s][i] = ("S", sm)
                            if i >= 4 * j:
                                d = i - 4 * j
                                slot = 4 * s + 2 * pp + u
                                mm_ = p.v(lambda e, slot=slot, d=d: e.tensor_tensor(
                                    ATb[:, TQ * slot:TQ * slot + TQ],
                                    ATb[:, TQ * slot:TQ * slot + TQ],
                                    bsl("masks", TQ * d, TQ), ALU.mult),
                                    wait=[("S", sm)], inc=True)
                                msk_marks[s][i] = ("V", mm_)

                def emit_av_quad(ip, last, hc=hc, exp_marks=exp_marks, msk_marks=msk_marks,
                                 ctx_hold=ctx_hold):
                    i0 = 2 * ip
                    for s in range(2):
                        h = 2 * hc + s
                        for u in range(2):
                            i = i0 + u
                            w = [msk_marks[s].get(i, exp_marks[s][i])]
                            if i == 0:
                                tC, iC, pwC = pool4.acq()
                                ctx_hold[s]["t"] = tC
                                ctx_hold[s]["i"] = iC
                                if pwC:
                                    w.append(pwC)
                                w.append(va_ready[hc])
                            tC = ctx_hold[s]["t"]
                            pp = ip % 2
                            slot = 4 * s + 2 * pp + u
                            is_last = last and u == 1
                            p.t(lambda e, i=i, h=h, tC=tC, slot=slot, is_last=is_last: e.matmul(
                                tC[0:DH + 1, :], VA[:, VAW * i + 65 * h:VAW * i + 65 * h + 65],
                                ATb[:, TQ * slot:TQ * slot + TQ],
                                start=(i == 0), stop=is_last),
                                wait=w, inc=is_last)
                            if u == 1:
                                at_free[(s, pp)] = ("T", p.c["T"])

                emit_qk_quad(0)
                if npair == 1:
                    for t_ in pending_tails[0]:
                        t_()
                    pending_tails[0] = []
                for ip in range(1, npair):
                    emit_qk_quad(ip)
                    if ip == 1:
                        for t_ in pending_tails[0]:
                            t_()
                        pending_tails[0] = []
                    emit_av_quad(ip - 1, False)
                emit_av_quad(npair - 1, True)
                t_av = p.c["T"]
                pending_tails[0] = [t_() for t_ in [
                    make_tail(hc, 0, j, ctx_hold[0]["t"], ctx_hold[0]["i"], t_av),
                    make_tail(hc, 1, j, ctx_hold[1]["t"], ctx_hold[1]["i"], t_av)]]
                if hc == 4 and j == 0:
                    hooks["hc4"] = ("T", p.c["T"])

        for m in range(CH):
            emit_qkv_pair(m)
            if m >= 1:
                emit_attention_hc(m - 1)
                if m - 1 == 3:
                    hooks["hc3_done"] = ("T", p.c["T"])
                if m - 1 == 6:
                    hooks["hc6_done"] = ("T", p.c["T"])
        drain_tag(CH)
        hooks["qkv_done"] = ("T", p.c["T"])
        emit_dump("k", KB, [("S", p.c["S"])], bf16=True)
        emit_dump("va", VA, [("V", p.c["V"])], bf16=True)
        p.flush()
        pa.close()
        esx1.close()

        esH2 = ExitStack()
        sbH2 = esH2.enter_context(nc.sbuf_tensor("H2", [128, CH * T], F32, side="left"))
        h2v = sbH2[:, :].rearrange("p (c t) -> p c t", t=T)
        spv = spill_h.rearrange("p (c t) -> p c t", t=T)
        gate_h2 = [("T", p.c["T"]), ("S", p.c["S"])]
        d_rel_h = []
        sems_rh = ["DSH", "DS2"]
        spill_done = [("DSH", d_spill_h1), ("DS2", d_spill_h2)]
        for hh in range(2):
            d_rel_h.append(p.y(lambda e, hh=hh: e.dma_start(
                h2v[:, 4 * hh:4 * (hh + 1), :], spv[:, 4 * hh:4 * (hh + 1), :]),
                wait=gate_h2 + [spill_done[hh]], sem=sems_rh[hh]))
        emit_attention_hc(CH - 1)
        for t_ in pending_tails[0]:
            t_()
        pending_tails[0] = []
        ctx_dma_done = [("DCT0", p.c["DCT0"]), ("DCT1", p.c["DCT1"])]
        emit_dump("ctx", QB, [("V", p.c["V"])] + ctx_dma_done, bf16=True)
        p.barrier("tvs")
        p.t(lambda e: e.nop(), wait=ctx_dma_done)
        p.flush()
        psB.close()
        pb.close()
        esKV.close()

        ps8b = ExitStack()
        pool_tiles2 = [ps8b.enter_context(nc.psum_tensor("pq%d" % i, [128, TQ], F32))
                       for i in range(8)]
        pool = Pool(p, pool_tiles2)

        po = ExitStack()
        oU = po.enter_context(nc.sbuf_tensor("oU", [128, CH * 768], BF16, side="right"))
        oV = po.enter_context(nc.sbuf_tensor("oV", [128, 6 * D], BF16, side="right"))
        O1 = po.enter_context(nc.sbuf_tensor("O1", [128, 6 * T], BF16, side="right"))
        b1sb = po.enter_context(nc.sbuf_tensor("b1sb", [128, 6], F32, side="right"))
        d_oU = p.g(lambda e: e.dma_start(oU[:, :], dram["outU"]),
                   wait=[hooks["hc3_done"]], sem="DOU")
        d_oV = p.g(lambda e: e.dma_start(oV[:, :], dram["outV"]),
                   wait=[hooks["hc6_done"]], sem="DOV")
        tile, ti, pw = pool.acq()
        for m in range(6):
            for c in range(CH):
                p.t(lambda e, tile=tile, m=m, c=c: e.matmul(
                    tile[:, m:m + 1], oU[:, 768 * c + 128 * m:768 * c + 128 * (m + 1)],
                    bsl("vb_bf", c, 1), start=(c == 0), stop=(c == CH - 1)),
                    wait=([("DOU", d_oU)] if (m == 0 and c == 0) else []) +
                         ([pw] if (pw and m == 0 and c == 0) else []),
                    inc=(m == 5 and c == CH - 1))
        tb1 = p.c["T"]
        vb1 = p.v(lambda e, tile=tile: e.tensor_copy(b1sb[:, :], tile[:, 0:6]),
                  wait=[("T", tb1)], inc=True)
        pool.rel(ti, ("V", vb1))
        for m in range(6):
            for tt in range(NTQ):
                tile, ti, pw = pool.acq()
                for c in range(CH):
                    p.t(lambda e, tile=tile, m=m, tt=tt, c=c: e.matmul(
                        tile[:, :], oU[:, 768 * c + 128 * m:768 * c + 128 * (m + 1)],
                        QB[:, T * c + TQ * tt:T * c + TQ * tt + TQ],
                        start=(c == 0), stop=(c == CH - 1)),
                        wait=([pw] if (pw and c == 0) else []), inc=(c == CH - 1))
                tmm = p.c["T"]
                sm = p.s(lambda e, tile=tile, m=m, tt=tt: e.activation(
                    O1[:, T * m + TQ * tt:T * m + TQ * tt + TQ], tile[:, :], AF.Identity,
                    bias=b1sb[:, m:m + 1], scale=1.0),
                    wait=[("T", tmm), ("V", vb1)], inc=True)
                pool.rel(ti, ("S", sm))
        s1_done = ("S", p.c["S"])
        h2_chunk_marks = {}
        for do in range(CH):
            for tt in range(NTQ):
                tile, ti, pw = pool.acq()
                for k6 in range(6):
                    p.t(lambda e, tile=tile, do=do, tt=tt, k6=k6: e.matmul(
                        tile[:, :], oV[:, D * k6 + 128 * do:D * k6 + 128 * (do + 1)],
                        O1[:, T * k6 + TQ * tt:T * k6 + TQ * tt + TQ],
                        start=(k6 == 0), stop=(k6 == 5)),
                        wait=([s1_done, ("DOV", d_oV)] if k6 == 0 else []) +
                             ([pw] if (pw and k6 == 0) else []),
                        inc=(k6 == 5))
                tmm = p.c["T"]
                vm = p.v(lambda e, tile=tile, do=do, tt=tt: e.scalar_tensor_tensor(
                    sbH2[:, T * do + TQ * tt:T * do + TQ * tt + TQ], tile[:, :],
                    scol("outb", do), sbH2[:, T * do + TQ * tt:T * do + TQ * tt + TQ],
                    ALU.add, ALU.add),
                    wait=[("T", tmm), (sems_rh[do // 4], d_rel_h[do // 4])], inc=True)
                pool.rel(ti, ("V", vm))
                h2_chunk_marks[do] = ("V", vm)
        mark_h = ("V", p.c["V"])
        emit_dump("h", sbH2, [mark_h])
        p.barrier("tvs")
        p.flush()
        po.close()
        esQB.close()
        if stop_after == "outproj":
            finalize_early(sbH2)
            return nc

        pm0 = ExitStack()
        scr2 = pm0.enter_context(nc.sbuf_tensor("scr2", [128, 2 * CH * TH], BF16, side="right"))
        mark_b2 = emit_ln_tiles(pool, sbH2, scr2[:, 0:CH * TH], [],
                                [h2_chunk_marks[c] for c in range(CH)],
                                scratch2=scr2[:, CH * TH:2 * CH * TH])
        p.flush()
        pm0.close()

        pm = ExitStack()
        f1U = pm.enter_context(nc.sbuf_tensor("f1U", [128, CH * 512], BF16, side="right"))
        f1V = pm.enter_context(nc.sbuf_tensor("f1V", [128, 4 * 4096], BF16, side="right"))
        f2U = pm.enter_context(nc.sbuf_tensor("f2U", [128, 32 * 512], BF16, side="right"))
        f2V = pm.enter_context(nc.sbuf_tensor("f2V", [128, 4 * D], BF16, side="right"))
        x2t = pm.enter_context(nc.sbuf_tensor("x2t", [128, 2 * CH * TQ], BF16, side="right"))
        P1t = pm.enter_context(nc.sbuf_tensor("P1t", [128, 4 * TQ], BF16, side="right"))
        P2t = pm.enter_context(nc.sbuf_tensor("P2t", [128, 4 * TQ], BF16, side="right"))
        HMt = pm.enter_context(nc.sbuf_tensor("HMt", [128, 2 * TQ], BF16, side="right"))
        mlp_gate = ("T", p.c["T"])
        b2m = pm.enter_context(nc.sbuf_tensor("b2m", [128, 4], F32, side="right"))
        d_f1U = p.g(lambda e: e.dma_start(f1U[:, :], dram["fc1U"]), wait=[mlp_gate], sem="DF1U")
        d_f1V = p.g(lambda e: e.dma_start(f1V[:, :], dram["fc1V"]), wait=[mlp_gate], sem="DF1V")
        d_f2U = p.g(lambda e: e.dma_start(f2U[:, :], dram["fc2U"]), wait=[mlp_gate], sem="DF2U")
        d_f2V = p.g(lambda e: e.dma_start(f2V[:, :], dram["fc2V"]), wait=[mlp_gate], sem="DF2V")

        tileb, tib, pwb = pool.acq()
        for pc in range(4):
            for c in range(CH):
                p.t(lambda e, tileb=tileb, pc=pc, c=c: e.matmul(
                    tileb[:, pc:pc + 1], f1U[:, 512 * c + 128 * pc:512 * c + 128 * (pc + 1)],
                    bsl("b2_bf", c, 1), start=(c == 0), stop=(c == CH - 1)),
                    wait=([("DF1U", d_f1U)] if (pc == 0 and c == 0) else []) +
                         ([pwb] if (pwb and pc == 0 and c == 0) else []),
                    inc=(pc == 3 and c == CH - 1))
        t_b2m = p.c["T"]
        v_b2m = p.v(lambda e, tileb=tileb: e.tensor_copy(b2m[:, :], tileb[:, 0:4]),
                    wait=[("T", t_b2m)], inc=True)
        pool.rel(tib, ("V", v_b2m))
        g2_marks = []
        for c in range(CH):
            smsc = p.s(lambda e, c=c: e.activation(
                f1U[:, 512 * c:512 * (c + 1)], f1U[:, 512 * c:512 * (c + 1)],
                AF.Identity, bias=0.0, scale=scol("g2", c)),
                wait=[("T", t_b2m)], inc=True)
            g2_marks.append(smsc)
        f1u_ready = ("S", g2_marks[-1])

        x2_free = [("T", 0), ("T", 0)]
        p1_free = ("T", 0)
        p2_free = ("V", 0)

        def emit_x2(jt):
            slot = jt % 2
            x0 = CH * TQ * slot
            marks = []
            for c in range(CH):
                p.v(lambda e, c=c, jt=jt, x0=x0: e.tensor_tensor(
                    x2t[:, x0 + TQ * c:x0 + TQ * (c + 1)],
                    sbH2[:, T * c + TQ * jt:T * c + TQ * jt + TQ],
                    bA[:, TQ * jt:TQ * jt + TQ], ALU.mult),
                    wait=[mark_b2, x2_free[slot], mark_h], inc=True)
                vmx = p.v(lambda e, c=c, jt=jt, x0=x0: e.tensor_tensor(
                    x2t[:, x0 + TQ * c:x0 + TQ * (c + 1)], x2t[:, x0 + TQ * c:x0 + TQ * (c + 1)],
                    bB[:, TQ * jt:TQ * jt + TQ], ALU.add), inc=True)
                marks.append(("V", vmx))
            return marks

        mlp_chunk_marks = {}
        x2_marks_by_tile = {0: emit_x2(0)}
        for jt in range(NTQ):
            slot = jt % 2
            x0 = CH * TQ * slot
            x2_marks = x2_marks_by_tile[jt]
            p1_marks = []
            for pc in range(4):
                tile, ti, pw = pool.acq()
                for c in range(CH):
                    p.t(lambda e, tile=tile, pc=pc, c=c, x0=x0: e.matmul(
                        tile[:, :], f1U[:, 512 * c + 128 * pc:512 * c + 128 * (pc + 1)],
                        x2t[:, x0 + TQ * c:x0 + TQ * (c + 1)],
                        start=(c == 0), stop=(c == CH - 1)),
                        wait=([x2_marks[-1], f1u_ready, p1_free] if c == 0 else []) +
                             ([pw] if (pw and c == 0) else []),
                        inc=(c == CH - 1))
                tmm = p.c["T"]
                vm = p.s(lambda e, tile=tile, pc=pc: e.activation(
                    P1t[:, TQ * pc:TQ * (pc + 1)], tile[:, :], AF.Identity,
                    bias=b2m[:, pc:pc + 1], scale=1.0),
                    wait=[("T", tmm), ("V", v_b2m)], inc=True)
                pool.rel(ti, ("S", vm))
                p1_marks.append(vm)
            p1_done = ("S", p1_marks[-1])
            s1_T = p.c["T"]
            if jt + 1 < NTQ:
                x2_marks_by_tile[jt + 1] = emit_x2(jt + 1)
            acc = []
            for p2 in range(4):
                tileA, tiA, pwA = pool.acq()
                acc.append((tileA, tiA, pwA))
            hm_free = [("T", 0), ("T", 0)]
            for ic in range(32):
                tile, ti, pw = pool.acq()
                for pc in range(4):
                    p.t(lambda e, tile=tile, ic=ic, pc=pc: e.matmul(
                        tile[:, :], f1V[:, 4096 * pc + 128 * ic:4096 * pc + 128 * (ic + 1)],
                        P1t[:, TQ * pc:TQ * (pc + 1)],
                        start=(pc == 0), stop=(pc == 3)),
                        wait=([p1_done, ("DF1V", d_f1V)] if pc == 0 else []) +
                             ([pw] if (pw and pc == 0) else []),
                        inc=(pc == 3))
                tmm = p.c["T"]
                hslot = ic % 2
                smg = p.s(lambda e, tile=tile, ic=ic, hslot=hslot: e.activation(
                    HMt[:, TQ * hslot:TQ * (hslot + 1)], tile[:, :], AF.Gelu_apprx_tanh,
                    bias=scol("fc1b", ic), scale=1.0),
                    wait=[("T", tmm), hm_free[hslot]], inc=True)
                pool.rel(ti, ("S", smg))
                for p2 in range(4):
                    tileA, tiA, pwA = acc[p2]
                    p.t(lambda e, tileA=tileA, ic=ic, p2=p2, hslot=hslot: e.matmul(
                        tileA[:, :], f2U[:, 512 * ic + 128 * p2:512 * ic + 128 * (p2 + 1)],
                        HMt[:, TQ * hslot:TQ * (hslot + 1)],
                        start=(ic == 0), stop=(ic == 31)),
                        wait=[("S", smg), ("DF2U", d_f2U)] +
                             ([pwA] if (pwA and ic == 0) else []) +
                             ([p2_free] if ic == 0 else []),
                        inc=(p2 == 3))
                hm_free[hslot] = ("T", p.c["T"])
            t_acc = p.c["T"]
            for p2 in range(4):
                tileA, tiA, _ = acc[p2]
                vm = p.s(lambda e, tileA=tileA, p2=p2: e.activation(
                    P2t[:, TQ * p2:TQ * (p2 + 1)], tileA[:, :], AF.Identity,
                    bias=0.0, scale=1.0), wait=[("T", t_acc)], inc=True)
                pool.rel(tiA, ("S", vm))
            p2_done = ("S", p.c["S"])
            for do in range(CH):
                tile, ti, pw = pool.acq()
                for p2 in range(4):
                    p.t(lambda e, tile=tile, do=do, p2=p2: e.matmul(
                        tile[:, :], f2V[:, D * p2 + 128 * do:D * p2 + 128 * (do + 1)],
                        P2t[:, TQ * p2:TQ * (p2 + 1)],
                        start=(p2 == 0), stop=(p2 == 3)),
                        wait=([p2_done, ("DF2V", d_f2V)] if p2 == 0 else []) +
                             ([pw] if (pw and p2 == 0) else []),
                        inc=(p2 == 3))
                tmm = p.c["T"]
                vm = p.v(lambda e, tile=tile, do=do, jt=jt: e.scalar_tensor_tensor(
                    sbH2[:, T * do + TQ * jt:T * do + TQ * jt + TQ], tile[:, :],
                    scol("fc2b", do), sbH2[:, T * do + TQ * jt:T * do + TQ * jt + TQ],
                    ALU.add, ALU.add), wait=[("T", tmm)], inc=True)
                pool.rel(ti, ("V", vm))
                mlp_chunk_marks[do] = ("V", vm)
            x2_free[slot] = ("T", t_acc)
            p1_free = ("T", t_acc)
            p2_free = ("V", p.c["V"])
        mark_mlp = ("V", p.c["V"])
        emit_dump("mlp", sbH2, [mark_mlp])
        p.barrier("tvs")
        p.flush()
        pm.close()
        if stop_after == "mlp":
            finalize_early(sbH2)
            return nc

        esS2 = ExitStack()
        sbS2 = esS2.enter_context(nc.sbuf_tensor("S2", [128, CH * T], F32, side="right"))
        s2v = sbS2[:, :].rearrange("p (c t) -> p c t", t=T)
        sp2v = spill_s.rearrange("p (c t) -> p c t", t=T)
        gate_s2 = [("T", p.c["T"]), ("V", p.c["V"]), ("S", p.c["S"])]
        d_rel = []
        sems_rel = ["DSS", "DS2"]
        for hh in range(2):
            d_rel.append(p.y(lambda e, hh=hh: e.dma_start(
                s2v[:, 4 * hh:4 * (hh + 1), :], sp2v[:, 4 * hh:4 * (hh + 1), :]),
                wait=gate_s2, sem=sems_rel[hh]))
        add_marks_ch = []
        for c in range(CH):
            vm = p.v(lambda e, c=c: e.tensor_tensor(
                sbS2[:, T * c:T * (c + 1)], sbS2[:, T * c:T * (c + 1)],
                sbH2[:, T * c:T * (c + 1)], ALU.add),
                wait=[(sems_rel[c // 4], d_rel[c // 4]), mlp_chunk_marks[c]], inc=True)
            add_marks_ch.append(("V", vm))
        p.flush()
        esH2.close()

        ffn2 = ExitStack()
        XRb2 = ffn2.enter_context(nc.sbuf_tensor("XR2", [128, CH * TH], BF16, side="right"))
        XKb2 = ffn2.enter_context(nc.sbuf_tensor("XK2", [128, CH * TH], BF16, side="right"))
        KAb2 = ffn2.enter_context(nc.sbuf_tensor("KA2", [128, CH * TH], BF16, side="right"))
        wrS2 = ffn2.enter_context(nc.sbuf_tensor("wrS2", [128, CH * D], BF16, side="right"))
        wkS2 = ffn2.enter_context(nc.sbuf_tensor("wkS2", [128, CH * D], BF16, side="right"))
        wvS2 = ffn2.enter_context(nc.sbuf_tensor("wvS2", [128, CH * D], BF16, side="right"))
        rt2 = ffn2.enter_context(nc.sbuf_tensor("rt2", [128, TQ], BF16, side="right"))
        krl2 = ffn2.enter_context(nc.sbuf_tensor("krl2", [128, TQ], BF16, side="right"))
        t1f2 = ffn2.enter_context(nc.sbuf_tensor("t1f2", [128, TQ], F32, side="right"))
        dwk2 = p.g(lambda e: e.dma_start(wkS2[:, :], dram["wkT"]), sem="DWK")
        dwv2 = p.g(lambda e: e.dma_start(wvS2[:, :], dram["wvT"]), sem="DWV")
        dwr2 = p.g(lambda e: e.dma_start(wrS2[:, :], dram["wrT"]), sem="DWR")
        wm2 = {"wk": ("DWK", dwk2), "wv": ("DWV", dwv2), "wr": ("DWR", dwr2)}
        mark_ffn2, ffn2_chunk_marks = emit_ffn(sbS2, XRb2, XKb2, KAb2, wrS2, wkS2,
                                               wvS2, rt2, t1f2, krl2, wm2, [],
                                               add_marks_ch)
        emit_dump("out_fm", sbS2, [mark_ffn2])
        p.barrier("tvs")
        p.flush()
        ffn2.close()
        if stop_after == "ffn2":
            finalize_early(sbS2)
            return nc

        xv_out = x_out.rearrange("p (c t) -> p c t", t=T)
        sv_out = sbS2[:, :].rearrange("p (c t) -> p c t", t=T)
        for hh in range(2):
            w = [ffn2_chunk_marks[c] for c in range(4 * hh, 4 * hh + 4)]
            p.y(lambda e, hh=hh: e.dma_start(
                xv_out[:, 4 * hh:4 * (hh + 1), :], sv_out[:, 4 * hh:4 * (hh + 1), :]),
                wait=w, sem=["DO0", "DO1"][hh])
        p._mk("y", lambda eng: eng.nop(),
              [(sn, p.c[sn]) for sn in DMA_SEMS if p.c[sn] > 0], [])
        p.flush()
        ps8b.close()
        esS2.close()

    return nc



def kernel(**inputs):
    host = prep_host(inputs)
    nc = build_program(host)
    x = _f32(inputs["L0_attn_hidden_states"])
    in_maps = []
    for c in range(8):
        m = {k: host[k] for k in WTENSORS}
        m["x_in"] = feat_chunks(np.ascontiguousarray(x[c].T))
        in_maps.append(m)
    res = run_bass_kernel_spmd(nc, in_maps, core_ids=list(range(8)))
    outs = []
    for c in range(8):
        fm = np.asarray(res.results[c]["x_out"])
        td = fm.reshape(128, CH, T).transpose(1, 0, 2).reshape(D, T).T
        outs.append(np.ascontiguousarray(td))
    return np.stack(outs, axis=0)


# version 34
# speedup vs baseline: 1.0394x; 1.0134x over previous
import numpy as np
from contextlib import ExitStack

import concourse.bass as bass
import concourse.mybir as mybir
from concourse.bass_utils import run_bass_kernel_spmd

F32 = mybir.dt.float32
BF16 = mybir.dt.bfloat16
AF = mybir.ActivationFunctionType
ALU = mybir.AluOpType

D = 1024
T = 2048
CH = D // 128
NH = 16
DH = 64
RQ = 48
TQ = 512
NTQ = T // TQ
TH = 1024
EPS = 1e-5
INVD = 1.0 / D
VAW = NH * (DH + 1)

CONST_VALS = [0.0, 1.0, -1.0, 0.5, -0.5, 0.125, INVD, EPS]

WTENSORS = ["wrT", "wkT", "wvT", "Upm", "V2",
            "outU", "outV", "fc1U", "fc1V", "fc2U", "fc2V",
            "smalls", "rows", "bconsts", "ident"]


def _f32(a):
    a = np.asarray(a)
    assert a.dtype == np.float32, a.dtype
    return a

def feat_chunks(A):
    A = np.ascontiguousarray(_f32(A))
    C, X = A.shape
    return np.ascontiguousarray(A.reshape(C // 128, 128, X).transpose(1, 0, 2).reshape(128, -1))

def vec_cols(v):
    v = _f32(v).reshape(-1)
    return np.ascontiguousarray(v.reshape(-1, 128).T)

def pack_U(U):
    U = _f32(U)
    Up = np.zeros((D, 1024), np.float32)
    for h in range(NH):
        c0 = 128 * (h // 2) + 64 * (h % 2)
        Up[:, c0:c0 + RQ] = U[:, h, :]
    return feat_chunks(Up)

def pack_V2(Vw):
    Vw = _f32(Vw)
    P = np.zeros((128, 1024), np.float32)
    for h in range(NH):
        P[64 * (h % 2):64 * (h % 2) + RQ, DH * h:DH * h + DH] = Vw[h]
    return P

def build_masks():
    q = np.arange(TQ)[None, :]
    k = np.arange(128)[:, None]
    return np.concatenate([(q >= 128 * d + k).astype(np.float32) for d in range(4)], axis=1)


def prep_host(inputs):
    g = {}
    g["wrT"] = feat_chunks(np.ascontiguousarray(_f32(inputs["L0_ffn_W_r"]).T))
    g["wkT"] = feat_chunks(np.ascontiguousarray(_f32(inputs["L0_ffn_W_k"]).T))
    g["wvT"] = feat_chunks(np.ascontiguousarray(_f32(inputs["L0_ffn_W_v"]).T))
    qU = pack_U(inputs["L0_attn_q_U"])
    kU = pack_U(inputs["L0_attn_k_U"])
    vU = pack_U(inputs["L0_attn_v_U"])
    Upm = np.zeros((128, 8 * 3072), np.float32)
    for m in range(8):
        for pi, Uf in enumerate((qU, kU, vU)):
            for c in range(8):
                Upm[:, 3072 * m + 1024 * pi + 128 * c:3072 * m + 1024 * pi + 128 * (c + 1)] = \
                    Uf[:, 1024 * c + 128 * m:1024 * c + 128 * (m + 1)]
    g["Upm"] = np.ascontiguousarray(Upm)
    g["V2"] = np.ascontiguousarray(np.concatenate(
        [pack_V2(inputs["L0_attn_q_V"]), pack_V2(inputs["L0_attn_k_V"]),
         pack_V2(inputs["L0_attn_v_V"])], axis=1))
    g["outU"] = feat_chunks(inputs["L0_attn_out_U"])
    g["outV"] = feat_chunks(inputs["L0_attn_out_V"])
    g["fc1U"] = feat_chunks(inputs["L0_attn_fc1_U"])
    g["fc1V"] = feat_chunks(inputs["L0_attn_fc1_V"])
    g["fc2U"] = feat_chunks(inputs["L0_attn_fc2_U"])
    g["fc2V"] = feat_chunks(inputs["L0_attn_fc2_V"])

    cols, reg = [], {}
    def add(name, arr):
        arr = _f32(arr)
        if arr.ndim == 1:
            arr = arr[:, None]
        reg[name] = (sum(c.shape[1] for c in cols), arr.shape[1])
        cols.append(arr)
    add("consts", np.broadcast_to(np.array(CONST_VALS, np.float32),
                                  (128, len(CONST_VALS))).copy())
    add("qb", vec_cols(_f32(inputs["L0_attn_q_b"]).reshape(-1)))
    add("kb", vec_cols(_f32(inputs["L0_attn_k_b"]).reshape(-1)))
    add("outb", vec_cols(inputs["L0_attn_out_b"]))
    add("fc1b", vec_cols(inputs["L0_attn_fc1_b"]))
    add("fc2b", vec_cols(inputs["L0_attn_fc2_b"]))
    add("mur", vec_cols(inputs["L0_ffn_mu_r"]))
    add("muk", vec_cols(inputs["L0_ffn_mu_k"]))
    add("g1", vec_cols(inputs["L0_attn_ln1_g"]))
    add("b1", vec_cols(inputs["L0_attn_ln1_b"]))
    add("g2", vec_cols(inputs["L0_attn_ln2_g"]))
    add("b2", vec_cols(inputs["L0_attn_ln2_b"]))
    g["smalls"] = np.ascontiguousarray(np.concatenate(cols, axis=1))
    g["smalls_reg"] = reg

    g["rows"] = np.ones((1, 128), np.float32)

    bcols, breg = [], {}
    def addb(name, arr):
        arr = _f32(arr)
        breg[name] = (sum(c.shape[1] for c in bcols), arr.shape[1])
        bcols.append(arr)
    addb("masks", build_masks())
    addb("ones", np.ones((128, 64), np.float32))
    addb("vb_bf", vec_cols(_f32(inputs["L0_attn_v_b"]).reshape(-1)))
    addb("b2_bf", vec_cols(inputs["L0_attn_ln2_b"]))
    g["bconsts"] = np.ascontiguousarray(np.concatenate(bcols, axis=1))
    g["bconsts_reg"] = breg

    g["ident"] = np.ascontiguousarray(np.eye(128, dtype=np.float32))
    return g



DMA_SEMS = ["DC", "DBC", "DI0", "DI1", "DSS", "DS2", "DSH", "DCT0", "DCT1",
            "DO0", "DO1", "DD", "DGD", "DWK", "DWV", "DWR", "DV2", "DWU",
            "DWU2", "DOU", "DOV", "DF1U", "DF1V", "DF2U", "DF2V", "DACC0", "DACC1"]


class Prog:
    SEM_OF = {"t": "T", "v": "V", "s": "S"}

    def __init__(self, nc):
        self.nc = nc
        self.q = {k: [] for k in "tvsyg"}
        self.c = {sn: 0 for sn in ["T", "V", "S"] + DMA_SEMS}
        self.sems = None

    def _mk(self, e, fn, wait, incs):
        wait = tuple(w for w in wait if w is not None and w[1] > 0)
        def run(eng, fn=fn, wait=wait, incs=tuple(incs)):
            for sn, v in wait:
                eng.wait_ge(self.sems[sn], v)
            inst = fn(eng)
            for sn, amt in incs:
                inst = inst.then_inc(self.sems[sn], amt)
        self.q[e].append(run)

    def op(self, e, fn, wait=(), inc=False):
        sn = self.SEM_OF[e]
        if e in "vs":
            wait = list(wait) + [(sn, self.c[sn])]
            inc = True
        elif e == "t" and inc:
            wait = list(wait) + [(sn, self.c[sn])]
        incs = [(sn, 1)] if inc else []
        self._mk(e, fn, wait, incs)
        if inc:
            self.c[sn] += 1
            return self.c[sn]
        return None

    def t(self, fn, wait=(), inc=False): return self.op("t", fn, wait, inc)
    def v(self, fn, wait=(), inc=False): return self.op("v", fn, wait, inc)
    def s(self, fn, wait=(), inc=False): return self.op("s", fn, wait, inc)

    def dma(self, e, sem, fn, wait=()):
        self._mk(e, fn, wait, [(sem, 16)])
        self.c[sem] += 16
        return self.c[sem]

    def y(self, fn, wait=(), sem="DD"):  return self.dma("y", sem, fn, wait)
    def g(self, fn, wait=(), sem="DGD"): return self.dma("g", sem, fn, wait)

    def barrier(self, engines="tvs"):
        engines = [e for e in engines if e in "tvs"]
        marks = {}
        for e in engines:
            marks[e] = (self.SEM_OF[e], self.op(e, lambda eng: eng.nop(), inc=True))
        for e in engines:
            waits = [marks[o] for o in engines if o != e]
            self._mk(e, lambda eng: eng.nop(), waits, [])

    def flush(self):
        with self.nc.Block() as block:
            for e, sect in (("t", block.tensor), ("v", block.vector),
                            ("s", block.scalar), ("y", block.sync),
                            ("g", block.gpsimd)):
                ops = self.q[e]
                if ops:
                    sect(lambda eng, ops=tuple(ops): [f(eng) for f in ops])
                self.q[e] = []


class Pool:
    def __init__(self, p, tiles):
        self.p = p
        self.tiles = tiles
        self.st = [None] * len(tiles)
        self.rr = 0

    def acq(self):
        n = len(self.tiles)
        for k in range(n):
            i = (self.rr + k) % n
            if self.st[i] != "held":
                break
        else:
            raise RuntimeError("psum pool exhausted")
        self.rr = i + 1
        w = self.st[i]
        self.st[i] = "held"
        return self.tiles[i], i, w

    def rel(self, i, wait_tuple):
        assert self.st[i] == "held"
        self.st[i] = wait_tuple



def build_program(host, dumps=(), stop_after=None):
    nc = bass.Bass("TRN2", target_bir_lowering=False)
    p = Prog(nc)

    x_in = nc.dram_tensor("x_in", [128, CH * T], F32, kind="ExternalInput").ap()
    x_out = nc.dram_tensor("x_out", [128, CH * T], F32, kind="ExternalOutput").ap()
    dram = {}
    for name in WTENSORS:
        arr = host[name]
        dram[name] = nc.dram_tensor(name, list(arr.shape), F32, kind="ExternalInput").ap()
    spill_s = nc.dram_tensor("spill_s", [128, CH * T], F32, kind="Internal").ap()
    spill_h = nc.dram_tensor("spill_h", [128, CH * T], F32, kind="Internal").ap()
    dump_aps = {}
    for dname, ncols in dumps:
        dump_aps[dname] = nc.dram_tensor("dump_" + dname, [128, ncols], F32,
                                         kind="ExternalOutput").ap()
    dump_names = set(dump_aps)

    sreg = host["smalls_reg"]
    breg = host["bconsts_reg"]

    with ExitStack() as es:
        p.sems = {sn: es.enter_context(nc.semaphore(sn))
                  for sn in ["T", "V", "S"] + DMA_SEMS}

        csts = es.enter_context(nc.sbuf_tensor("csts", [128, host["smalls"].shape[1]], F32, side="left"))
        rows = es.enter_context(nc.sbuf_tensor("rows_sb", [1, 128], F32, side="left"))
        bcst = es.enter_context(nc.sbuf_tensor("bcst", [128, host["bconsts"].shape[1]], BF16, side="left"))
        ident = es.enter_context(nc.sbuf_tensor("ident_sb", [128, 128], F32, side="left"))
        statA = es.enter_context(nc.sbuf_tensor("statA", [1, T], F32, side="left"))
        bA = es.enter_context(nc.sbuf_tensor("bA", [128, T], F32, side="left"))
        bB = es.enter_context(nc.sbuf_tensor("bB", [128, T], F32, side="left"))
        bnd = es.enter_context(nc.sbuf_tensor("bnd", [128, CH], F32, side="left"))

        def scol(name, j=0):
            off, n = sreg[name]
            assert j < n, (name, j, n)
            return csts[:, off + j:off + j + 1]
        CONST_COL = {v: i for i, v in enumerate(CONST_VALS)}
        coff = sreg["consts"][0]
        def ccol(v):
            return csts[:, coff + CONST_COL[v]:coff + CONST_COL[v] + 1]
        for v in CONST_VALS:
            nc.const_aps.register(F32, float(v), ccol(v))
        def bsl(name, j0, n, p0=0, pn=128):
            off, _ = breg[name]
            return bcst[p0:p0 + pn, off + j0:off + j0 + n]

        p.y(lambda e: e.dma_start(csts[:, :], dram["smalls"]), sem="DC")
        p.y(lambda e: e.dma_start(rows[:, :], dram["rows"]), sem="DC")
        d_id = p.y(lambda e: e.dma_start(ident[:, :], dram["ident"]), sem="DC")
        d_bc = p.g(lambda e: e.dma_start(bcst[:, :], dram["bconsts"]), sem="DBC")
        W_INIT = [("DC", d_id), ("DBC", d_bc)]

        def finalize_early(buf):
            dst = x_out
            p.g(lambda e, buf=buf, dst=dst: e.dma_start(dst, buf[:, :]),
                wait=[("V", p.c["V"]), ("S", p.c["S"]), ("T", p.c["T"])], sem="DGD")
            p._mk("y", lambda eng: eng.nop(),
                  [(sn, p.c[sn]) for sn in DMA_SEMS if p.c[sn] > 0], [])
            p.flush()

        def emit_dump(name, buf, wait, bf16=False):
            if name not in dump_names:
                return
            if bf16:
                dm = p.g(lambda e, name=name, buf=buf: e.dma_start(dump_aps[name], buf[:, :]),
                         wait=list(wait), sem="DGD")
                gate = ("DGD", dm)
            else:
                dm = p.y(lambda e, name=name, buf=buf: e.dma_start(dump_aps[name], buf[:, :]),
                         wait=list(wait), sem="DD")
                gate = ("DD", dm)
            p.v(lambda e: e.nop(), wait=[gate])
            p.s(lambda e: e.nop(), wait=[gate])


        def emit_ln(pool, X, scratch, pre_w=(), pre_w_chunk=None, scratch2=None):
            inv_tile = [None] * NTQ
            neg_tile = [None] * NTQ
            half_T_end = [0, 0]
            scr_of = [scratch, scratch2 if scratch2 is not None else scratch]
            for half in range(2):
                scratch = scr_of[half]
                cast_marks = []
                for c in range(CH):
                    w = list(pre_w)
                    if pre_w_chunk is not None:
                        w.append(pre_w_chunk[c])
                    if half > 0 and scratch2 is None:
                        w.append(("T", half_T_end[0]))
                    if c % 2 == 0:
                        cm = ("S", p.s(lambda e, c=c, half=half, X=X, scratch=scratch:
                                       e.activation(
                            scratch[:, TH * c:TH * (c + 1)],
                            X[:, T * c + TH * half: T * c + TH * half + TH],
                            AF.Identity), wait=w, inc=True))
                    else:
                        cm = ("V", p.v(lambda e, c=c, half=half, X=X, scratch=scratch:
                                       e.tensor_copy(
                            scratch[:, TH * c:TH * (c + 1)],
                            X[:, T * c + TH * half: T * c + TH * half + TH]),
                            wait=w, inc=True))
                    cast_marks.append(cm)
                t1a, i1a, pw1a = pool.acq()
                t1b, i1b, pw1b = pool.acq()
                s1tiles = [t1a, t1b]
                c_done = []
                for c in range(CH):
                    for tt in range(2):
                        p.t(lambda e, tile=s1tiles[tt], c=c, tt=tt, scratch=scratch: e.matmul(
                            tile[0:1, :], bsl("ones", 0, 1),
                            scratch[:, TH * c + TQ * tt: TH * c + TQ * tt + TQ],
                            start=(c == 0), stop=(c == CH - 1)),
                            wait=[cast_marks[c]] +
                                 ([pw1a] if (pw1a and c == 0 and tt == 0) else []) +
                                 ([pw1b] if (pw1b and c == 0 and tt == 1) else []),
                            inc=(tt == 1))
                    c_done.append(p.c["T"])
                sq_marks = []
                for c in range(CH):
                    vm = p.v(lambda e, c=c, scratch=scratch: e.tensor_tensor(
                        scratch[:, TH * c:TH * (c + 1)], scratch[:, TH * c:TH * (c + 1)],
                        scratch[:, TH * c:TH * (c + 1)], ALU.mult),
                        wait=[("T", c_done[c])], inc=True)
                    sq_marks.append(vm)
                t2a, i2a, pw2a = pool.acq()
                t2b, i2b, pw2b = pool.acq()
                s2tiles = [t2a, t2b]
                for c in range(CH):
                    for tt in range(2):
                        p.t(lambda e, tile=s2tiles[tt], c=c, tt=tt, scratch=scratch: e.matmul(
                            tile[0:1, :], bsl("ones", 0, 1),
                            scratch[:, TH * c + TQ * tt: TH * c + TQ * tt + TQ],
                            start=(c == 0), stop=(c == CH - 1)),
                            wait=[("V", sq_marks[c])] +
                                 ([pw2a] if (pw2a and c == 0 and tt == 0) else []) +
                                 ([pw2b] if (pw2b and c == 0 and tt == 1) else []),
                            inc=(tt == 1))
                half_T_end[half] = p.c["T"]
                mm_done = ("T", p.c["T"])
                for tt in range(2):
                    j = 2 * half + tt
                    sl = slice(TQ * j, TQ * (j + 1))
                    p.v(lambda e, tile=s1tiles[tt], sl=sl: e.tensor_scalar(
                        statA[0:1, sl], tile[0:1, :], float(INVD), None, ALU.mult),
                        wait=[mm_done], inc=True)
                    p.v(lambda e, tile=s2tiles[tt], sl=sl: e.tensor_scalar(
                        bB[0:1, sl], tile[0:1, :], float(INVD), None, ALU.mult), inc=True)
                    vrel = p.c["V"]
                    p.v(lambda e, sl=sl: e.tensor_tensor(
                        bA[0:1, sl], statA[0:1, sl], statA[0:1, sl], ALU.mult),
                        wait=[("V", p.c["V"])], inc=True)
                    vv = p.v(lambda e, sl=sl: e.tensor_tensor(
                        bB[0:1, sl], bB[0:1, sl], bA[0:1, sl], ALU.subtract),
                        wait=[("V", p.c["V"])], inc=True)
                    p.s(lambda e, sl=sl: e.activation(bB[0:1, sl], bB[0:1, sl], AF.Ln,
                                                      bias=ccol(EPS)[0:1, :], scale=1.0),
                        wait=[("V", vv)], inc=True)
                    sv = p.s(lambda e, sl=sl: e.activation(bB[0:1, sl], bB[0:1, sl],
                                                           AF.Exp, scale=-0.5), inc=True)
                    inv_tile[j] = ("S", sv)
                    nm = p.v(lambda e, sl=sl: e.scalar_tensor_tensor(
                        statA[0:1, sl], statA[0:1, sl], -1.0, bB[0:1, sl],
                        ALU.mult, ALU.mult), wait=[inv_tile[j]], inc=True)
                    neg_tile[j] = ("V", nm)
                    pool.rel(i1a if tt == 0 else i1b, ("V", vrel))
                    pool.rel(i2a if tt == 0 else i2b, ("V", vrel))
            return {"inv": inv_tile[-1][1], "neg": neg_tile[-1][1],
                    "inv_tile": inv_tile, "neg_tile": neg_tile}

        def emit_outer_pair(pool, rowA, rowB, wA, wB, wA_tile=None, wB_tile=None):
            for j in range(NTQ):
                tile, ti, pw = pool.acq()
                tm = p.t(lambda e, tile=tile, j=j, rowA=rowA: e.matmul(
                    tile[:, :], rows[0:1, :], rowA[0:1, TQ * j:TQ * (j + 1)],
                    start=True, stop=True),
                    wait=[wA_tile[j] if wA_tile else wA] + ([pw] if pw else []), inc=True)
                vmA = p.v(lambda e, tile=tile, j=j: e.tensor_copy(
                    bA[:, TQ * j:TQ * (j + 1)], tile[:, :]), wait=[("T", tm)], inc=True)
                pool.rel(ti, ("V", vmA))
                tile2, ti2, pw2 = pool.acq()
                tm2 = p.t(lambda e, tile2=tile2, j=j, rowB=rowB: e.matmul(
                    tile2[:, :], rows[0:1, :], rowB[0:1, TQ * j:TQ * (j + 1)],
                    start=True, stop=True),
                    wait=[wB_tile[j] if wB_tile else wB, ("V", vmA)] +
                         ([pw2] if pw2 else []), inc=True)
                vmB = p.v(lambda e, tile2=tile2, j=j: e.tensor_copy(
                    bB[:, TQ * j:TQ * (j + 1)], tile2[:, :]), wait=[("T", tm2)], inc=True)
                pool.rel(ti2, ("V", vmB))
            return ("V", p.c["V"])

        def emit_ln_tiles(pool, X, scratch, pre_w=(), pre_w_chunk=None, scratch2=None):
            r = emit_ln(pool, X, scratch, pre_w, pre_w_chunk, scratch2)
            return emit_outer_pair(pool, bB, statA, None, None,
                                   wA_tile=r["inv_tile"], wB_tile=r["neg_tile"])

        def emit_apply(X, OUT, mark_b, affine=None, extra_w=()):
            marks = []
            for c in range(CH):
                p.v(lambda e, c=c, X=X, OUT=OUT: e.tensor_tensor(
                    OUT[:, T * c:T * (c + 1)], X[:, T * c:T * (c + 1)], bA[:, :], ALU.mult),
                    wait=[mark_b] + list(extra_w), inc=True)
                vm = p.v(lambda e, c=c, OUT=OUT: e.tensor_tensor(
                    OUT[:, T * c:T * (c + 1)], OUT[:, T * c:T * (c + 1)], bB[:, :], ALU.add),
                    wait=[("V", p.c["V"])], inc=True)
                if affine is not None:
                    gname, bname = affine
                    sm = p.s(lambda e, c=c, gname=gname, bname=bname, OUT=OUT: e.activation(
                        OUT[:, T * c:T * (c + 1)], OUT[:, T * c:T * (c + 1)], AF.Identity,
                        bias=scol(bname, c), scale=scol(gname, c)),
                        wait=[("V", vm)], inc=True)
                    marks.append(("S", sm))
                else:
                    marks.append(("V", vm))
            return marks

        esS = ExitStack()
        sbS = esS.enter_context(nc.sbuf_tensor("S", [128, CH * T], F32, side="right"))

        ps8 = ExitStack()
        pool_tiles = [ps8.enter_context(nc.psum_tensor("pp%d" % i, [128, TQ], F32))
                      for i in range(8)]
        pool = Pool(p, pool_tiles)

        sv_in = sbS[:, :].rearrange("p (c t) -> p c t", t=T)
        xv_in = x_in.rearrange("p (c t) -> p c t", t=T)
        d_in = []
        for hh in range(2):
            d_in.append(p.y(lambda e, hh=hh: e.dma_start(
                sv_in[:, 4 * hh:4 * (hh + 1), :], xv_in[:, 4 * hh:4 * (hh + 1), :]),
                sem=["DI0", "DI1"][hh]))
        in_chunk_marks = [(["DI0", "DI1"][c // 4], d_in[c // 4]) for c in range(CH)]
        emit_dump("p0", sbS, [in_chunk_marks[-1]] + W_INIT)
        p.flush()

        def emit_ffn(X, XR, XK, KA, wr, wk, wv, rt, t1f, krl, wmarks, pre_w,
                     pre_w_chunk=None):
            mark_b = emit_ln_tiles(pool, X, XR, pre_w, pre_w_chunk, scratch2=XK)
            QTR = CH * TQ
            slot_free = [("T", 0), ("T", 0)]
            add_chunk_marks = {}

            def emit_mix(q):
                s0 = QTR * (q % 2)
                t0 = TQ * q
                marks = []
                for c in range(CH):
                    w = [mark_b, slot_free[q % 2]] + list(pre_w)
                    if q > 0:
                        p.v(lambda e, c=c, t0=t0: e.tensor_tensor(
                            bnd[:, c:c + 1], X[:, T * c + t0 - 1:T * c + t0],
                            bA[:, t0 - 1:t0], ALU.mult), wait=w, inc=True)
                        p.v(lambda e, c=c, t0=t0: e.tensor_tensor(
                            bnd[:, c:c + 1], bnd[:, c:c + 1], bB[:, t0 - 1:t0],
                            ALU.add), inc=True)
                        w = []
                    p.v(lambda e, c=c, t0=t0, s0=s0: e.tensor_tensor(
                        KA[:, s0 + TQ * c:s0 + TQ * (c + 1)],
                        X[:, T * c + t0:T * c + t0 + TQ],
                        bA[:, t0:t0 + TQ], ALU.mult), wait=w, inc=True)
                    p.v(lambda e, c=c, s0=s0: e.tensor_tensor(
                        KA[:, s0 + TQ * c:s0 + TQ * (c + 1)],
                        KA[:, s0 + TQ * c:s0 + TQ * (c + 1)],
                        bB[:, TQ * q:TQ * q + TQ], ALU.add), inc=True)
                    p.v(lambda e, c=c, s0=s0: e.tensor_tensor(
                        XK[:, s0 + TQ * c + 1:s0 + TQ * c + TQ],
                        KA[:, s0 + TQ * c + 1:s0 + TQ * c + TQ],
                        KA[:, s0 + TQ * c:s0 + TQ * c + TQ - 1], ALU.subtract), inc=True)
                    if q == 0:
                        p.v(lambda e, c=c, s0=s0: e.tensor_scalar(
                            XR[:, s0 + TQ * c:s0 + TQ * c + 1],
                            KA[:, s0 + TQ * c:s0 + TQ * c + 1],
                            scol("mur", c), None, ALU.mult), inc=True)
                        p.v(lambda e, c=c, s0=s0: e.tensor_scalar(
                            XK[:, s0 + TQ * c:s0 + TQ * c + 1],
                            KA[:, s0 + TQ * c:s0 + TQ * c + 1],
                            scol("muk", c), None, ALU.mult), inc=True)
                    else:
                        p.v(lambda e, c=c, s0=s0: e.tensor_tensor(
                            XR[:, s0 + TQ * c:s0 + TQ * c + 1],
                            KA[:, s0 + TQ * c:s0 + TQ * c + 1],
                            bnd[:, c:c + 1], ALU.subtract), inc=True)
                        p.v(lambda e, c=c, s0=s0: e.scalar_tensor_tensor(
                            XK[:, s0 + TQ * c:s0 + TQ * c + 1],
                            XR[:, s0 + TQ * c:s0 + TQ * c + 1],
                            scol("muk", c), bnd[:, c:c + 1], ALU.mult, ALU.add), inc=True)
                        p.v(lambda e, c=c, s0=s0: e.scalar_tensor_tensor(
                            XR[:, s0 + TQ * c:s0 + TQ * c + 1],
                            XR[:, s0 + TQ * c:s0 + TQ * c + 1],
                            scol("mur", c), bnd[:, c:c + 1], ALU.mult, ALU.add), inc=True)
                    p.v(lambda e, c=c, s0=s0: e.scalar_tensor_tensor(
                        XR[:, s0 + TQ * c + 1:s0 + TQ * c + TQ],
                        XK[:, s0 + TQ * c + 1:s0 + TQ * c + TQ],
                        scol("mur", c), KA[:, s0 + TQ * c:s0 + TQ * c + TQ - 1],
                        ALU.mult, ALU.add), inc=True)
                    vm = p.v(lambda e, c=c, s0=s0: e.scalar_tensor_tensor(
                        XK[:, s0 + TQ * c + 1:s0 + TQ * c + TQ],
                        XK[:, s0 + TQ * c + 1:s0 + TQ * c + TQ],
                        scol("muk", c), KA[:, s0 + TQ * c:s0 + TQ * c + TQ - 1],
                        ALU.mult, ALU.add), inc=True)
                    marks.append(vm)
                return ("V", marks[-1])

            mix_marks = {0: emit_mix(0)}
            for q in range(4):
                s0 = QTR * (q % 2)
                t0 = TQ * q
                mix_done = mix_marks[q]
                relu_last = 0
                for dk in range(CH):
                    tile, ti, pw = pool.acq()
                    for c in range(CH):
                        p.t(lambda e, tile=tile, dk=dk, c=c, s0=s0: e.matmul(
                            tile[:, :], wk[:, D * c + 128 * dk:D * c + 128 * (dk + 1)],
                            XK[:, s0 + TQ * c:s0 + TQ * (c + 1)],
                            start=(c == 0), stop=(c == CH - 1)),
                            wait=([mix_done, wmarks["wk"]] if c == 0 else []) +
                                 ([pw] if (pw and c == 0) else []),
                            inc=(c == CH - 1))
                    tmm = p.c["T"]
                    p.s(lambda e, tile=tile, krl=krl: e.activation(
                        krl[:, :], tile[:, :], AF.Relu), wait=[("T", tmm)], inc=True)
                    relu_last = p.s(lambda e, dk=dk, s0=s0, krl=krl: e.activation(
                        KA[:, s0 + TQ * dk:s0 + TQ * (dk + 1)], krl[:, :],
                        AF.Square), inc=True)
                    pool.rel(ti, ("S", relu_last))
                ka_done = ("S", relu_last)
                if q + 1 < 4:
                    mix_marks[q + 1] = emit_mix(q + 1)
                for dv in range(CH):
                    tileKV, tiKV, pwKV = pool.acq()
                    for c in range(CH):
                        p.t(lambda e, tile=tileKV, dv=dv, c=c, s0=s0: e.matmul(
                            tile[:, :], wv[:, D * c + 128 * dv:D * c + 128 * (dv + 1)],
                            KA[:, s0 + TQ * c:s0 + TQ * (c + 1)],
                            start=(c == 0), stop=(c == CH - 1)),
                            wait=([ka_done, wmarks["wv"]] if c == 0 else []) +
                                 ([pwKV] if (pwKV and c == 0) else []),
                            inc=(c == CH - 1))
                    tileR, tiR, pwR = pool.acq()
                    for c in range(CH):
                        p.t(lambda e, tile=tileR, dv=dv, c=c, s0=s0: e.matmul(
                            tile[:, :], wr[:, D * c + 128 * dv:D * c + 128 * (dv + 1)],
                            XR[:, s0 + TQ * c:s0 + TQ * (c + 1)],
                            start=(c == 0), stop=(c == CH - 1)),
                            wait=([wmarks["wr"]] if c == 0 else []) +
                                 ([pwR] if (pwR and c == 0) else []),
                            inc=(c == CH - 1))
                    tmm = p.c["T"]
                    sm = p.s(lambda e, tile=tileR, rt=rt: e.activation(
                        rt[:, :], tile[:, :], AF.Sigmoid),
                        wait=[("T", tmm), ("V", p.c["V"])], inc=True)
                    pool.rel(tiR, ("S", sm))
                    p.v(lambda e, tile=tileKV, rt=rt, t1f=t1f: e.scalar_tensor_tensor(
                        t1f[:, :], tile[:, :], 0.5, rt[:, :], ALU.mult, ALU.mult),
                        wait=[("S", sm)], inc=True)
                    vm = p.v(lambda e, dv=dv, t0=t0, X=X, t1f=t1f: e.tensor_tensor(
                        X[:, T * dv + t0:T * dv + t0 + TQ],
                        X[:, T * dv + t0:T * dv + t0 + TQ],
                        t1f[:, :], ALU.add), inc=True)
                    pool.rel(tiKV, ("V", vm))
                    add_chunk_marks[dv] = ("V", vm)
                slot_free[q % 2] = ("T", p.c["T"])
            last = add_chunk_marks[CH - 1]
            return last, [add_chunk_marks[c] for c in range(CH)]

        ffn1 = ExitStack()
        XRb = ffn1.enter_context(nc.sbuf_tensor("XR1", [128, CH * TH], BF16, side="right"))
        XKb = ffn1.enter_context(nc.sbuf_tensor("XK1", [128, CH * TH], BF16, side="right"))
        KAb = ffn1.enter_context(nc.sbuf_tensor("KA1", [128, CH * TH], BF16, side="right"))
        wrS = ffn1.enter_context(nc.sbuf_tensor("wrS1", [128, CH * D], BF16, side="right"))
        wkS = ffn1.enter_context(nc.sbuf_tensor("wkS1", [128, CH * D], BF16, side="right"))
        wvS = ffn1.enter_context(nc.sbuf_tensor("wvS1", [128, CH * D], BF16, side="right"))
        rt1 = ffn1.enter_context(nc.sbuf_tensor("rt1", [128, TQ], BF16, side="right"))
        krl1 = ffn1.enter_context(nc.sbuf_tensor("krl1", [128, TQ], BF16, side="right"))
        t1f1 = ffn1.enter_context(nc.sbuf_tensor("t1f1", [128, TQ], F32, side="right"))
        dwk = p.g(lambda e: e.dma_start(wkS[:, :], dram["wkT"]), sem="DWK")
        dwv = p.g(lambda e: e.dma_start(wvS[:, :], dram["wvT"]), sem="DWV")
        dwr = p.g(lambda e: e.dma_start(wrS[:, :], dram["wrT"]), sem="DWR")
        wm1 = {"wk": ("DWK", dwk), "wv": ("DWV", dwv), "wr": ("DWR", dwr)}
        mark_ffn1, ffn1_chunk_marks = emit_ffn(sbS, XRb, XKb, KAb, wrS, wkS, wvS,
                                               rt1, t1f1, krl1, wm1, W_INIT,
                                               in_chunk_marks)
        ffn1_T_end = ("T", p.c["T"])
        d_spill_s = p.y(lambda e: e.dma_start(spill_s, sbS[:, :]), wait=[mark_ffn1],
                        sem="DSS")
        emit_dump("ffn1", sbS, [mark_ffn1])
        p.barrier("tvs")
        p.flush()
        ffn1.close()
        if stop_after == "ffn1":
            finalize_early(sbS)
            return nc

        esH = ExitStack()
        sbH = esH.enter_context(nc.sbuf_tensor("H", [128, CH * T], F32, side="left"))
        statB1 = esH.enter_context(nc.sbuf_tensor("statB1", [1, T], F32, side="left"))
        statB2 = esH.enter_context(nc.sbuf_tensor("statB2", [1, T], F32, side="left"))
        p2a = ExitStack()
        scr = p2a.enter_context(nc.sbuf_tensor("scr", [128, 2 * CH * TH], BF16, side="right"))

        scrA = scr[:, 0:CH * TH]
        scrB = scr[:, CH * TH:2 * CH * TH]
        mark_b0 = emit_ln_tiles(pool, sbS, scrA, [], ffn1_chunk_marks, scratch2=scrB)
        hs_marks = emit_apply(sbS, sbH, mark_b0)
        mark_hs = hs_marks[-1]
        emit_dump("hs", sbH, [mark_hs])
        sv1 = p.v(lambda e: e.tensor_copy(statB1[0:1, :], bA[0:1, :]), wait=[mark_b0], inc=True)
        sv2 = p.v(lambda e: e.tensor_copy(statB2[0:1, :], bB[0:1, :]), inc=True)
        spvh = spill_h.rearrange("p (c t) -> p c t", t=T)
        sbhv = sbH[:, :].rearrange("p (c t) -> p c t", t=T)
        d_spill_h1 = p.y(lambda e: e.dma_start(spvh[:, 0:4, :], sbhv[:, 0:4, :]),
                         wait=[hs_marks[3]], sem="DSH")
        d_spill_h2 = p.y(lambda e: e.dma_start(spvh[:, 4:8, :], sbhv[:, 4:8, :]),
                         wait=[mark_hs], sem="DS2")
        uq = p.v(lambda e: e.tensor_tensor(statA[0:1, :], statB1[0:1, :], statB1[0:1, :],
                                           ALU.mult), wait=[("V", sv2)], inc=True)
        p.s(lambda e: e.activation(statA[0:1, :], statA[0:1, :], AF.Identity,
                                   bias=ccol(1.0)[0:1, :], scale=-1.0),
            wait=[("V", uq)], inc=True)
        p.s(lambda e: e.activation(statA[0:1, :], statA[0:1, :], AF.Ln,
                                   bias=ccol(1.0)[0:1, :], scale=float(EPS)), inc=True)
        si1 = p.s(lambda e: e.activation(statA[0:1, :], statA[0:1, :], AF.Exp,
                                         scale=-0.5), inc=True)
        p.v(lambda e: e.tensor_tensor(statB1[0:1, :], statB1[0:1, :], statA[0:1, :],
                                      ALU.mult), wait=[("S", si1)], inc=True)
        cm2 = p.v(lambda e: e.tensor_tensor(statB2[0:1, :], statB2[0:1, :], statA[0:1, :],
                                            ALU.mult), wait=[("V", p.c["V"])], inc=True)
        mark_bx = emit_outer_pair(pool, statB1, statB2, ("V", cm2), ("V", cm2))
        p.flush()
        p2a.close()
        esH.close()
        if stop_after == "p2a":
            finalize_early(sbS)
            return nc

        esx1 = ExitStack()
        x1b = esx1.enter_context(nc.sbuf_tensor("x1", [128, CH * T], BF16, side="left"))
        x1_marks = emit_apply(sbS, x1b, mark_bx, affine=("g1", "b1"),
                              extra_w=[("DSH", d_spill_h1)])
        mark_x1 = x1_marks[-1]
        x1_chunk_marks = list(x1_marks)
        emit_dump("x1", x1b, [mark_x1], bf16=True)
        p.barrier("tvs")
        p.flush()
        esS.close()

        esQB = ExitStack()
        QB = esQB.enter_context(nc.sbuf_tensor("QB", [128, CH * T], BF16, side="right"))
        esKV = ExitStack()
        KB = esKV.enter_context(nc.sbuf_tensor("KB", [128, CH * T], BF16, side="right"))
        VA = esKV.enter_context(nc.sbuf_tensor("VA", [128, 16 * VAW], BF16, side="right"))
        pb = ExitStack()
        ATb = pb.enter_context(nc.sbuf_tensor("AT", [128, 8 * TQ], BF16, side="right"))
        CTb = pb.enter_context(nc.sbuf_tensor("CT", [64, 2 * TQ], BF16, side="right"))
        rlb = pb.enter_context(nc.sbuf_tensor("rl", [128, 2 * TQ], BF16, side="right"))
        brl = pb.enter_context(nc.sbuf_tensor("brl", [64, 2 * TQ], BF16, side="right"))
        lnt = pb.enter_context(nc.sbuf_tensor("lnt", [128, TQ], F32, side="right"))
        pa = ExitStack()
        Pq = pa.enter_context(nc.sbuf_tensor("Pq", [128, 3 * T], BF16, side="right"))
        Wu = pa.enter_context(nc.sbuf_tensor("Wu", [128, 2 * 3072], BF16, side="right"))
        V2s = pa.enter_context(nc.sbuf_tensor("V2s", [128, 3 * 1024], BF16, side="right"))

        ps8.close()
        psB = ExitStack()
        SD = [psB.enter_context(nc.psum_tensor("sd%d" % i, [128, 2 * TQ], F32)) for i in range(2)]
        pool4 = Pool(p, [psB.enter_context(nc.psum_tensor("pb%d" % i, [128, TQ], F32))
                         for i in range(4)])

        spill_gate = [("DSS", d_spill_s), ("DSH", d_spill_h1), ("DS2", d_spill_h2)]
        d_v2 = p.g(lambda e: e.dma_start(V2s[:, :], dram["V2"]),
                   wait=[ffn1_T_end, ("DSH", d_spill_h1), ("DS2", d_spill_h2)], sem="DV2")
        VA_view = VA[:, :].rearrange("p (tc h e) -> p tc h e", h=NH, e=DH + 1)
        ones_set = p.v(lambda e: e.memset(VA_view[:, :, :, DH:DH + 1], 1.0),
                       wait=spill_gate, inc=True)

        pq_free = [("T", 0), ("T", 0), ("T", 0)]
        wu_free = [("T", 0), ("T", 0)]
        qk_ready = {}
        va_ready = {}
        qb_of = {0: "qb", 1: "kb"}

        qkv_queue = []

        def drain_qkv(n=1):
            k = 0
            while qkv_queue and k < n:
                qkv_queue.pop(0)[1]()
                k += 1
            return k

        def drain_tag(tag):
            while qkv_queue and qkv_queue[0][0] <= tag:
                qkv_queue.pop(0)[1]()

        def emit_qkv_pair(m):
            wslot = m % 2
            woff = 3072 * wslot
            wsem = "DWU" if wslot == 0 else "DWU2"
            dwu = p.g(lambda e, m=m, woff=woff: e.dma_start(
                Wu[:, woff:woff + 3072], dram["Upm"][:, 3072 * m:3072 * (m + 1)]),
                wait=[wu_free[wslot]] + ([ffn1_T_end] if m < 2 else []), sem=wsem)
            wu_mark = (wsem, dwu)
            state = {"s1_done": {}}

            def mk_s1(tt, pi):
                def run(tt=tt, pi=pi, m=m, woff=woff, wu_mark=wu_mark):
                    tile, ti, pw = pool4.acq()
                    for c in range(CH):
                        p.t(lambda e, tile=tile, m=m, pi=pi, tt=tt, c=c, woff=woff: e.matmul(
                            tile[:, :],
                            Wu[:, woff + 1024 * pi + 128 * c:woff + 1024 * pi + 128 * (c + 1)],
                            x1b[:, T * c + TQ * tt:T * c + TQ * tt + TQ],
                            start=(c == 0), stop=(c == CH - 1)),
                            wait=([wu_mark, pq_free[pi]] if c == 0 else []) +
                                 [x1_chunk_marks[c]] +
                                 ([pw] if (pw and c == 0) else []),
                            inc=(c == CH - 1))
                    tmm = p.c["T"]
                    vm = p.v(lambda e, tile=tile, pi=pi, tt=tt: e.tensor_copy(
                        Pq[:, T * pi + TQ * tt:T * pi + TQ * tt + TQ], tile[:, :]),
                        wait=[("T", tmm)], inc=True)
                    pool4.rel(ti, ("V", vm))
                    if tt == NTQ - 1:
                        state["s1_done"][pi] = ("V", vm)
                return run

            def mk_s2qk(tt, pi):
                def run(tt=tt, pi=pi, m=m):
                    dstb = QB if pi == 0 else KB
                    tile, ti, pw = pool4.acq()
                    p.t(lambda e, tile=tile, pi=pi, m=m, tt=tt: e.matmul(
                        tile[:, :], V2s[:, 1024 * pi + 128 * m:1024 * pi + 128 * (m + 1)],
                        Pq[:, T * pi + TQ * tt:T * pi + TQ * tt + TQ],
                        start=True, stop=True),
                        wait=[state["s1_done"][pi], ("DV2", d_v2)] + ([pw] if pw else []),
                        inc=True)
                    tmm = p.c["T"]
                    sm = p.v(lambda e, tile=tile, dstb=dstb, m=m, tt=tt, pi=pi: e.tensor_scalar(
                        dstb[:, T * m + TQ * tt:T * m + TQ * tt + TQ], tile[:, :],
                        scol(qb_of[pi], m), None, ALU.add),
                        wait=[("T", tmm)] +
                             (spill_gate if (m == 0 and pi == 0 and tt == 0) else []),
                        inc=True)
                    pool4.rel(ti, ("V", sm))
                    if tt == NTQ - 1:
                        pq_free[pi] = ("T", p.c["T"])
                        if pi == 1:
                            qk_ready[m] = ("V", p.c["V"])
                return run

            def mk_s2v(tg):
                def run(tg=tg, m=m, wslot=wslot):
                    tile, ti, pw = pool4.acq()
                    for k in range(4):
                        tc = tg * 4 + k
                        p.t(lambda e, tile=tile, m=m, tc=tc, k=k: e.matmul(
                            tile[:, 128 * k:128 * (k + 1)],
                            Pq[:, 2 * T + 128 * tc:2 * T + 128 * (tc + 1)],
                            V2s[:, 2048 + 128 * m:2048 + 128 * (m + 1)],
                            start=True, stop=True),
                            wait=([state["s1_done"][2], ("V", ones_set), ("DV2", d_v2)] if k == 0 else []) +
                                 ([pw] if (pw and k == 0) else []),
                            inc=(k == 3))
                    tmm = p.c["T"]
                    dst = VA_view[:, tg * 4:tg * 4 + 4, 2 * m:2 * m + 2, 0:DH]
                    src_ = tile[:, :].rearrange("p (tc rest) -> p tc rest", rest=128) \
                                     .rearrange("p tc (he e) -> p tc he e", e=DH)
                    vm = p.v(lambda e, dst=dst, src_=src_: e.tensor_copy(dst, src_),
                             wait=[("T", tmm)], inc=True)
                    pool4.rel(ti, ("V", vm))
                    if tg == 3:
                        pq_free[2] = ("T", p.c["T"])
                        va_ready[m] = ("V", p.c["V"])
                        wu_free[wslot] = ("T", p.c["T"])
                return run

            for pi in range(2):
                for tt in range(NTQ):
                    qkv_queue.append((m, mk_s1(tt, pi)))
                for tt in range(NTQ):
                    qkv_queue.append((m, mk_s2qk(tt, pi)))
            for tt in range(NTQ):
                qkv_queue.append((m, mk_s1(tt, 2)))
            for tg in range(4):
                qkv_queue.append((m, mk_s2v(tg)))

        sd_free = [("S", 0), ("S", 0)]
        at_free = {(s, pp): ("T", 0) for s in range(2) for pp in range(2)}
        ct_free = [("DCT0", 0), ("DCT1", 0)]
        rl_free = [("V", 0), ("V", 0)]
        brl_free = [("V", 0), ("V", 0)]
        tail_idx = [0]
        pending_tails = [[]]
        hooks = {}

        def make_tail(hc, hp, j, tC, iC, t_av):
            par = tail_idx[0] % 2
            tail_idx[0] += 1

            def tailA():
                p.s(lambda e: e.activation(
                    lnt[64:65, 0:TQ], tC[DH:DH + 1, :], AF.Ln),
                    wait=[("T", t_av), rl_free[par]], inc=True)
                vr_ = p.s(lambda e: e.activation(
                    rlb[64:65, TQ * par:TQ * par + TQ], lnt[64:65, 0:TQ],
                    AF.Exp, scale=-1.0), inc=True)
                return vr_

            def tailB(vr):
                tileB, tiB, pwB = pool4.acq()
                tb = p.t(lambda e, tileB=tileB: e.matmul(
                    tileB[0:64, :], bsl("ones", 0, 64, p0=64, pn=1),
                    rlb[64:65, TQ * par:TQ * par + TQ], start=True, stop=True),
                    wait=[("S", vr)] + ([pwB] if pwB else []), inc=True)
                rl_free[par] = ("T", tb)
                sb_ = p.v(lambda e, tileB=tileB: e.tensor_copy(
                    brl[0:64, TQ * par:TQ * par + TQ], tileB[0:64, :]),
                    wait=[("T", tb), brl_free[par]], inc=True)
                pool4.rel(tiB, ("V", sb_))
                if hp == 0:
                    vm = p.v(lambda e: e.tensor_tensor(
                        QB[0:64, T * hc + TQ * j:T * hc + TQ * j + TQ], tC[0:64, :],
                        brl[0:64, TQ * par:TQ * par + TQ], ALU.mult),
                        wait=[("V", sb_)], inc=True)
                    brl_free[par] = ("V", vm)
                    pool4.rel(iC, ("V", vm))
                else:
                    cpar = par
                    vm = p.v(lambda e: e.tensor_tensor(
                        CTb[0:64, TQ * cpar:TQ * cpar + TQ], tC[0:64, :],
                        brl[0:64, TQ * par:TQ * par + TQ], ALU.mult),
                        wait=[("V", sb_), ct_free[cpar]], inc=True)
                    brl_free[par] = ("V", vm)
                    pool4.rel(iC, ("V", vm))
                    dm = p.y(lambda e: e.dma_start(
                        QB[64:128, T * hc + TQ * j:T * hc + TQ * j + TQ],
                        CTb[0:64, TQ * cpar:TQ * cpar + TQ]), wait=[("V", vm)],
                        sem="DCT%d" % cpar)
                    ct_free[cpar] = ("DCT%d" % cpar, dm)

            def tail():
                vr = tailA()
                return lambda vr=vr: tailB(vr)
            return tail

        def emit_attention_hc(hc):
            drain_tag(hc)
            for j in range(NTQ):
                kcm = 4 * (j + 1)
                npair = kcm // 2
                exp_marks = [{}, {}]
                msk_marks = [{}, {}]
                ctx_hold = [{}, {}]

                def emit_qk_quad(ip, hc=hc, j=j, exp_marks=exp_marks, msk_marks=msk_marks):
                    i0 = 2 * ip
                    if not drain_qkv(1):
                        for s in range(2):
                            p.t(lambda e, s=s: e.matmul(
                                SD[s][:, 0:TQ], bsl("masks", 0, 128), bsl("masks", 0, TQ),
                                start=True, stop=True), wait=[sd_free[s]])
                    for u in range(2):
                        i = i0 + u
                        for s in range(2):
                            r0 = 64 * s
                            p.t(lambda e, s=s, u=u, i=i, r0=r0, hc=hc, j=j: e.matmul(
                                SD[s][:, TQ * u:TQ * (u + 1)],
                                KB[r0:r0 + 64, T * hc + 128 * i:T * hc + 128 * (i + 1)],
                                QB[r0:r0 + 64, T * hc + TQ * j:T * hc + TQ * j + TQ],
                                start=True, stop=True),
                                wait=([sd_free[s]] if u == 0 else []) +
                                     ([qk_ready[hc]] if (u == 0 and s == 0 and ip == 0 and j == 0) else []),
                                inc=(u == 1 and s == 1))
                    tmm = p.c["T"]
                    pp = ip % 2
                    for s in range(2):
                        sm = p.s(lambda e, s=s, i0=i0, pp=pp: e.activation(
                            ATb[:, TQ * (4 * s + 2 * pp):TQ * (4 * s + 2 * pp) + 2 * TQ],
                            SD[s][:, :], AF.Exp, scale=0.125),
                            wait=[("T", tmm), at_free[(s, pp)]], inc=True)
                        sd_free[s] = ("S", sm)
                        for u in range(2):
                            i = i0 + u
                            exp_marks[s][i] = ("S", sm)
                            if i >= 4 * j:
                                d = i - 4 * j
                                slot = 4 * s + 2 * pp + u
                                mm_ = p.v(lambda e, slot=slot, d=d: e.tensor_tensor(
                                    ATb[:, TQ * slot:TQ * slot + TQ],
                                    ATb[:, TQ * slot:TQ * slot + TQ],
                                    bsl("masks", TQ * d, TQ), ALU.mult),
                                    wait=[("S", sm)], inc=True)
                                msk_marks[s][i] = ("V", mm_)

                def emit_av_quad(ip, last, hc=hc, exp_marks=exp_marks, msk_marks=msk_marks,
                                 ctx_hold=ctx_hold):
                    i0 = 2 * ip
                    for s in range(2):
                        h = 2 * hc + s
                        for u in range(2):
                            i = i0 + u
                            w = [msk_marks[s].get(i, exp_marks[s][i])]
                            if i == 0:
                                tC, iC, pwC = pool4.acq()
                                ctx_hold[s]["t"] = tC
                                ctx_hold[s]["i"] = iC
                                if pwC:
                                    w.append(pwC)
                                w.append(va_ready[hc])
                            tC = ctx_hold[s]["t"]
                            pp = ip % 2
                            slot = 4 * s + 2 * pp + u
                            is_last = last and u == 1
                            p.t(lambda e, i=i, h=h, tC=tC, slot=slot, is_last=is_last: e.matmul(
                                tC[0:DH + 1, :], VA[:, VAW * i + 65 * h:VAW * i + 65 * h + 65],
                                ATb[:, TQ * slot:TQ * slot + TQ],
                                start=(i == 0), stop=is_last),
                                wait=w, inc=is_last)
                            if u == 1:
                                at_free[(s, pp)] = ("T", p.c["T"])

                emit_qk_quad(0)
                if npair == 1:
                    for t_ in pending_tails[0]:
                        t_()
                    pending_tails[0] = []
                for ip in range(1, npair):
                    emit_qk_quad(ip)
                    if ip == 1:
                        for t_ in pending_tails[0]:
                            t_()
                        pending_tails[0] = []
                    emit_av_quad(ip - 1, False)
                emit_av_quad(npair - 1, True)
                t_av = p.c["T"]
                pending_tails[0] = [t_() for t_ in [
                    make_tail(hc, 0, j, ctx_hold[0]["t"], ctx_hold[0]["i"], t_av),
                    make_tail(hc, 1, j, ctx_hold[1]["t"], ctx_hold[1]["i"], t_av)]]
                if hc == 4 and j == 0:
                    hooks["hc4"] = ("T", p.c["T"])

        for m in range(CH):
            emit_qkv_pair(m)
            if m >= 1:
                emit_attention_hc(m - 1)
                if m - 1 == 3:
                    hooks["hc3_done"] = ("T", p.c["T"])
                if m - 1 == 6:
                    hooks["hc6_done"] = ("T", p.c["T"])
        drain_tag(CH)
        hooks["qkv_done"] = ("T", p.c["T"])
        emit_dump("k", KB, [("S", p.c["S"])], bf16=True)
        emit_dump("va", VA, [("V", p.c["V"])], bf16=True)
        p.flush()
        pa.close()
        esx1.close()

        esH2 = ExitStack()
        sbH2 = esH2.enter_context(nc.sbuf_tensor("H2", [128, CH * T], F32, side="left"))
        h2v = sbH2[:, :].rearrange("p (c t) -> p c t", t=T)
        spv = spill_h.rearrange("p (c t) -> p c t", t=T)
        gate_h2 = [("T", p.c["T"]), ("S", p.c["S"])]
        d_rel_h = []
        sems_rh = ["DSH", "DS2"]
        spill_done = [("DSH", d_spill_h1), ("DS2", d_spill_h2)]
        for hh in range(2):
            d_rel_h.append(p.y(lambda e, hh=hh: e.dma_start(
                h2v[:, 4 * hh:4 * (hh + 1), :], spv[:, 4 * hh:4 * (hh + 1), :]),
                wait=gate_h2 + [spill_done[hh]], sem=sems_rh[hh]))
        emit_attention_hc(CH - 1)
        for t_ in pending_tails[0]:
            t_()
        pending_tails[0] = []
        ctx_dma_done = [("DCT0", p.c["DCT0"]), ("DCT1", p.c["DCT1"])]
        emit_dump("ctx", QB, [("V", p.c["V"])] + ctx_dma_done, bf16=True)
        p.barrier("tvs")
        p.t(lambda e: e.nop(), wait=ctx_dma_done)
        p.flush()
        psB.close()
        pb.close()
        esKV.close()

        ps8b = ExitStack()
        pool_tiles2 = [ps8b.enter_context(nc.psum_tensor("pq%d" % i, [128, TQ], F32))
                       for i in range(8)]
        pool = Pool(p, pool_tiles2)

        po = ExitStack()
        oU = po.enter_context(nc.sbuf_tensor("oU", [128, CH * 768], BF16, side="right"))
        oV = po.enter_context(nc.sbuf_tensor("oV", [128, 6 * D], BF16, side="right"))
        O1 = po.enter_context(nc.sbuf_tensor("O1", [128, 6 * T], BF16, side="right"))
        b1sb = po.enter_context(nc.sbuf_tensor("b1sb", [128, 6], F32, side="right"))
        d_oU = p.g(lambda e: e.dma_start(oU[:, :], dram["outU"]),
                   wait=[hooks["hc3_done"]], sem="DOU")
        d_oV = p.g(lambda e: e.dma_start(oV[:, :], dram["outV"]),
                   wait=[hooks["hc6_done"]], sem="DOV")
        tile, ti, pw = pool.acq()
        for m in range(6):
            for c in range(CH):
                p.t(lambda e, tile=tile, m=m, c=c: e.matmul(
                    tile[:, m:m + 1], oU[:, 768 * c + 128 * m:768 * c + 128 * (m + 1)],
                    bsl("vb_bf", c, 1), start=(c == 0), stop=(c == CH - 1)),
                    wait=([("DOU", d_oU)] if (m == 0 and c == 0) else []) +
                         ([pw] if (pw and m == 0 and c == 0) else []),
                    inc=(m == 5 and c == CH - 1))
        tb1 = p.c["T"]
        vb1 = p.v(lambda e, tile=tile: e.tensor_copy(b1sb[:, :], tile[:, 0:6]),
                  wait=[("T", tb1)], inc=True)
        pool.rel(ti, ("V", vb1))
        for m in range(6):
            for tt in range(NTQ):
                tile, ti, pw = pool.acq()
                for c in range(CH):
                    p.t(lambda e, tile=tile, m=m, tt=tt, c=c: e.matmul(
                        tile[:, :], oU[:, 768 * c + 128 * m:768 * c + 128 * (m + 1)],
                        QB[:, T * c + TQ * tt:T * c + TQ * tt + TQ],
                        start=(c == 0), stop=(c == CH - 1)),
                        wait=([pw] if (pw and c == 0) else []), inc=(c == CH - 1))
                tmm = p.c["T"]
                sm = p.s(lambda e, tile=tile, m=m, tt=tt: e.activation(
                    O1[:, T * m + TQ * tt:T * m + TQ * tt + TQ], tile[:, :], AF.Identity,
                    bias=b1sb[:, m:m + 1], scale=1.0),
                    wait=[("T", tmm), ("V", vb1)], inc=True)
                pool.rel(ti, ("S", sm))
        s1_done = ("S", p.c["S"])
        h2_chunk_marks = {}
        for do in range(CH):
            for tt in range(NTQ):
                tile, ti, pw = pool.acq()
                for k6 in range(6):
                    p.t(lambda e, tile=tile, do=do, tt=tt, k6=k6: e.matmul(
                        tile[:, :], oV[:, D * k6 + 128 * do:D * k6 + 128 * (do + 1)],
                        O1[:, T * k6 + TQ * tt:T * k6 + TQ * tt + TQ],
                        start=(k6 == 0), stop=(k6 == 5)),
                        wait=([s1_done, ("DOV", d_oV)] if k6 == 0 else []) +
                             ([pw] if (pw and k6 == 0) else []),
                        inc=(k6 == 5))
                tmm = p.c["T"]
                vm = p.v(lambda e, tile=tile, do=do, tt=tt: e.scalar_tensor_tensor(
                    sbH2[:, T * do + TQ * tt:T * do + TQ * tt + TQ], tile[:, :],
                    scol("outb", do), sbH2[:, T * do + TQ * tt:T * do + TQ * tt + TQ],
                    ALU.add, ALU.add),
                    wait=[("T", tmm), (sems_rh[do // 4], d_rel_h[do // 4])], inc=True)
                pool.rel(ti, ("V", vm))
                h2_chunk_marks[do] = ("V", vm)
        mark_h = ("V", p.c["V"])
        emit_dump("h", sbH2, [mark_h])
        p.barrier("tvs")
        p.flush()
        po.close()
        esQB.close()
        if stop_after == "outproj":
            finalize_early(sbH2)
            return nc

        pm0 = ExitStack()
        scr2 = pm0.enter_context(nc.sbuf_tensor("scr2", [128, 2 * CH * TH], BF16, side="right"))
        mark_b2 = emit_ln_tiles(pool, sbH2, scr2[:, 0:CH * TH], [],
                                [h2_chunk_marks[c] for c in range(CH)],
                                scratch2=scr2[:, CH * TH:2 * CH * TH])
        p.flush()
        pm0.close()

        pm = ExitStack()
        f1U = pm.enter_context(nc.sbuf_tensor("f1U", [128, CH * 512], BF16, side="right"))
        f1V = pm.enter_context(nc.sbuf_tensor("f1V", [128, 4 * 4096], BF16, side="right"))
        f2U = pm.enter_context(nc.sbuf_tensor("f2U", [128, 32 * 512], BF16, side="right"))
        f2V = pm.enter_context(nc.sbuf_tensor("f2V", [128, 4 * D], BF16, side="right"))
        x2t = pm.enter_context(nc.sbuf_tensor("x2t", [128, 2 * CH * TQ], BF16, side="right"))
        P1t = pm.enter_context(nc.sbuf_tensor("P1t", [128, 4 * TQ], BF16, side="right"))
        P2t = pm.enter_context(nc.sbuf_tensor("P2t", [128, 4 * TQ], BF16, side="right"))
        HMt = pm.enter_context(nc.sbuf_tensor("HMt", [128, 2 * TQ], BF16, side="right"))
        mlp_gate = ("T", p.c["T"])
        b2m = pm.enter_context(nc.sbuf_tensor("b2m", [128, 4], F32, side="right"))
        d_f1U = p.g(lambda e: e.dma_start(f1U[:, :], dram["fc1U"]), wait=[mlp_gate], sem="DF1U")
        d_f1V = p.g(lambda e: e.dma_start(f1V[:, :], dram["fc1V"]), wait=[mlp_gate], sem="DF1V")
        d_f2U = p.g(lambda e: e.dma_start(f2U[:, :], dram["fc2U"]), wait=[mlp_gate], sem="DF2U")
        d_f2V = p.g(lambda e: e.dma_start(f2V[:, :], dram["fc2V"]), wait=[mlp_gate], sem="DF2V")

        tileb, tib, pwb = pool.acq()
        for pc in range(4):
            for c in range(CH):
                p.t(lambda e, tileb=tileb, pc=pc, c=c: e.matmul(
                    tileb[:, pc:pc + 1], f1U[:, 512 * c + 128 * pc:512 * c + 128 * (pc + 1)],
                    bsl("b2_bf", c, 1), start=(c == 0), stop=(c == CH - 1)),
                    wait=([("DF1U", d_f1U)] if (pc == 0 and c == 0) else []) +
                         ([pwb] if (pwb and pc == 0 and c == 0) else []),
                    inc=(pc == 3 and c == CH - 1))
        t_b2m = p.c["T"]
        v_b2m = p.v(lambda e, tileb=tileb: e.tensor_copy(b2m[:, :], tileb[:, 0:4]),
                    wait=[("T", t_b2m)], inc=True)
        pool.rel(tib, ("V", v_b2m))
        g2_marks = []
        for c in range(CH):
            smsc = p.s(lambda e, c=c: e.activation(
                f1U[:, 512 * c:512 * (c + 1)], f1U[:, 512 * c:512 * (c + 1)],
                AF.Identity, bias=0.0, scale=scol("g2", c)),
                wait=[("T", t_b2m)], inc=True)
            g2_marks.append(smsc)
        f1u_ready = ("S", g2_marks[-1])

        x2_free = [("T", 0), ("T", 0)]
        p1_free = ("T", 0)
        p2_free = ("V", 0)

        def emit_x2(jt):
            slot = jt % 2
            x0 = CH * TQ * slot
            marks = []
            for c in range(CH):
                p.v(lambda e, c=c, jt=jt, x0=x0: e.tensor_tensor(
                    x2t[:, x0 + TQ * c:x0 + TQ * (c + 1)],
                    sbH2[:, T * c + TQ * jt:T * c + TQ * jt + TQ],
                    bA[:, TQ * jt:TQ * jt + TQ], ALU.mult),
                    wait=[mark_b2, x2_free[slot], mark_h], inc=True)
                vmx = p.v(lambda e, c=c, jt=jt, x0=x0: e.tensor_tensor(
                    x2t[:, x0 + TQ * c:x0 + TQ * (c + 1)], x2t[:, x0 + TQ * c:x0 + TQ * (c + 1)],
                    bB[:, TQ * jt:TQ * jt + TQ], ALU.add), inc=True)
                marks.append(("V", vmx))
            return marks

        mlp_chunk_marks = {}
        x2_marks_by_tile = {0: emit_x2(0)}
        for jt in range(NTQ):
            slot = jt % 2
            x0 = CH * TQ * slot
            x2_marks = x2_marks_by_tile[jt]
            p1_marks = []
            for pc in range(4):
                tile, ti, pw = pool.acq()
                for c in range(CH):
                    p.t(lambda e, tile=tile, pc=pc, c=c, x0=x0: e.matmul(
                        tile[:, :], f1U[:, 512 * c + 128 * pc:512 * c + 128 * (pc + 1)],
                        x2t[:, x0 + TQ * c:x0 + TQ * (c + 1)],
                        start=(c == 0), stop=(c == CH - 1)),
                        wait=([x2_marks[-1], f1u_ready, p1_free] if c == 0 else []) +
                             ([pw] if (pw and c == 0) else []),
                        inc=(c == CH - 1))
                tmm = p.c["T"]
                vm = p.s(lambda e, tile=tile, pc=pc: e.activation(
                    P1t[:, TQ * pc:TQ * (pc + 1)], tile[:, :], AF.Identity,
                    bias=b2m[:, pc:pc + 1], scale=1.0),
                    wait=[("T", tmm), ("V", v_b2m)], inc=True)
                pool.rel(ti, ("S", vm))
                p1_marks.append(vm)
            p1_done = ("S", p1_marks[-1])
            s1_T = p.c["T"]
            if jt + 1 < NTQ:
                x2_marks_by_tile[jt + 1] = emit_x2(jt + 1)
            acc = []
            for p2 in range(4):
                tileA, tiA, pwA = pool.acq()
                acc.append((tileA, tiA, pwA))
            hm_free = [("T", 0), ("T", 0)]
            for ic in range(32):
                tile, ti, pw = pool.acq()
                for pc in range(4):
                    p.t(lambda e, tile=tile, ic=ic, pc=pc: e.matmul(
                        tile[:, :], f1V[:, 4096 * pc + 128 * ic:4096 * pc + 128 * (ic + 1)],
                        P1t[:, TQ * pc:TQ * (pc + 1)],
                        start=(pc == 0), stop=(pc == 3)),
                        wait=([p1_done, ("DF1V", d_f1V)] if pc == 0 else []) +
                             ([pw] if (pw and pc == 0) else []),
                        inc=(pc == 3))
                tmm = p.c["T"]
                hslot = ic % 2
                smg = p.s(lambda e, tile=tile, ic=ic, hslot=hslot: e.activation(
                    HMt[:, TQ * hslot:TQ * (hslot + 1)], tile[:, :], AF.Gelu_apprx_tanh,
                    bias=scol("fc1b", ic), scale=1.0),
                    wait=[("T", tmm), hm_free[hslot]], inc=True)
                pool.rel(ti, ("S", smg))
                for p2 in range(4):
                    tileA, tiA, pwA = acc[p2]
                    p.t(lambda e, tileA=tileA, ic=ic, p2=p2, hslot=hslot: e.matmul(
                        tileA[:, :], f2U[:, 512 * ic + 128 * p2:512 * ic + 128 * (p2 + 1)],
                        HMt[:, TQ * hslot:TQ * (hslot + 1)],
                        start=(ic == 0), stop=(ic == 31)),
                        wait=[("S", smg), ("DF2U", d_f2U)] +
                             ([pwA] if (pwA and ic == 0) else []) +
                             ([p2_free] if ic == 0 else []),
                        inc=(p2 == 3))
                hm_free[hslot] = ("T", p.c["T"])
            t_acc = p.c["T"]
            for p2 in range(4):
                tileA, tiA, _ = acc[p2]
                vm = p.s(lambda e, tileA=tileA, p2=p2: e.activation(
                    P2t[:, TQ * p2:TQ * (p2 + 1)], tileA[:, :], AF.Identity,
                    bias=0.0, scale=1.0), wait=[("T", t_acc)], inc=True)
                pool.rel(tiA, ("S", vm))
            p2_done = ("S", p.c["S"])
            for do in range(CH):
                tile, ti, pw = pool.acq()
                for p2 in range(4):
                    p.t(lambda e, tile=tile, do=do, p2=p2: e.matmul(
                        tile[:, :], f2V[:, D * p2 + 128 * do:D * p2 + 128 * (do + 1)],
                        P2t[:, TQ * p2:TQ * (p2 + 1)],
                        start=(p2 == 0), stop=(p2 == 3)),
                        wait=([p2_done, ("DF2V", d_f2V)] if p2 == 0 else []) +
                             ([pw] if (pw and p2 == 0) else []),
                        inc=(p2 == 3))
                tmm = p.c["T"]
                vm = p.v(lambda e, tile=tile, do=do, jt=jt: e.scalar_tensor_tensor(
                    sbH2[:, T * do + TQ * jt:T * do + TQ * jt + TQ], tile[:, :],
                    scol("fc2b", do), sbH2[:, T * do + TQ * jt:T * do + TQ * jt + TQ],
                    ALU.add, ALU.add), wait=[("T", tmm)], inc=True)
                pool.rel(ti, ("V", vm))
                mlp_chunk_marks[do] = ("V", vm)
            x2_free[slot] = ("T", t_acc)
            p1_free = ("T", t_acc)
            p2_free = ("V", p.c["V"])
        mark_mlp = ("V", p.c["V"])
        emit_dump("mlp", sbH2, [mark_mlp])
        p.barrier("tvs")
        p.flush()
        pm.close()
        if stop_after == "mlp":
            finalize_early(sbH2)
            return nc

        esS2 = ExitStack()
        sbS2 = esS2.enter_context(nc.sbuf_tensor("S2", [128, CH * T], F32, side="right"))
        s2v = sbS2[:, :].rearrange("p (c t) -> p c t", t=T)
        sp2v = spill_s.rearrange("p (c t) -> p c t", t=T)
        gate_s2 = [("T", p.c["T"]), ("V", p.c["V"]), ("S", p.c["S"])]
        d_rel = []
        sems_rel = ["DSS", "DS2"]
        for hh in range(2):
            d_rel.append(p.y(lambda e, hh=hh: e.dma_start(
                s2v[:, 4 * hh:4 * (hh + 1), :], sp2v[:, 4 * hh:4 * (hh + 1), :]),
                wait=gate_s2, sem=sems_rel[hh]))
        add_marks_ch = []
        for c in range(CH):
            vm = p.v(lambda e, c=c: e.tensor_tensor(
                sbS2[:, T * c:T * (c + 1)], sbS2[:, T * c:T * (c + 1)],
                sbH2[:, T * c:T * (c + 1)], ALU.add),
                wait=[(sems_rel[c // 4], d_rel[c // 4]), mlp_chunk_marks[c]], inc=True)
            add_marks_ch.append(("V", vm))
        p.flush()
        esH2.close()

        ffn2 = ExitStack()
        XRb2 = ffn2.enter_context(nc.sbuf_tensor("XR2", [128, CH * TH], BF16, side="right"))
        XKb2 = ffn2.enter_context(nc.sbuf_tensor("XK2", [128, CH * TH], BF16, side="right"))
        KAb2 = ffn2.enter_context(nc.sbuf_tensor("KA2", [128, CH * TH], BF16, side="right"))
        wrS2 = ffn2.enter_context(nc.sbuf_tensor("wrS2", [128, CH * D], BF16, side="right"))
        wkS2 = ffn2.enter_context(nc.sbuf_tensor("wkS2", [128, CH * D], BF16, side="right"))
        wvS2 = ffn2.enter_context(nc.sbuf_tensor("wvS2", [128, CH * D], BF16, side="right"))
        rt2 = ffn2.enter_context(nc.sbuf_tensor("rt2", [128, TQ], BF16, side="right"))
        krl2 = ffn2.enter_context(nc.sbuf_tensor("krl2", [128, TQ], BF16, side="right"))
        t1f2 = ffn2.enter_context(nc.sbuf_tensor("t1f2", [128, TQ], F32, side="right"))
        dwk2 = p.g(lambda e: e.dma_start(wkS2[:, :], dram["wkT"]), sem="DWK")
        dwv2 = p.g(lambda e: e.dma_start(wvS2[:, :], dram["wvT"]), sem="DWV")
        dwr2 = p.g(lambda e: e.dma_start(wrS2[:, :], dram["wrT"]), sem="DWR")
        wm2 = {"wk": ("DWK", dwk2), "wv": ("DWV", dwv2), "wr": ("DWR", dwr2)}
        mark_ffn2, ffn2_chunk_marks = emit_ffn(sbS2, XRb2, XKb2, KAb2, wrS2, wkS2,
                                               wvS2, rt2, t1f2, krl2, wm2, [],
                                               add_marks_ch)
        emit_dump("out_fm", sbS2, [mark_ffn2])
        p.barrier("tvs")
        p.flush()
        ffn2.close()
        if stop_after == "ffn2":
            finalize_early(sbS2)
            return nc

        xv_out = x_out.rearrange("p (c t) -> p c t", t=T)
        sv_out = sbS2[:, :].rearrange("p (c t) -> p c t", t=T)
        for hh in range(2):
            w = [ffn2_chunk_marks[c] for c in range(4 * hh, 4 * hh + 4)]
            p.y(lambda e, hh=hh: e.dma_start(
                xv_out[:, 4 * hh:4 * (hh + 1), :], sv_out[:, 4 * hh:4 * (hh + 1), :]),
                wait=w, sem=["DO0", "DO1"][hh])
        p._mk("y", lambda eng: eng.nop(),
              [(sn, p.c[sn]) for sn in DMA_SEMS if p.c[sn] > 0], [])
        p.flush()
        ps8b.close()
        esS2.close()

    return nc



def kernel(**inputs):
    host = prep_host(inputs)
    nc = build_program(host)
    x = _f32(inputs["L0_attn_hidden_states"])
    in_maps = []
    for c in range(8):
        m = {k: host[k] for k in WTENSORS}
        m["x_in"] = feat_chunks(np.ascontiguousarray(x[c].T))
        in_maps.append(m)
    res = run_bass_kernel_spmd(nc, in_maps, core_ids=list(range(8)))
    outs = []
    for c in range(8):
        fm = np.asarray(res.results[c]["x_out"])
        td = fm.reshape(128, CH, T).transpose(1, 0, 2).reshape(D, T).T
        outs.append(np.ascontiguousarray(td))
    return np.stack(outs, axis=0)
